# version 04
# baseline (speedup 1.0000x reference)
import os
import sys
import numpy as np
import ml_dtypes
from contextlib import ExitStack

import concourse.bass as bass
import concourse.mybir as mybir
from concourse.bass_utils import run_bass_kernel_spmd

F32 = mybir.dt.float32
BF16 = mybir.dt.bfloat16
I32 = mybir.dt.int32
AF = mybir.ActivationFunctionType
ALU = mybir.AluOpType
AX = mybir.AxisListType
BF = ml_dtypes.bfloat16

NCORES = 8
D = 2048
TPC = 1024
T = 4096
NST = 16
DTR = 64
NDC = D // 128
NTT = TPC // 128
LN_EPS = 1e-5


def _make_consts(vals):
    a = np.zeros((128, max(1, len(vals))), np.float32)
    for i, v in enumerate(vals):
        a[:, i] = v
    return a


LAD4 = (8.0, 4.0, 2.0, 1.0)
LAD4_C = 8.5
LAD8 = (128.0, 64.0, 32.0, 16.0, 8.0, 4.0, 2.0, 1.0)
LAD8_C = 128.5


def _emit_ladder(v, chunks, powers):
    assert len(chunks) >= 2
    for pw in powers:
        for m, g in chunks:
            v.tensor_scalar(out=g, in0=m, scalar1=pw, scalar2=-pw,
                            op0=ALU.is_ge, op1=ALU.mult)
        for m, g in chunks:
            v.scalar_tensor_tensor(out=m, in0=g, scalar=1.0, in1=m,
                                   op0=ALU.mult, op1=ALU.add)


class _Serial:

    def __init__(self, eng, sem, prog, key):
        self._eng, self._sem, self._prog, self._key = eng, sem, prog, key

    def wait_ge(self, sem, val):
        if val > 0:
            self._eng.wait_ge(sem, val)

    def wait_mark(self, key, mark):
        self._prog.wait_mark(self._eng, key, mark)

    def __getattr__(self, name):
        f = getattr(self._eng, name)
        if not callable(f):
            return f

        def wrapped(*a, **k):
            is_dma = name in ("dma_start", "dma_start_transpose")
            key = self._key
            dkey = key + "d"
            cnt = self._prog.cnt[key]
            if cnt > 0:
                self._eng.wait_ge(self._sem, cnt)
            if dkey in self._prog.auto and self._prog.cnt[dkey] > 0:
                self._eng.wait_ge(self._prog.auto[dkey], self._prog.cnt[dkey])
            inst = f(*a, **k)
            if is_dma and dkey in self._prog.auto:
                inst.then_inc(self._prog.auto[dkey], 16)
                self._prog.cnt[dkey] += 16
            else:
                inst.then_inc(self._sem, 16 if is_dma else 1)
                self._prog.cnt[key] += 16 if is_dma else 1
            return inst

        return wrapped


class Prog:

    AUTOKEYS = ("v", "s", "g", "gd")

    def __init__(self, nc, stack):
        self.nc = nc
        self.block = stack.enter_context(nc.Block())
        self.auto = {k: stack.enter_context(nc.semaphore(f"auto_{k}"))
                     for k in self.AUTOKEYS}
        self.cnt = {k: 0 for k in self.AUTOKEYS}
        self._nsem = 0
        self._stack = stack

    def sem(self, name=None):
        self._nsem += 1
        return self._stack.enter_context(
            self.nc.semaphore(name or f"s{self._nsem}"))

    def v(self, fn):
        self.block.vector(lambda e: fn(_Serial(e, self.auto["v"], self, "v")))

    def s(self, fn):
        self.block.scalar(lambda e: fn(_Serial(e, self.auto["s"], self, "s")))

    def g(self, fn):
        self.block.gpsimd(lambda e: fn(_Serial(e, self.auto["g"], self, "g")))

    def t(self, fn):
        self.block.tensor(fn)

    def y(self, fn):
        self.block.sync(fn)

    def mark(self, key):
        if key == "g":
            return (self.cnt["g"], self.cnt["gd"])
        return self.cnt[key]

    def wait_mark(self, eng, key, mark):
        if key == "g":
            a, b = mark if isinstance(mark, tuple) else (mark, 0)
            if a > 0:
                eng.wait_ge(self.auto["g"], a)
            if b > 0:
                eng.wait_ge(self.auto["gd"], b)
        elif mark > 0:
            eng.wait_ge(self.auto[key], mark)


CONSTS_A = [LN_EPS, -0.5, -1.0, 1.0]


def _build_A():
    nc = bass.Bass("TRN2", target_bir_lowering=False)
    xr = nc.dram_tensor("xr", [TPC, D], F32, kind="ExternalInput").ap()
    wx = nc.dram_tensor("wx", [D, 96], F32, kind="ExternalInput").ap()
    wdt = nc.dram_tensor("wdt", [DTR, D], F32, kind="ExternalInput").ap()
    bdtc = nc.dram_tensor("bdtc", [128, NDC], F32, kind="ExternalInput").ap()
    ident = nc.dram_tensor("ident", [128, 128], F32, kind="ExternalInput").ap()
    constv = nc.dram_tensor("constv", [128, len(CONSTS_A)], F32,
                            kind="ExternalInput").ap()
    o_dt = nc.dram_tensor("o_dt", [D, TPC], F32, kind="ExternalOutput").ap()
    o_dtu = nc.dram_tensor("o_dtu", [D, TPC], F32, kind="ExternalOutput").ap()
    o_bc = nc.dram_tensor("o_bc", [32, TPC], F32, kind="ExternalOutput").ap()

    st = ExitStack()
    E = st.enter_context
    xtm = E(nc.sbuf_tensor("xtm", [128, 2 * D], F32))
    sqt = E(nc.sbuf_tensor("sqt", [128, 2 * D], F32))
    x0t = E(nc.sbuf_tensor("x0t", [128, 2 * D], F32))
    x0n = E(nc.sbuf_tensor("x0n", [128, NDC * TPC], F32))
    x0nb = E(nc.sbuf_tensor("x0nb", [128, NDC * TPC], BF16))
    statc = E(nc.sbuf_tensor("statc", [128, NTT * 8], F32))
    wx_sb = E(nc.sbuf_tensor("wx_sb", [128, NDC * 96], BF16))
    wdt_sb = E(nc.sbuf_tensor("wdt_sb", [DTR, D], BF16))
    bdt_sb = E(nc.sbuf_tensor("bdt_sb", [128, NDC], F32))
    id_sb = E(nc.sbuf_tensor("id_sb", [128, 128], F32))
    cst_sb = E(nc.sbuf_tensor("cst_sb", [128, len(CONSTS_A)], F32))
    delta_bf = E(nc.sbuf_tensor("delta_bf", [DTR, TPC], BF16))
    bc_sb = E(nc.sbuf_tensor("bc_sb", [32, TPC], F32))
    absz = E(nc.sbuf_tensor("absz", [128, TPC], F32))
    l1p = E(nc.sbuf_tensor("l1p", [128, TPC], F32))
    relu = E(nc.sbuf_tensor("relu", [128, TPC], F32))
    dt_sb = E(nc.sbuf_tensor("dt_sb", [128, 2 * TPC], F32))
    dtu_sb = E(nc.sbuf_tensor("dtu_sb", [128, 2 * TPC], F32))

    P = [E(nc.psum_tensor(f"P{i}", [128, 512], F32)) for i in range(8)]

    for i, v in enumerate(CONSTS_A):
        nc.const_aps.register(F32, v, cst_sb[:, i:i + 1])

    p = Prog(nc, st)
    sLc = p.sem("sLc")
    sLx = [p.sem("sLx0"), p.sem("sLx1")]
    sLw = p.sem("sLw")
    sTr = p.sem("sTr")
    sXz = p.sem("sXz")
    sDtM = p.sem("sDtM")
    sOutD = [p.sem("sOutD0"), p.sem("sOutD1")]
    sOutB = p.sem("sOutB")

    def consts(y):
        y.dma_start(out=bdt_sb[:, :], in_=bdtc).then_inc(sLc, 16)
        y.dma_start(out=id_sb[:, :], in_=ident).then_inc(sLc, 16)
        y.dma_start(out=cst_sb[:, :], in_=constv).then_inc(sLc, 16)
    p.y(consts)

    def wloads(g):
        g.dma_start(out=wx_sb[:, :], in_=wx.rearrange("(c p) n -> p c n", p=128))
        g.dma_start(out=wdt_sb[:, :], in_=wdt)
    p.g(wloads)
    m_wload = p.mark("g")

    aff_mark = [0] * NTT
    red_mark = [0] * NTT
    evac_mark = [0] * NTT

    for tt in range(NTT):
        sl = slice((tt % 2) * D, (tt % 2) * D + D)

        def load(y, tt=tt, sl=sl):
            if tt >= 2:
                y.wait_ge(p.auto["v"], aff_mark[tt - 2])
            y.dma_start(out=xtm[:, sl],
                        in_=xr[tt * 128:(tt + 1) * 128, :]).then_inc(sLx[tt % 2], 16)
        p.y(load)

        def square(s, tt=tt, sl=sl):
            s.wait_ge(sLc, 48)
            s.wait_ge(sLx[tt % 2], 16 * (tt // 2 + 1))
            if tt >= 2:
                s.wait_mark("v", red_mark[tt - 2])
            s.activation(out=sqt[:, sl], in_=xtm[:, sl], func=AF.Square)
        p.s(square)
        m_sq = p.mark("s")

        def stats(v, tt=tt, sl=sl):
            c = lambda k: statc[:, tt * 8 + k: tt * 8 + k + 1]
            v.wait_mark("s", m_sq)
            v.reduce_sum(c(0), xtm[:, sl], axis=AX.X)
            v.reduce_sum(c(1), sqt[:, sl], axis=AX.X)
            red_mark[tt] = p.cnt["v"]
            v.tensor_scalar_mul(c(2), c(0), 1.0 / D)
            v.tensor_scalar_mul(c(3), c(1), 1.0 / D)
            v.tensor_mul(c(7), c(2), c(2))
            v.tensor_sub(c(4), c(3), c(7))
        p.v(stats)
        red_mark[tt] = red_mark[tt]
        m_var = p.mark("v")

        def rstd(s, tt=tt):
            c = lambda k: statc[:, tt * 8 + k: tt * 8 + k + 1]
            s.wait_mark("v", m_var)
            s.activation(out=c(5), in_=c(4), func=AF.Ln, bias=LN_EPS)
            s.activation(out=c(6), in_=c(5), func=AF.Exp, scale=-0.5)
        p.s(rstd)
        m_rstd = p.mark("s")

        def affine(v, tt=tt, sl=sl):
            c = lambda k: statc[:, tt * 8 + k: tt * 8 + k + 1]
            v.wait_mark("s", m_rstd)
            if tt >= 2:
                v.wait_ge(sTr, tt - 1)
            v.tensor_scalar(out=x0t[:, sl], in0=xtm[:, sl], scalar1=c(2),
                            scalar2=c(6), op0=ALU.subtract, op1=ALU.mult)
            aff_mark[tt] = p.cnt["v"]
        p.v(affine)

        def transposes(t, tt=tt):
            t.wait_ge(p.auto["v"], aff_mark[tt])
            t.wait_ge(sLc, 48)
            if tt >= 2:
                t.wait_ge(p.auto["s"], evac_mark[tt - 2])
            last = None
            for k in range(NDC):
                bank = 4 * (tt % 2) + k // 4
                last = t.transpose(
                    P[bank][:, (k % 4) * 128:(k % 4) * 128 + 128],
                    x0t[:, (tt % 2) * D + k * 128:(tt % 2) * D + (k + 1) * 128],
                    id_sb[:, :])
            last.then_inc(sTr, 1)
        p.t(transposes)

        def evacs(s, tt=tt):
            s.wait_ge(sTr, tt + 1)
            for k in range(NDC):
                bank = 4 * (tt % 2) + k // 4
                s.activation(
                    out=x0n[:, k * TPC + tt * 128: k * TPC + (tt + 1) * 128],
                    in_=P[bank][:, (k % 4) * 128:(k % 4) * 128 + 128],
                    func=AF.Copy)
            evac_mark[tt] = p.cnt["s"]
        p.s(evacs)

    def casts(s):
        for dc in range(NDC):
            s.activation(out=x0nb[:, dc * TPC:(dc + 1) * TPC],
                         in_=x0n[:, dc * TPC:(dc + 1) * TPC], func=AF.Copy)
    p.s(casts)
    m_cast = p.mark("s")

    def xz_mm(t):
        t.wait_ge(p.auto["s"], m_cast)
        p.wait_mark(t, "g", m_wload)
        for dc in range(NDC):
            for h in range(2):
                mm = t.matmul(
                    P[h][0:96, :], wx_sb[:, dc * 96:(dc + 1) * 96],
                    x0nb[:, dc * TPC + h * 512: dc * TPC + (h + 1) * 512],
                    start=(dc == 0), stop=(dc == NDC - 1))
        mm.then_inc(sXz, 1)
    p.t(xz_mm)

    def delta_evac(s):
        s.wait_ge(sXz, 1)
        for h in range(2):
            s.activation(out=delta_bf[:, h * 512:(h + 1) * 512],
                         in_=P[h][0:DTR, :], func=AF.Copy)
        s.activation(out=bc_sb[:, 0:512], in_=P[0][64:96, :], func=AF.Copy)
        s.activation(out=bc_sb[:, 512:1024], in_=P[1][64:96, :], func=AF.Copy)
    p.s(delta_evac)
    m_dbf = p.mark("s")

    abs_mark = [[0, 0] for _ in range(NDC)]
    relu_mark = [[0, 0] for _ in range(NDC)]
    dtl_mark = [[0, 0] for _ in range(NDC)]
    dtu_mark = [0] * NDC

    for dc in range(NDC):
        def dt_mm(t, dc=dc):
            if dc == 0:
                t.wait_ge(p.auto["s"], m_dbf)
            for h in range(2):
                if dc > 0:
                    t.wait_ge(p.auto["s"], abs_mark[dc - 1][h])
                    t.wait_ge(p.auto["v"], relu_mark[dc - 1][h])
                t.matmul(P[2 + h][:, :], wdt_sb[:, dc * 128:(dc + 1) * 128],
                         delta_bf[:, h * 512:(h + 1) * 512],
                         start=True, stop=True).then_inc(sDtM, 1)
        p.t(dt_mm)

        def softplus(s, dc=dc):
            for h in range(2):
                s.wait_ge(sDtM, 2 * dc + h + 1)
                if dc > 0:
                    s.wait_mark("v", dtl_mark[dc - 1][h])
                hs = slice(h * 512, (h + 1) * 512)
                s.activation(out=absz[:, hs], in_=P[2 + h][:, :], func=AF.Abs,
                             bias=bdt_sb[:, dc:dc + 1])
                abs_mark[dc][h] = p.cnt["s"]
                s.activation(out=absz[:, hs], in_=absz[:, hs], func=AF.Exp,
                             scale=-1.0)
                s.activation(out=l1p[:, hs], in_=absz[:, hs], func=AF.Ln,
                             bias=1.0)
        p.s(softplus)
        m_sp = [abs_mark[dc][0] + 2, abs_mark[dc][1] + 2]

        def dt_ep(v, dc=dc, m_sp=m_sp):
            dsl0 = (dc % 2) * TPC
            for h in range(2):
                hs = slice(h * 512, (h + 1) * 512)
                v.wait_ge(sDtM, 2 * dc + h + 1)
                v.tensor_scalar(out=relu[:, hs], in0=P[2 + h][:, :],
                                scalar1=bdt_sb[:, dc:dc + 1], scalar2=0.0,
                                op0=ALU.add, op1=ALU.max)
                relu_mark[dc][h] = p.cnt["v"]
                v.wait_mark("s", m_sp[h])
                if dc >= 2 and h == 0:
                    v.wait_ge(sOutD[dc % 2], 32 * (dc // 2))
                v.tensor_add(dt_sb[:, dsl0 + h * 512: dsl0 + (h + 1) * 512],
                             relu[:, hs], l1p[:, hs])
                dtl_mark[dc][h] = p.cnt["v"]
            v.tensor_mul(dtu_sb[:, dsl0:dsl0 + TPC], dt_sb[:, dsl0:dsl0 + TPC],
                         x0n[:, dc * TPC:(dc + 1) * TPC])
            dtu_mark[dc] = p.cnt["v"]
        p.v(dt_ep)

        def dt_out(y, dc=dc):
            y.wait_ge(p.auto["v"], dtu_mark[dc])
            dsl = slice((dc % 2) * TPC, (dc % 2) * TPC + TPC)
            y.dma_start(out=o_dt[dc * 128:(dc + 1) * 128, :],
                        in_=dt_sb[:, dsl]).then_inc(sOutD[dc % 2], 16)
            y.dma_start(out=o_dtu[dc * 128:(dc + 1) * 128, :],
                        in_=dtu_sb[:, dsl]).then_inc(sOutD[dc % 2], 16)
        p.y(dt_out)

    def finish(y):
        y.wait_ge(p.auto["s"], m_dbf)
        y.dma_start(out=o_bc, in_=bc_sb[:, :]).then_inc(sOutB, 16)
        y.wait_ge(sOutD[0], 32 * (NDC // 2))
        y.wait_ge(sOutD[1], 32 * (NDC // 2))
        y.wait_ge(sOutB, 16)
    p.y(finish)

    st.close()
    return nc


def _mk_masks():
    m = np.zeros((128, 16, 128), np.float32)
    for lb in range(16):
        for j in range(8):
            for n in range(16):
                m[j * 16 + n, lb, 8 * lb + j] = 1.0
    return m.reshape(128, 16 * 128).astype(BF)


def _alog_lanes(alog_slice):
    a = alog_slice.reshape(4, 16, 8, 16)
    return np.ascontiguousarray(a.transpose(2, 3, 0, 1).reshape(128, 64))


def _build_B():
    TC = 1024
    NTC = T // TC
    nc = bass.Bass("TRN2", target_bir_lowering=False)
    dt_in = nc.dram_tensor("dt_in", [512, T], F32, kind="ExternalInput").ap()
    dtu_in = nc.dram_tensor("dtu_in", [512, T], F32, kind="ExternalInput").ap()
    b_in = nc.dram_tensor("b_in", [NST, T], F32, kind="ExternalInput").ap()
    c_in = nc.dram_tensor("c_in", [NST, T], F32, kind="ExternalInput").ap()
    alogl = nc.dram_tensor("alogl", [128, 64], F32, kind="ExternalInput").ap()
    masks = nc.dram_tensor("masks", [128, 16 * 128], BF16,
                           kind="ExternalInput").ap()
    o_y = nc.dram_tensor("o_y", [512, T], F32, kind="ExternalOutput").ap()

    st = ExitStack()
    E = st.enter_context
    dtr = E(nc.sbuf_tensor("dtr", [128, 2 * 4 * TC], F32))
    dur = E(nc.sbuf_tensor("dur", [128, 2 * 4 * TC], F32))
    a_sb = E(nc.sbuf_tensor("a_sb", [128, 4 * TC], F32))
    b_sb = E(nc.sbuf_tensor("b_sb", [128, 4 * TC], F32))
    h_sb = E(nc.sbuf_tensor("h_sb", [128, 2 * TC], F32))
    tmp_sb = E(nc.sbuf_tensor("tmp_sb", [128, 4 * TC], BF16))
    b16 = E(nc.sbuf_tensor("b16", [128, TC], F32))
    c16 = E(nc.sbuf_tensor("c16", [128, TC], F32))
    nal = E(nc.sbuf_tensor("nal", [128, 64], F32))
    msk = E(nc.sbuf_tensor("msk", [128, 16 * 128], BF16))
    stt = E(nc.sbuf_tensor("stt", [128, 64], F32))
    y_sb = E(nc.sbuf_tensor("y_sb", [128, 2 * TC], F32))

    Py = [E(nc.psum_tensor(f"Py{i}", [128, 512], F32)) for i in range(4)]

    p = Prog(nc, st)
    sLc = p.sem("sLc")
    sRepD = [p.sem("sRepD0"), p.sem("sRepD1")]
    sRepU = [p.sem("sRepU0"), p.sem("sRepU1")]
    sB16 = p.sem("sB16")
    sYmm = p.sem("sYmm")
    sOutY = [p.sem("sOutY0"), p.sem("sOutY1")]

    def consts(y):
        y.dma_start(out=nal[:, :], in_=alogl).then_inc(sLc, 16)
        y.dma_start(out=msk[:, :], in_=masks).then_inc(sLc, 16)
    p.y(consts)

    def negate(s):
        s.wait_ge(sLc, 32)
        s.activation(out=nal[:, :], in_=nal[:, :], func=AF.Exp)
        s.mul(nal[:, :], nal[:, :], -1.0)
    p.s(negate)
    m_nal = p.mark("s")

    a_mark = {}
    b_mark = {}
    tmp_mark = {}
    scan_mark = {}
    mmcnt_after = {}
    ev_mark = {}
    mm_cnt = 0
    b16_last_v = 0
    b16_last_g = 0

    for tc in range(NTC):
        tsl = slice(tc * TC, (tc + 1) * TC)

        def b16_load(y, tc=tc, tsl=tsl):
            if tc > 0:
                y.wait_ge(p.auto["g"], b16_last_g)
                y.wait_ge(p.auto["v"], b16_last_v)
            y.dma_start(out=b16[:, :],
                        in_=b_in[None, :, tsl].broadcast_to([8, NST, TC])
                        ).then_inc(sB16, 16)
            y.dma_start(out=c16[:, :],
                        in_=c_in[None, :, tsl].broadcast_to([8, NST, TC])
                        ).then_inc(sB16, 16)
        p.y(b16_load)

        for cc in range(4):
            for gr in range(4):
                gidx = (tc * 4 + cc) * 4 + gr
                slot = gidx % 2

                def rep_load(y, tc=tc, cc=cc, gr=gr, gidx=gidx, slot=slot):
                    if gidx >= 2:
                        pg = gidx - 2
                        y.wait_ge(p.auto["s"], a_mark[(pg, 3)])
                        y.wait_ge(p.auto["g"], b_mark[(pg, 3)])
                    r0 = cc * 128 + gr * 32
                    for src, dst, sem in ((dt_in, dtr, sRepD[slot]),
                                          (dtu_in, dur, sRepU[slot])):
                        for k in range(4):
                            v = src[r0 + 8 * k:r0 + 8 * k + 8,
                                    tc * TC:(tc + 1) * TC]
                            v = v[:, None, :].broadcast_to([8, NST, TC])
                            y.dma_start(
                                out=dst[:, (slot * 4 + k) * TC:
                                        (slot * 4 + k + 1) * TC],
                                in_=v).then_inc(sem, 16)
                p.y(rep_load)

                def a_ops(s, tc=tc, cc=cc, gr=gr, gidx=gidx, slot=slot):
                    s.wait_mark("s", m_nal)
                    s.wait_ge(sRepD[slot], 64 * (gidx // 2 + 1))
                    for k in range(4):
                        li = gidx * 4 + k
                        if li >= 4:
                            s.wait_mark("v", scan_mark.get(li - 4, 0))
                        col = cc * 16 + gr * 4 + k
                        s.activation(
                            out=a_sb[:, (li % 4) * TC:(li % 4 + 1) * TC],
                            in_=dtr[:, (slot * 4 + k) * TC:(slot * 4 + k + 1) * TC],
                            func=AF.Exp, scale=nal[:, col:col + 1])
                        a_mark[(gidx, k)] = p.cnt["s"]
                p.s(a_ops)

                def b_ops(g, tc=tc, gidx=gidx, slot=slot):
                    g.wait_ge(sRepU[slot], 64 * (gidx // 2 + 1))
                    g.wait_ge(sB16, 32 * (tc + 1))
                    for k in range(4):
                        li = gidx * 4 + k
                        if li >= 4:
                            g.wait_mark("v", scan_mark.get(li - 4, 0))
                        g.tensor_tensor(
                            out=b_sb[:, (li % 4) * TC:(li % 4 + 1) * TC],
                            in0=dur[:, (slot * 4 + k) * TC:(slot * 4 + k + 1) * TC],
                            in1=b16[:, :], op=ALU.mult)
                        b_mark[(gidx, k)] = p.cnt["g"]
                p.g(b_ops)

                def v_ops(v, tc=tc, cc=cc, gr=gr, gidx=gidx):
                    nonlocal b16_last_v
                    for k in range(4):
                        li = gidx * 4 + k
                        col = cc * 16 + gr * 4 + k
                        v.wait_mark("s", a_mark[(gidx, k)])
                        v.wait_mark("g", b_mark[(gidx, k)])
                        asl = a_sb[:, (li % 4) * TC:(li % 4 + 1) * TC]
                        bsl = b_sb[:, (li % 4) * TC:(li % 4 + 1) * TC]
                        hsl = h_sb[:, (li % 2) * TC:(li % 2 + 1) * TC]
                        init = 0.0 if tc == 0 else stt[:, col:col + 1]
                        v.tensor_tensor_scan(out=hsl, data0=asl, data1=bsl,
                                             initial=init, op0=ALU.mult,
                                             op1=ALU.add)
                        scan_mark[li] = p.cnt["v"]
                        v.tensor_copy(out=stt[:, col:col + 1],
                                      in_=hsl[:, TC - 1:TC])
                        if li >= 4:
                            v.wait_ge(sYmm, mmcnt_after.get(li - 4, 0))
                        v.tensor_tensor(
                            out=tmp_sb[:, (li % 4) * TC:(li % 4 + 1) * TC],
                            in0=hsl, in1=c16[:, :], op=ALU.mult)
                        tmp_mark[li] = p.cnt["v"]
                        b16_last_v = p.cnt["v"]
                p.v(v_ops)

                def t_ops(t, tc=tc, cc=cc, gr=gr, gidx=gidx):
                    nonlocal mm_cnt
                    pb = 2 * (cc % 2)
                    for k in range(4):
                        li = gidx * 4 + k
                        lb = gr * 4 + k
                        t.wait_ge(p.auto["v"], tmp_mark[li])
                        if lb == 0:
                            t.wait_ge(p.auto["s"], ev_mark.get((tc - 1, cc), 0)
                                      if tc > 0 else 0)
                        for h in range(2):
                            t.matmul(
                                Py[pb + h][:, :],
                                msk[:, lb * 128:(lb + 1) * 128],
                                tmp_sb[:, (li % 4) * TC + h * 512:
                                       (li % 4) * TC + (h + 1) * 512],
                                start=(lb == 0), stop=(lb == 15)
                            ).then_inc(sYmm, 1)
                            mm_cnt += 1
                        mmcnt_after[li] = mm_cnt
                p.t(t_ops)

            def y_evac(s, tc=tc, cc=cc):
                pb = 2 * (cc % 2)
                s.wait_ge(sYmm, mm_cnt)
                s.wait_ge(sOutY[cc % 2], 16 * (tc * 2 + cc // 2))
                for h in range(2):
                    s.activation(out=y_sb[:, (cc % 2) * TC + h * 512:
                                          (cc % 2) * TC + (h + 1) * 512],
                                 in_=Py[pb + h][:, :], func=AF.Copy)
                ev_mark[(tc, cc)] = p.cnt["s"]
            p.s(y_evac)

            def y_out(y, tc=tc, cc=cc):
                y.wait_ge(p.auto["s"], ev_mark[(tc, cc)])
                y.dma_start(
                    out=o_y[cc * 128:(cc + 1) * 128, tc * TC:(tc + 1) * TC],
                    in_=y_sb[:, (cc % 2) * TC:(cc % 2 + 1) * TC]
                ).then_inc(sOutY[cc % 2], 16)
            p.y(y_out)

        def b16_done(g, tc=tc):
            nonlocal b16_last_g
            b16_last_g = p.cnt["g"]
        b16_done(None)

    def finish(y):
        y.wait_ge(sOutY[0], 16 * 8)
        y.wait_ge(sOutY[1], 16 * 8)
    p.y(finish)

    st.close()
    return nc




def _ln_fm(p, P, MM, mm_cnt, sMM, cst_sb, or_sb, rows, lnscr, src_sb, dst_sb,
           row0, extra_wait_mark=None, first_sq_waits=()):
    m_src = p.mark("v")
    R = lambda k: rows[0:1, (row0 + k) * TPC:(row0 + k) * TPC + TPC]

    def s1mm(t):
        t.wait_ge(p.auto["v"], m_src)
        if extra_wait_mark is not None:
            t.wait_ge(p.auto["s"], extra_wait_mark)
        for dc in range(NDC):
            for h in range(2):
                MM(t, P[h][0:1, :], cst_sb[:, 3:4],
                   src_sb[:, dc * TPC + h * 512:dc * TPC + (h + 1) * 512],
                   start=(dc == 0), stop=(dc == NDC - 1))
    p.t(s1mm)

    sqm = {}
    mm_after = {}
    for dc in range(NDC):
        def sq(s, dc=dc):
            if dc == 0:
                for sem_, val_ in first_sq_waits:
                    s.wait_ge(sem_, val_)
                s.wait_ge(p.auto["v"], m_src)
            if dc >= 1:
                s.wait_ge(sMM, mm_after[dc - 1])
            s.activation(out=lnscr[:, 0:TPC],
                         in_=src_sb[:, dc * TPC:(dc + 1) * TPC], func=AF.Square)
            sqm[dc] = p.cnt["s"]
        p.s(sq)

        def s2mm(t, dc=dc):
            t.wait_ge(p.auto["s"], sqm[dc])
            for h in range(2):
                MM(t, P[2 + h][0:1, :], cst_sb[:, 3:4],
                   lnscr[:, h * 512:(h + 1) * 512],
                   start=(dc == 0), stop=(dc == NDC - 1))
            mm_after[dc] = mm_cnt[0]
        p.t(s2mm)
    mm_s2 = mm_cnt[0]

    def rowstats(v):
        v.wait_ge(sMM, mm_s2)
        for h in range(2):
            hs = slice(h * 512, (h + 1) * 512)
            v.tensor_scalar_mul(R(0)[:, hs], P[h][0:1, :], 1.0 / D)
            v.tensor_scalar_mul(R(1)[:, hs], P[2 + h][0:1, :], 1.0 / D)
        v.tensor_mul(R(2), R(0), R(0))
        v.tensor_sub(R(1), R(1), R(2))
    p.v(rowstats)
    m_vr = p.mark("v")

    def rowrstd(s):
        s.wait_mark("v", m_vr)
        s.activation(out=R(1), in_=R(1), func=AF.Ln, bias=LN_EPS)
        s.activation(out=R(1), in_=R(1), func=AF.Exp, scale=-0.5)
    p.s(rowrstd)
    m_rr = p.mark("s")

    def rowmr(v):
        v.wait_mark("s", m_rr)
        v.tensor_mul(R(2), R(0), R(1))
    p.v(rowmr)
    m_mr = p.mark("v")

    def bcast(t):
        t.wait_ge(p.auto["v"], m_mr)
        for h in range(2):
            hs = slice(h * 512, (h + 1) * 512)
            MM(t, P[4 + h][:, :], or_sb[:, :], R(1)[:, hs])
            MM(t, P[6 + h][:, :], or_sb[:, :], R(2)[:, hs])
    p.t(bcast)
    mm_bc = mm_cnt[0]

    def norm(v):
        v.wait_ge(sMM, mm_bc)
        for dc in range(NDC):
            for h in range(2):
                cs = slice(dc * TPC + h * 512, dc * TPC + (h + 1) * 512)
                scr = lnscr[:, h * 512:(h + 1) * 512]
                v.tensor_mul(scr, src_sb[:, cs], P[4 + h][:, :])
                v.tensor_sub(dst_sb[:, cs], scr, P[6 + h][:, :])
    p.v(norm)
    return p.mark("v")


LN2C = 0.6931471805599453
CONSTS_C = [LN_EPS, -0.5, LN2C, 1.0]


def _build_C1():
    nc = bass.Bass("TRN2", target_bir_lowering=False)
    inp = lambda n, sh, dt=F32: nc.dram_tensor(n, sh, dt, kind="ExternalInput").ap()
    xr = inp("xr", [TPC, D])
    yin = inp("yin", [D, TPC])
    dpc = inp("dpc", [128, NDC])
    p1t = inp("p1t", [D, 4 * D])
    b1c = inp("b1c", [128, 64])
    p2t = inp("p2t", [4 * D, D])
    b2c = inp("b2c", [128, NDC])
    ident = inp("ident", [128, 128])
    onesrow = inp("onesrow", [1, 128])
    constv = inp("constv", [128, len(CONSTS_C)])
    o_x2 = nc.dram_tensor("o_x2", [D, TPC], F32, kind="ExternalOutput").ap()

    st = ExitStack()
    E = st.enter_context
    x1 = E(nc.sbuf_tensor("x1", [128, NDC * TPC], F32))
    x1n = E(nc.sbuf_tensor("x1n", [128, NDC * TPC], BF16))
    lnscr = E(nc.sbuf_tensor("lnscr", [128, TPC], F32))
    rows = E(nc.sbuf_tensor("rows", [1, 4 * TPC], F32))
    statc = E(nc.sbuf_tensor("statc", [128, NTT * 8], F32))
    dp_sb = E(nc.sbuf_tensor("dp_sb", [128, NDC], F32))
    b1_sb = E(nc.sbuf_tensor("b1_sb", [128, 64], F32))
    b2_sb = E(nc.sbuf_tensor("b2_sb", [128, NDC], F32))
    id_sb = E(nc.sbuf_tensor("id_sb", [128, 128], F32))
    or_sb = E(nc.sbuf_tensor("or_sb", [1, 128], F32))
    cst_sb = E(nc.sbuf_tensor("cst_sb", [128, len(CONSTS_C)], F32))
    P = [E(nc.psum_tensor(f"P{i}", [128, 512], F32)) for i in range(8)]

    for i, v in enumerate(CONSTS_C):
        nc.const_aps.register(F32, v, cst_sb[:, i:i + 1])

    p = Prog(nc, st)
    sLc = p.sem("sLc")
    sLx = [p.sem("sLx0"), p.sem("sLx1")]
    sLy = [p.sem("sLy0"), p.sem("sLy1")]
    sTr = p.sem("sTr")
    sMM = p.sem("sMM")
    sOut = [p.sem("sOut0"), p.sem("sOut1")]
    mm_cnt = [0]

    def MM(t, out_ap, lhs, rhs, start=True, stop=True):
        t.matmul(out_ap, lhs, rhs, start=start, stop=stop).then_inc(sMM, 1)
        mm_cnt[0] += 1

    def consts(y):
        for dst, srcd in ((dp_sb, dpc), (b1_sb, b1c), (b2_sb, b2c),
                          (id_sb, ident), (or_sb, onesrow), (cst_sb, constv)):
            y.dma_start(out=dst[:, :], in_=srcd).then_inc(sLc, 16)
    p.y(consts)
    NLC = 16 * 6

    ph1 = ExitStack()
    E1 = ph1.enter_context
    xtm = E1(nc.sbuf_tensor("xtm", [128, 2 * D], F32))
    sqt = E1(nc.sbuf_tensor("sqt", [128, 2 * D], F32))
    x0t = E1(nc.sbuf_tensor("x0t", [128, 2 * D], F32))
    y_t = E1(nc.sbuf_tensor("y_t", [128, 2 * D], F32))

    aff_mark = [0] * NTT
    red_mark = [0] * NTT
    ev1_mark = [0] * NTT

    for tt in range(NTT):
        sl = slice((tt % 2) * D, (tt % 2) * D + D)

        def load(y, tt=tt, sl=sl):
            if tt >= 2:
                y.wait_ge(p.auto["v"], aff_mark[tt - 2])
                y.wait_ge(sTr, 2 * (tt - 1))
            y.dma_start(out=xtm[:, sl],
                        in_=xr[tt * 128:(tt + 1) * 128, :]).then_inc(sLx[tt % 2], 16)
            if tt >= 2:
                y.wait_ge(p.auto["v"], ev1_mark[tt - 2])
            y.dma_start(
                out=y_t[:, sl],
                in_=yin.rearrange("(c p) t -> p c t", p=128)
                [:, :, tt * 128:(tt + 1) * 128]).then_inc(sLy[tt % 2], 16)
        p.y(load)

        def square(s, tt=tt, sl=sl):
            s.wait_ge(sLc, NLC)
            s.wait_ge(sLx[tt % 2], 16 * (tt // 2 + 1))
            if tt >= 2:
                s.wait_mark("v", red_mark[tt - 2])
            s.activation(out=sqt[:, sl], in_=xtm[:, sl], func=AF.Square)
        p.s(square)
        m_sq = p.mark("s")

        def stats(v, tt=tt, sl=sl):
            c = lambda k: statc[:, tt * 8 + k: tt * 8 + k + 1]
            v.wait_mark("s", m_sq)
            v.reduce_sum(c(0), xtm[:, sl], axis=AX.X)
            v.reduce_sum(c(1), sqt[:, sl], axis=AX.X)
            red_mark[tt] = p.cnt["v"]
            v.tensor_scalar_mul(c(2), c(0), 1.0 / D)
            v.tensor_scalar_mul(c(3), c(1), 1.0 / D)
            v.tensor_mul(c(7), c(2), c(2))
            v.tensor_sub(c(4), c(3), c(7))
        p.v(stats)
        m_var = p.mark("v")

        def rstd(s, tt=tt):
            c = lambda k: statc[:, tt * 8 + k: tt * 8 + k + 1]
            s.wait_mark("v", m_var)
            s.activation(out=c(5), in_=c(4), func=AF.Ln, bias=LN_EPS)
            s.activation(out=c(6), in_=c(5), func=AF.Exp, scale=-0.5)
        p.s(rstd)
        m_rstd = p.mark("s")

        def affine(v, tt=tt, sl=sl):
            c = lambda k: statc[:, tt * 8 + k: tt * 8 + k + 1]
            v.wait_mark("s", m_rstd)
            if tt >= 2:
                v.wait_ge(sTr, 2 * (tt - 1))
            v.tensor_scalar(out=x0t[:, sl], in0=xtm[:, sl], scalar1=c(2),
                            scalar2=c(6), op0=ALU.subtract, op1=ALU.mult)
            aff_mark[tt] = p.cnt["v"]
        p.v(affine)

        def transposes(t, tt=tt):
            t.wait_ge(p.auto["v"], aff_mark[tt])
            t.wait_ge(sLc, NLC)
            if tt >= 1:
                t.wait_ge(p.auto["v"], ev1_mark[tt - 1])
            last = None
            for k in range(NDC):
                last = t.transpose(
                    P[k // 4][:, (k % 4) * 128:(k % 4) * 128 + 128],
                    xtm[:, (tt % 2) * D + k * 128:(tt % 2) * D + (k + 1) * 128],
                    id_sb[:, :])
            for k in range(NDC):
                last = t.transpose(
                    P[4 + k // 4][:, (k % 4) * 128:(k % 4) * 128 + 128],
                    x0t[:, (tt % 2) * D + k * 128:(tt % 2) * D + (k + 1) * 128],
                    id_sb[:, :])
            last.then_inc(sTr, 2)
        p.t(transposes)

        def ev1(v, tt=tt):
            v.wait_ge(sTr, 2 * (tt + 1))
            v.wait_ge(sLy[tt % 2], 16 * (tt // 2 + 1))
            for k in range(NDC):
                xb = P[k // 4][:, (k % 4) * 128:(k % 4) * 128 + 128]
                nb = P[4 + k // 4][:, (k % 4) * 128:(k % 4) * 128 + 128]
                cs = slice(k * TPC + tt * 128, k * TPC + (tt + 1) * 128)
                v.scalar_tensor_tensor(
                    out=x1[:, cs], in0=nb, scalar=dp_sb[:, k:k + 1],
                    in1=y_t[:, (tt % 2) * D + k * 128:(tt % 2) * D + (k + 1) * 128],
                    op0=ALU.mult, op1=ALU.add)
                v.tensor_add(x1[:, cs], x1[:, cs], xb)
            ev1_mark[tt] = p.cnt["v"]
        p.v(ev1)

    ph1.close()

    ln1_mark = _ln_fm(p, P, MM, mm_cnt, sMM, cst_sb, or_sb, rows, lnscr,
                      x1, x1n, 0)

    ph2 = ExitStack()
    E2 = ph2.enter_context
    hh = E2(nc.sbuf_tensor("hh", [128, 64 * 512], BF16))
    w1 = E2(nc.sbuf_tensor("w1", [128, 2 * 2048], BF16))
    w2 = E2(nc.sbuf_tensor("w2", [128, 2 * 4096], BF16))

    HOC = 64
    wl_mark = {}
    mmw_mark = {}
    ge_mark = {}
    w2l_mark = {}
    mm2_mark = {}
    ev2_mark = {}

    for h in range(2):
        for oc in range(HOC):
            widx = h * HOC + oc
            ws_ = widx % 2

            def wload(g, oc=oc, widx=widx, ws_=ws_):
                if widx == 0:
                    g.wait_ge(p.auto["v"], ln1_mark)
                if widx >= 2:
                    g.wait_ge(sMM, mmw_mark.get(widx - 2, 0))
                g.dma_start(
                    out=w1[:, ws_ * 2048:ws_ * 2048 + 2048],
                    in_=p1t.rearrange("(c p) o -> p c o", p=128)
                    [:, :, oc * 128:(oc + 1) * 128])
                wl_mark[widx] = p.cnt["gd"]
            p.g(wload)

            def p1mm(t, h=h, oc=oc, widx=widx, ws_=ws_):
                t.wait_ge(p.auto["gd"], wl_mark[widx])
                t.wait_ge(p.auto["v"], ln1_mark)
                if widx >= 2:
                    t.wait_ge(p.auto["s"], ge_mark.get(widx - 2, 0))
                for dc in range(NDC):
                    MM(t, P[widx % 2][:, :],
                       w1[:, ws_ * 2048 + dc * 128: ws_ * 2048 + (dc + 1) * 128],
                       x1n[:, dc * TPC + h * 512: dc * TPC + (h + 1) * 512],
                       start=(dc == 0), stop=(dc == NDC - 1))
                mmw_mark[widx] = mm_cnt[0]
            p.t(p1mm)

            def gelu_ev(s, oc=oc, widx=widx):
                s.wait_ge(sMM, mmw_mark[widx])
                s.activation(out=hh[:, oc * 512:(oc + 1) * 512],
                             in_=P[widx % 2][:, :], func=AF.Gelu,
                             bias=b1_sb[:, oc:oc + 1])
                ge_mark[widx] = p.cnt["s"]
            p.s(gelu_ev)

        for ec in range(NDC):
            eidx = h * NDC + ec
            for kh in range(2):
                ki = 2 * eidx + kh

                def w2load(g, ec=ec, kh=kh, ki=ki):
                    if ki >= 2:
                        g.wait_ge(sMM, mm2_mark.get(ki - 2, 0))
                    g.dma_start(
                        out=w2[:, (ki % 2) * 4096:(ki % 2) * 4096 + 4096],
                        in_=p2t.rearrange("(c p) o -> p c o", p=128)
                        [:, kh * 32:(kh + 1) * 32, ec * 128:(ec + 1) * 128])
                    w2l_mark[ki] = p.cnt["gd"]
                p.g(w2load)

                def p2mm(t, h=h, ec=ec, eidx=eidx, kh=kh, ki=ki):
                    t.wait_ge(p.auto["gd"], w2l_mark[ki])
                    t.wait_ge(p.auto["s"], ge_mark[h * HOC + HOC - 1])
                    if kh == 0 and eidx >= 2:
                        t.wait_ge(p.auto["v"], ev2_mark.get(eidx - 2, 0))
                    for j in range(32):
                        hc = kh * 32 + j
                        MM(t, P[2 + eidx % 2][:, :],
                           w2[:, (ki % 2) * 4096 + j * 128:(ki % 2) * 4096 + (j + 1) * 128],
                           hh[:, hc * 512:(hc + 1) * 512],
                           start=(hc == 0), stop=(hc == 63))
                    mm2_mark[ki] = mm_cnt[0]
                p.t(p2mm)

            def x2ev(v, h=h, ec=ec, eidx=eidx):
                v.wait_ge(sMM, mm2_mark[2 * eidx + 1])
                cs = slice(ec * TPC + h * 512, ec * TPC + (h + 1) * 512)
                v.scalar_tensor_tensor(out=x1[:, cs], in0=P[2 + eidx % 2][:, :],
                                       scalar=b2_sb[:, ec:ec + 1], in1=x1[:, cs],
                                       op0=ALU.add, op1=ALU.add)
                ev2_mark[eidx] = p.cnt["v"]
            p.v(x2ev)

            def x2out(y, h=h, ec=ec, eidx=eidx):
                y.wait_ge(p.auto["v"], ev2_mark[eidx])
                y.dma_start(
                    out=o_x2[ec * 128:(ec + 1) * 128, h * 512:(h + 1) * 512],
                    in_=x1[:, ec * TPC + h * 512:ec * TPC + (h + 1) * 512]
                ).then_inc(sOut[eidx % 2], 16)
            p.y(x2out)

    ph2.close()

    def finish(y):
        y.wait_ge(sOut[0], 16 * NDC)
        y.wait_ge(sOut[1], 16 * NDC)
    p.y(finish)

    st.close()
    return nc




def _tender_quant(p, P, MM, mm_cnt, sMM, id_sb, or_sb, lnscr, bufs,
                  src_sb, dst_bf, start_mm_wait):
    (chb, chb_bf, cmx, statq, tmx, t4, trow, tb, tbj, idxq, scq, rscq) = bufs
    CH = 4
    for ch in range(CH):
        for dc in range(NDC):
            def stat1(v, ch=ch, dc=dc):
                if ch == 0 and dc == 0:
                    v.wait_ge(sMM, start_mm_wait)
                cs = slice(dc * TPC + ch * 256, dc * TPC + (ch + 1) * 256)
                c0 = statq[:, 0:1]; c1 = statq[:, 1:2]; c2 = statq[:, 2:3]
                cb = chb[:, dc * 4 + ch:dc * 4 + ch + 1]
                v.reduce_max(c0, src_sb[:, cs], axis=AX.X)
                v.tensor_reduce(c1, src_sb[:, cs], axis=AX.X, op=ALU.min)
                v.tensor_add(c2, c0, c1)
                v.tensor_scalar_mul(cb, c2, 0.5)
                v.tensor_scalar(out=lnscr[:, 0:256], in0=src_sb[:, cs],
                                scalar1=cb, scalar2=None, op0=ALU.subtract)
                v.reduce_max(cmx[:, ch * NDC + dc:ch * NDC + dc + 1],
                             lnscr[:, 0:256], axis=AX.X,
                             apply_absolute_value=True)
            p.v(stat1)

        def tmax1(v, ch=ch):
            v.reduce_max(tmx[:, ch:ch + 1],
                         cmx[:, ch * NDC:(ch + 1) * NDC], axis=AX.X)
        p.v(tmax1)
    m_tmx = p.mark("v")

    def tmaxtr(t):
        t.wait_ge(p.auto["v"], m_tmx)
        t.transpose(P[0][0:4, 0:128], tmx[:, 0:4], id_sb[:, :]).then_inc(sMM, 1)
        mm_cnt[0] += 1
    p.t(tmaxtr)
    mm_t1 = mm_cnt[0]

    def tmaxred(v):
        v.wait_ge(sMM, mm_t1)
        v.reduce_max(t4[0:4, 0:1], P[0][0:4, 0:128], axis=AX.X)
    p.v(tmaxred)
    m_t4 = p.mark("v")

    def tmaxtr2(t):
        t.wait_ge(p.auto["v"], m_t4)
        t.transpose(P[1][0:1, 0:4], t4[0:4, 0:1],
                    id_sb[0:4, 0:4]).then_inc(sMM, 1)
        mm_cnt[0] += 1
    p.t(tmaxtr2)
    mm_t2 = mm_cnt[0]

    def trowev(v):
        v.wait_ge(sMM, mm_t2)
        v.tensor_copy(out=trow[0:1, :], in_=P[1][0:1, 0:4])
    p.v(trowev)
    m_trow = p.mark("v")

    def tbcast(t):
        t.wait_ge(p.auto["v"], m_trow)
        MM(t, P[1][:, 0:4], or_sb[:, :], trow[0:1, :])
    p.t(tbcast)
    mm_t3 = mm_cnt[0]

    def thr(v):
        v.wait_ge(sMM, mm_t3)
        v.tensor_copy(out=tb[:, :], in_=P[1][:, 0:4])
        for ch in range(CH):
            for j in range(7):
                v.tensor_scalar_mul(tbj[:, ch * 7 + j:ch * 7 + j + 1],
                                    tb[:, ch:ch + 1], 2.0 ** (j - 7))
        v.memset(idxq[:, :], 0.0)
        for ch in range(CH):
            for j in range(7):
                v.scalar_tensor_tensor(
                    out=idxq[:, ch * NDC:(ch + 1) * NDC],
                    in0=cmx[:, ch * NDC:(ch + 1) * NDC],
                    scalar=tbj[:, ch * 7 + j:ch * 7 + j + 1],
                    in1=idxq[:, ch * NDC:(ch + 1) * NDC],
                    op0=ALU.is_gt, op1=ALU.add)
    p.v(thr)
    m_idx = p.mark("v")

    def p2i(s):
        s.wait_mark("v", m_idx)
        s.activation(out=idxq[:, :], in_=idxq[:, :], func=AF.Exp, scale=LN2C)
    p.s(p2i)
    m_p2i = p.mark("s")

    def mkscale(v):
        v.wait_mark("s", m_p2i)
        for ch in range(CH):
            v.tensor_scalar(out=scq[:, ch * NDC:(ch + 1) * NDC],
                            in0=idxq[:, ch * NDC:(ch + 1) * NDC],
                            scalar1=tb[:, ch:ch + 1], scalar2=1.0 / 896.0,
                            op0=ALU.mult, op1=ALU.mult)
        v.tensor_scalar_max(scq[:, :], scq[:, :], 1e-9)
        v.reciprocal(rscq[:, :], scq[:, :])
        v.tensor_copy(out=chb_bf[:, :], in_=chb[:, :])
    p.v(mkscale)

    for dc in range(NDC):
        for ch in range(CH):
            def rq(v, dc=dc, ch=ch):
                cs = slice(dc * TPC + ch * 256, dc * TPC + (ch + 1) * 256)
                t0 = lnscr[:, 0:256]
                m0 = lnscr[:, 256:512]
                v.tensor_scalar(out=t0, in0=src_sb[:, cs],
                                scalar1=chb[:, dc * 4 + ch:dc * 4 + ch + 1],
                                scalar2=rscq[:, ch * NDC + dc:ch * NDC + dc + 1],
                                op0=ALU.subtract, op1=ALU.mult)
                v.tensor_scalar_add(m0, t0, LAD4_C)
                _emit_ladder(v, [(lnscr[:, 256:384], lnscr[:, 512:640]),
                                 (lnscr[:, 384:512], lnscr[:, 640:768])],
                             LAD4)
                v.scalar_tensor_tensor(out=t0, in0=m0, scalar=-1.0, in1=t0,
                                       op0=ALU.mult, op1=ALU.add)
                v.tensor_scalar(out=dst_bf[:, cs], in0=t0, scalar1=0.5,
                                scalar2=scq[:, ch * NDC + dc:ch * NDC + dc + 1],
                                op0=ALU.add, op1=ALU.mult)
            p.v(rq)
    return p.mark("v")


def _build_C2():
    nc = bass.Bass("TRN2", target_bir_lowering=False)
    inp = lambda n, sh, dt=F32: nc.dram_tensor(n, sh, dt, kind="ExternalInput").ap()
    x2in = inp("x2in", [D, TPC])
    wrow_d = [inp("qw", [D, D]), inp("kw", [D, D]), inp("vw", [D, D]),
              inp("ow", [D, D])]
    wtr_d = [inp("qwt", [D, D]), inp("kwt", [D, D]), inp("vwt", [D, D])]
    bcols = [inp("qbc", [128, NDC]), inp("kbc", [128, NDC]),
             inp("vbc", [128, NDC])]
    ident = inp("ident", [128, 128])
    onesrow = inp("onesrow", [1, 128])
    constv = inp("constv", [128, len(CONSTS_C)])
    o_q = nc.dram_tensor("o_q", [D, TPC], BF16, kind="ExternalOutput").ap()
    o_k = nc.dram_tensor("o_k", [D, TPC], BF16, kind="ExternalOutput").ap()
    o_v = nc.dram_tensor("o_v", [TPC, D], BF16, kind="ExternalOutput").ap()
    o_oqi = nc.dram_tensor("o_oqi", [D, D], BF16, kind="ExternalOutput").ap()
    o_osc = nc.dram_tensor("o_osc", [128, NDC], F32, kind="ExternalOutput").ap()
    scr = [nc.dram_tensor(f"scr{i}", [D, D], BF16, kind="Internal")
           for i in range(3)]

    SC = 0.125
    st = ExitStack()
    E = st.enter_context
    x2 = E(nc.sbuf_tensor("x2", [128, NDC * TPC], F32))
    resbf = E(nc.sbuf_tensor("resbf", [128, NDC * TPC], BF16))
    lnscr = E(nc.sbuf_tensor("lnscr", [128, TPC], F32))
    rows = E(nc.sbuf_tensor("rows", [1, 4 * TPC], F32))
    id_sb = E(nc.sbuf_tensor("id_sb", [128, 128], F32))
    or_sb = E(nc.sbuf_tensor("or_sb", [1, 128], F32))
    cst_sb = E(nc.sbuf_tensor("cst_sb", [128, len(CONSTS_C)], F32))
    bq_sb = E(nc.sbuf_tensor("bq_sb", [128, 3 * NDC], F32))
    chb = E(nc.sbuf_tensor("chb", [128, NDC * 4], F32))
    chb_bf = E(nc.sbuf_tensor("chb_bf", [128, NDC * 4], BF16))
    cmx = E(nc.sbuf_tensor("cmx", [128, 4 * NDC], F32))
    statq = E(nc.sbuf_tensor("statq", [128, 16], F32))
    tmx = E(nc.sbuf_tensor("tmx", [128, 4], F32))
    t4 = E(nc.sbuf_tensor("t4", [4, 1], F32))
    trow = E(nc.sbuf_tensor("trow", [1, 4], F32))
    tb = E(nc.sbuf_tensor("tb", [128, 4], F32))
    tbj = E(nc.sbuf_tensor("tbj", [128, 4 * 7], F32))
    idxq = E(nc.sbuf_tensor("idxq", [128, 4 * NDC], F32))
    scq = E(nc.sbuf_tensor("scq", [128, 4 * NDC], F32))
    rscq = E(nc.sbuf_tensor("rscq", [128, 4 * NDC], F32))
    ws = E(nc.sbuf_tensor("ws", [128, 4 * NDC], F32))
    wsx = E(nc.sbuf_tensor("wsx", [128, 3 * NDC], F32))
    cbw = E(nc.sbuf_tensor("cbw", [128, 3 * NDC * 4], F32))
    corr_s = E(nc.sbuf_tensor("corr_s", [4, 2048], F32))
    qv_sb = E(nc.sbuf_tensor("qv_sb", [128, 2 * TPC], BF16))
    vtm = E(nc.sbuf_tensor("vtm", [128, 2 * TPC], BF16))
    id_bf = E(nc.sbuf_tensor("id_bf", [128, 128], BF16))
    P = [E(nc.psum_tensor(f"P{i}", [128, 512], F32)) for i in range(8)]

    for i, v in enumerate(CONSTS_C):
        nc.const_aps.register(F32, v, cst_sb[:, i:i + 1])

    p = Prog(nc, st)
    sLc = p.sem("sLc")
    sLx2 = p.sem("sLx2")
    sMM = p.sem("sMM")
    sWr = [p.sem("sWr0"), p.sem("sWr1")]
    sQs = [p.sem("sQs0"), p.sem("sQs1")]
    sTi = p.sem("sTi")
    sOutQ = [p.sem("sOutQ0"), p.sem("sOutQ1")]
    sOutV = [p.sem("sOutV0"), p.sem("sOutV1")]
    sOutM = p.sem("sOutM")
    mm_cnt = [0]

    def MM(t, out_ap, lhs, rhs, start=True, stop=True):
        t.matmul(out_ap, lhs, rhs, start=start, stop=stop).then_inc(sMM, 1)
        mm_cnt[0] += 1

    def consts(y):
        for dst, srcd in ((id_sb, ident), (or_sb, onesrow), (cst_sb, constv)):
            y.dma_start(out=dst[:, :], in_=srcd).then_inc(sLc, 16)
        for i in range(3):
            y.dma_start(out=bq_sb[:, i * NDC:(i + 1) * NDC],
                        in_=bcols[i]).then_inc(sLc, 16)
        y.dma_start(out=x2[:, :],
                    in_=x2in.rearrange("(c p) t -> p c t", p=128)
                    ).then_inc(sLx2, 16)
    p.y(consts)
    NLC = 16 * 6

    def x2wait(v):
        v.wait_ge(sLx2, 16)
        v.wait_ge(sLc, NLC)
        v.tensor_copy(out=id_bf[:, :], in_=id_sb[:, :])
    p.v(x2wait)

    _ln_fm(p, P, MM, mm_cnt, sMM, cst_sb, or_sb, rows, lnscr, x2, x2, 0,
           first_sq_waits=((sLx2, 16), (sLc, NLC)))
    mm_ln = mm_cnt[0]

    tq_bufs = (chb, chb_bf, cmx, statq, tmx, t4, trow, tb, tbj, idxq, scq,
               rscq)
    _tender_quant(p, P, MM, mm_cnt, sMM, id_sb, or_sb, lnscr, tq_bufs,
                  x2, resbf, mm_ln)
    m_res = p.mark("v")

    ph_w = ExitStack()
    Ew = ph_w.enter_context
    wrow = Ew(nc.sbuf_tensor("wrow", [128, 2 * 2048], F32))
    wqr = Ew(nc.sbuf_tensor("wqr", [128, 2 * 2048], F32))
    qir = Ew(nc.sbuf_tensor("qir", [128, 2 * 2048], BF16))
    qr_mark = {}
    for w in range(4):
        for oc in range(NDC):
            qi = w * NDC + oc
            slot = qi % 2

            def wrload(y, w=w, oc=oc, qi=qi, slot=slot):
                if qi < 2:
                    y.wait_ge(p.auto["v"], m_res)
                if qi >= 2:
                    y.wait_ge(p.auto["v"], qr_mark.get(qi - 2, 0))
                y.dma_start(out=wrow[:, slot * 2048:(slot + 1) * 2048],
                            in_=wrow_d[w][oc * 128:(oc + 1) * 128, :]
                            ).then_inc(sWr[slot], 16)
            p.y(wrload)

            def wq_ops(v, w=w, oc=oc, qi=qi, slot=slot):
                ss = slice(slot * 2048, (slot + 1) * 2048)
                v.wait_ge(sWr[slot], 16 * (qi // 2 + 1))
                if qi >= 2:
                    v.wait_ge(sQs[slot], 16 * (qi // 2))
                c = ws[:, w * NDC + oc:w * NDC + oc + 1]
                r = lnscr[:, 0:1]
                v.reduce_max(c, wrow[:, ss], axis=AX.X, apply_absolute_value=True)
                v.tensor_scalar_mul(c, c, 1.0 / 7.0)
                v.reciprocal(r, c)
                v.tensor_scalar_mul(wqr[:, ss], wrow[:, ss], r)
                v.tensor_scalar_add(wrow[:, ss], wqr[:, ss], LAD4_C)
                h0 = slot * 2048
                _emit_ladder(v, [(wrow[:, h0:h0 + 1024], qir[:, h0:h0 + 1024]),
                                 (wrow[:, h0 + 1024:h0 + 2048],
                                  qir[:, h0 + 1024:h0 + 2048])], LAD4)
                v.scalar_tensor_tensor(out=wqr[:, ss], in0=wrow[:, ss],
                                       scalar=-1.0, in1=wqr[:, ss],
                                       op0=ALU.mult, op1=ALU.add)
                v.tensor_scalar_add(qir[:, ss], wqr[:, ss], 0.5)
                qr_mark[qi] = p.cnt["v"]
            p.v(wq_ops)

            def qout(y, w=w, oc=oc, qi=qi, slot=slot):
                y.wait_ge(p.auto["v"], qr_mark[qi])
                dst = o_oqi if w == 3 else scr[w].ap()
                y.dma_start(out=dst[oc * 128:(oc + 1) * 128, :],
                            in_=qir[:, slot * 2048:(slot + 1) * 2048]
                            ).then_inc(sQs[slot], 16)
            p.y(qout)
    ph_w.close()
    m_wq = p.mark("v")

    def wsx_prep(v):
        v.tensor_scalar_mul(wsx[:, 0:NDC], ws[:, 0:NDC], SC)
        v.tensor_copy(out=wsx[:, NDC:3 * NDC], in_=ws[:, NDC:3 * NDC])
    p.v(wsx_prep)

    def oscout(y):
        y.wait_ge(p.auto["v"], m_wq)
        y.dma_start(out=o_osc, in_=ws[:, 3 * NDC:4 * NDC]).then_inc(sOutM, 16)
    p.y(oscout)

    ph_c = ExitStack()
    wtb = ph_c.enter_context(nc.sbuf_tensor("wtb", [128, 2 * 2048], BF16))
    wt_mark = {}
    cbw_done = {}
    for w in range(3):
        for dc in range(NDC):
            ci = w * NDC + dc
            slot = ci % 2

            def wtload(g, w=w, dc=dc, ci=ci, slot=slot):
                if ci < 2:
                    g.wait_ge(p.auto["v"], m_wq)
                if ci >= 2:
                    g.wait_ge(sMM, wt_mark.get(ci - 2, 0))
                g.dma_start(out=wtb[:, slot * 2048:(slot + 1) * 2048],
                            in_=wtr_d[w][dc * 128:(dc + 1) * 128, :])
            p.g(wtload)
            gm = p.mark("g")

            def cmm(t, w=w, dc=dc, ci=ci, slot=slot, gm=gm):
                p.wait_mark(t, "g", gm)
                t.wait_ge(p.auto["v"], m_res)
                if dc == 0:
                    t.wait_ge(p.auto["v"], cbw_done.get(w - 1, 0))
                for og in range(4):
                    MM(t, P[2 + og][0:4, :], chb_bf[:, dc * 4:(dc + 1) * 4],
                       wtb[:, slot * 2048 + og * 512:slot * 2048 + (og + 1) * 512],
                       start=(dc == 0), stop=(dc == NDC - 1))
                wt_mark[ci] = mm_cnt[0]
            p.t(cmm)

        def cevac(s, w=w):
            s.wait_ge(sMM, wt_mark[w * NDC + NDC - 1])
            for og in range(4):
                s.activation(out=corr_s[:, og * 512:(og + 1) * 512],
                             in_=P[2 + og][0:4, :], func=AF.Copy)
        p.s(cevac)
        m_ce = p.mark("s")

        def ctr(t, w=w, m_ce=m_ce):
            t.wait_ge(p.auto["s"], m_ce)
            if w > 0:
                t.wait_ge(p.auto["v"], cbw_done[w - 1])
            last = None
            for oc in range(NDC):
                last = t.transpose(P[6][:, oc * 4:(oc + 1) * 4],
                                   corr_s[0:4, oc * 128:(oc + 1) * 128],
                                   id_sb[0:4, 0:4])
            last.then_inc(sMM, 1)
            mm_cnt[0] += 1
        p.t(ctr)
        mm_ct = mm_cnt[0]

        def cbias(v, w=w, mm_ct=mm_ct):
            v.wait_ge(sMM, mm_ct)
            for oc in range(NDC):
                v.tensor_scalar(
                    out=cbw[:, (w * NDC + oc) * 4:(w * NDC + oc) * 4 + 4],
                    in0=P[6][:, oc * 4:(oc + 1) * 4],
                    scalar1=bq_sb[:, w * NDC + oc:w * NDC + oc + 1],
                    scalar2=SC if w == 0 else 1.0,
                    op0=ALU.add, op1=ALU.mult)
            cbw_done[w] = p.cnt["v"]
        p.v(cbias)
    ph_c.close()
    g_corr_end = p.mark("g")

    ph_q = ExitStack()
    qiT = ph_q.enter_context(nc.sbuf_tensor("qiT", [128, NDC * 2048], BF16))
    ep_mark = {}
    vtr_mark = {}
    for w in range(3):
        def tload(y, w=w):
            if w == 0:
                p.wait_mark(y, "g", g_corr_end)
                y.wait_ge(p.auto["v"], cbw_done[2])
            y.wait_ge(sQs[0], 16 * 32)
            y.wait_ge(sQs[1], 16 * 32)
            if w > 0:
                for oc_ in (NDC - 2, NDC - 1):
                    y.wait_ge(sMM, ep_mark.get((w - 1, oc_, "mm"), 0))
            for dc in range(NDC):
                y.dma_start_transpose(
                    qiT[:, dc * 2048:(dc + 1) * 2048],
                    scr[w].ap()[:, dc * 128:(dc + 1) * 128]
                ).then_inc(sTi, 16)
        p.y(tload)
        m_ti = 16 * NDC * (w + 1)

        for oc in range(NDC):
            bk = [4 + 2 * (oc % 2), 5 + 2 * (oc % 2)]

            def qmm(t, w=w, oc=oc, bk=bk):
                t.wait_ge(sTi, m_ti)
                t.wait_ge(p.auto["v"], m_res)
                if oc >= 2:
                    t.wait_ge(p.auto["v"], ep_mark.get((w, oc - 2, "v"), 0))
                elif oc < 2 and w > 0:
                    t.wait_ge(p.auto["v"],
                              ep_mark.get((w - 1, NDC - 2 + oc, "v"), 0))
                for dc in range(NDC):
                    for h in range(2):
                        MM(t, P[bk[h]][:, :],
                           qiT[:, dc * 2048 + oc * 128:dc * 2048 + (oc + 1) * 128],
                           resbf[:, dc * TPC + h * 512:dc * TPC + (h + 1) * 512],
                           start=(dc == 0), stop=(dc == NDC - 1))
                ep_mark[(w, oc, "mm")] = mm_cnt[0]
            p.t(qmm)

            if w < 2:
                def epi(v, w=w, oc=oc, bk=bk):
                    v.wait_ge(sMM, ep_mark[(w, oc, "mm")])
                    if (w * NDC + oc) >= 2:
                        pw, poc = divmod(w * NDC + oc - 2, NDC)
                        v.wait_ge(sOutQ[oc % 2], 16 * ((w * NDC + oc) // 2))
                    for h in range(2):
                        for c2 in range(2):
                            ch = 2 * h + c2
                            v.tensor_scalar(
                                out=qv_sb[:, (oc % 2) * TPC + ch * 256:
                                          (oc % 2) * TPC + (ch + 1) * 256],
                                in0=P[bk[h]][:, c2 * 256:(c2 + 1) * 256],
                                scalar1=wsx[:, w * NDC + oc:w * NDC + oc + 1],
                                scalar2=cbw[:, (w * NDC + oc) * 4 + ch:
                                            (w * NDC + oc) * 4 + ch + 1],
                                op0=ALU.mult, op1=ALU.add)
                    ep_mark[(w, oc, "v")] = p.cnt["v"]
                p.v(epi)

                def qkout(y, w=w, oc=oc):
                    y.wait_ge(p.auto["v"], ep_mark[(w, oc, "v")])
                    dst = o_q if w == 0 else o_k
                    y.dma_start(out=dst[oc * 128:(oc + 1) * 128, :],
                                in_=qv_sb[:, (oc % 2) * TPC:(oc % 2 + 1) * TPC]
                                ).then_inc(sOutQ[oc % 2], 16)
                p.y(qkout)
            else:
                def epiv(v, w=w, oc=oc, bk=bk):
                    v.wait_ge(sMM, ep_mark[(w, oc, "mm")])
                    if oc >= 2:
                        v.wait_ge(sMM, ep_mark[(2, oc - 2, "vtr")])
                    else:
                        v.wait_ge(sOutQ[0], 16 * 16)
                        v.wait_ge(sOutQ[1], 16 * 16)
                    for h in range(2):
                        for c2 in range(2):
                            ch = 2 * h + c2
                            v.tensor_scalar(
                                out=qv_sb[:, (oc % 2) * TPC + ch * 256:
                                          (oc % 2) * TPC + (ch + 1) * 256],
                                in0=P[bk[h]][:, c2 * 256:(c2 + 1) * 256],
                                scalar1=wsx[:, w * NDC + oc:w * NDC + oc + 1],
                                scalar2=cbw[:, (w * NDC + oc) * 4 + ch:
                                            (w * NDC + oc) * 4 + ch + 1],
                                op0=ALU.mult, op1=ALU.add)
                    ep_mark[(w, oc, "v")] = p.cnt["v"]
                p.v(epiv)

                def vtr(t, oc=oc):
                    t.wait_ge(p.auto["v"], ep_mark[(2, oc, "v")])
                    if oc > 0:
                        t.wait_ge(p.auto["s"], vtr_mark[oc - 1])
                    pbf = P[0][:, :].bitcast(BF16)
                    last = None
                    for tt in range(8):
                        last = t.transpose(
                            pbf[:, tt * 128:(tt + 1) * 128],
                            qv_sb[:, (oc % 2) * TPC + tt * 128:
                                  (oc % 2) * TPC + (tt + 1) * 128],
                            id_bf[:, :])
                    last.then_inc(sMM, 1)
                    mm_cnt[0] += 1
                    ep_mark[(2, oc, "vtr")] = mm_cnt[0]
                p.t(vtr)
                mm_vt = mm_cnt[0]

                def vev(s, oc=oc, mm_vt=mm_vt):
                    s.wait_ge(sMM, mm_vt)
                    if oc >= 2:
                        s.wait_ge(sOutV[oc % 2], 16 * (oc // 2))
                    pbf = P[0][:, :].bitcast(BF16)
                    s.activation(out=vtm[:, (oc % 2) * TPC:(oc % 2 + 1) * TPC],
                                 in_=pbf[:, :], func=AF.Copy)
                    vtr_mark[oc] = p.cnt["s"]
                p.s(vev)

                def vout(y, oc=oc):
                    y.wait_ge(p.auto["s"], vtr_mark[oc])
                    y.dma_start(
                        out=o_v[:, oc * 128:(oc + 1) * 128].rearrange(
                            "(tt p) o -> p tt o", p=128),
                        in_=vtm[:, (oc % 2) * TPC:(oc % 2 + 1) * TPC].rearrange(
                            "p (tt o) -> p tt o", o=128)
                    ).then_inc(sOutV[oc % 2], 16)
                p.y(vout)
    ph_q.close()

    def finish(y):
        y.wait_ge(sOutQ[0], 16 * 16)
        y.wait_ge(sOutQ[1], 16 * 16)
        y.wait_ge(sOutV[0], 16 * 8)
        y.wait_ge(sOutV[1], 16 * 8)
        y.wait_ge(sOutM, 16)
    p.y(finish)

    st.close()
    return nc


CONSTS_D = [LN_EPS, -0.5, LN2C, 1.0, 1e-6]


def _build_D():
    nc = bass.Bass("TRN2", target_bir_lowering=False)
    inp = lambda n, sh, dt=F32: nc.dram_tensor(n, sh, dt, kind="ExternalInput").ap()
    x2f = inp("x2f", [D, TPC])
    qf_d = inp("qf", [D, TPC], BF16)
    kf_d = inp("kf", [D, T], BF16)
    vt_d = inp("vt", [T, D], BF16)
    oqit = inp("oqit", [D, D], BF16)
    osc_d = inp("osc", [128, NDC])
    owt_d = inp("owt", [D, D])
    obc_d = inp("obc", [128, NDC])
    w1t_d = inp("w1t", [D, 4 * D])
    w2t_d = inp("w2t", [D, 4 * D])
    w3t_d = inp("w3t", [4 * D, D])
    ident = inp("ident", [128, 128])
    onesrow = inp("onesrow", [1, 128])
    oab_d = inp("oab", [128, 65])
    or64_d = inp("or64", [65, 128])
    constv = inp("constv", [128, len(CONSTS_D)])
    o_x4 = nc.dram_tensor("o_x4", [D, TPC], F32, kind="ExternalOutput").ap()
    wq_scr = [nc.dram_tensor(f"wqs{i}", [D, 4 * D] if i < 2 else [4 * D, D],
                             BF16, kind="Internal") for i in range(3)]

    st = ExitStack()
    E = st.enter_context
    x2sb = E(nc.sbuf_tensor("x2sb", [128, NDC * TPC], F32))
    lnscr = E(nc.sbuf_tensor("lnscr", [128, TPC], F32))
    rows = E(nc.sbuf_tensor("rows", [1, 3 * TPC], F32))
    id_sb = E(nc.sbuf_tensor("id_sb", [128, 128], F32))
    id_bf = E(nc.sbuf_tensor("id_bf", [128, 128], BF16))
    or_sb = E(nc.sbuf_tensor("or_sb", [1, 128], F32))
    oab_sb = E(nc.sbuf_tensor("oab_sb", [128, 65], F32))
    or64_sb = E(nc.sbuf_tensor("or64_sb", [65, 128], F32))
    cst_sb = E(nc.sbuf_tensor("cst_sb", [128, len(CONSTS_D)], F32))
    osc_sb = E(nc.sbuf_tensor("osc_sb", [128, NDC], F32))
    obc_sb = E(nc.sbuf_tensor("obc_sb", [128, NDC], F32))
    chb = E(nc.sbuf_tensor("chb", [128, NDC * 4], F32))
    chb_bf = E(nc.sbuf_tensor("chb_bf", [128, NDC * 4], BF16))
    cmx = E(nc.sbuf_tensor("cmx", [128, 4 * NDC], F32))
    statq = E(nc.sbuf_tensor("statq", [128, 16], F32))
    tmx = E(nc.sbuf_tensor("tmx", [128, 4], F32))
    t4 = E(nc.sbuf_tensor("t4", [4, 1], F32))
    trow = E(nc.sbuf_tensor("trow", [1, 4], F32))
    tb = E(nc.sbuf_tensor("tb", [128, 4], F32))
    tbj = E(nc.sbuf_tensor("tbj", [128, 4 * 7], F32))
    idxq = E(nc.sbuf_tensor("idxq", [128, 4 * NDC], F32))
    scq = E(nc.sbuf_tensor("scq", [128, 4 * NDC], F32))
    rscq = E(nc.sbuf_tensor("rscq", [128, 4 * NDC], F32))
    cbo = E(nc.sbuf_tensor("cbo", [128, NDC * 4], F32))
    wsc = E(nc.sbuf_tensor("wsc", [128, 8], F32))
    P = [E(nc.psum_tensor(f"P{i}", [128, 512], F32)) for i in range(4)]
    SA2 = E(nc.psum_tensor("SA2", [128, 1024], F32))
    SB2 = E(nc.psum_tensor("SB2", [128, 1024], F32))

    phCtx = ExitStack()
    ctxf = phCtx.enter_context(nc.sbuf_tensor("ctxf", [128, NDC * TPC], BF16))
    PL = P + [SA2[:, 0:512], SA2[:, 512:1024], SB2[:, 0:512], SB2[:, 512:1024]]
    for i, v in enumerate(CONSTS_D):
        nc.const_aps.register(F32, v, cst_sb[:, i:i + 1])

    p = Prog(nc, st)
    sLc = p.sem("sLc")
    sLx2 = p.sem("sLx2")
    sMM = p.sem("sMM")
    sKL = [p.sem("sKL0"), p.sem("sKL1")]
    sVA = [p.sem("sVA0"), p.sem("sVA1")]
    sQL = p.sem("sQL")
    sOQ = [p.sem("sOQ0"), p.sem("sOQ1")]
    sOW = [p.sem("sOW0"), p.sem("sOW1")]
    sW1 = [p.sem("sW10"), p.sem("sW11")]
    sWQ = [p.sem("sWQ0"), p.sem("sWQ1")]
    sWL = [p.sem("sWL0"), p.sem("sWL1")]
    sOut = [p.sem("sOutD0"), p.sem("sOutD1")]
    sCtxB = p.sem("sCtxB")
    mm_cnt = [0]

    def MM(t, out_ap, lhs, rhs, start=True, stop=True):
        t.matmul(out_ap, lhs, rhs, start=start, stop=stop).then_inc(sMM, 1)
        mm_cnt[0] += 1

    def consts(y):
        for dst, srcd in ((id_sb, ident), (or_sb, onesrow), (cst_sb, constv),
                          (oab_sb, oab_d), (osc_sb, osc_d), (obc_sb, obc_d),
                          (or64_sb, or64_d)):
            y.dma_start(out=dst[:, :], in_=srcd).then_inc(sLc, 16)
        y.dma_start(out=x2sb[:, :],
                    in_=x2f.rearrange("(c p) t -> p c t", p=128)
                    ).then_inc(sLx2, 16)
    p.y(consts)
    NLC = 16 * 7

    phA = ExitStack()
    EA = phA.enter_context
    qsb = EA(nc.sbuf_tensor("qsb", [128, NDC * TPC], BF16))
    khp = EA(nc.sbuf_tensor("khp", [128, T], BF16))
    vaA = EA(nc.sbuf_tensor("vaA", [128, 32 * 65], BF16))
    vaB = EA(nc.sbuf_tensor("vaB", [128, 32 * 65], BF16))
    ptA = EA(nc.sbuf_tensor("ptA", [128, 2 * 1024], BF16))
    ptB = EA(nc.sbuf_tensor("ptB", [128, 2 * 1024], BF16))
    sqs = EA(nc.sbuf_tensor("sqs", [128, T], F32))
    qsq = EA(nc.sbuf_tensor("qsq", [128, TPC], F32))
    nrm = EA(nc.sbuf_tensor("nrm", [128, 8], F32))
    mcols = EA(nc.sbuf_tensor("mcols", [128, 2], F32))
    rl_sb = EA(nc.sbuf_tensor("rl_sb", [1, 2 * 512], F32))
    rlb = EA(nc.sbuf_tensor("rlb", [128, 2 * 512], F32))
    ctmp = EA(nc.sbuf_tensor("ctmp", [64, 512], BF16))

    def qload(y):
        y.dma_start(out=qsb[:, :],
                    in_=qf_d.rearrange("(c p) t -> p c t", p=128)
                    ).then_inc(sQL, 16)
    p.y(qload)

    def idcast(v):
        v.wait_ge(sLc, NLC)
        v.tensor_copy(out=id_bf[:, :], in_=id_sb[:, :])
        for ti in range(32):
            v.memset(vaA[:, ti * 65 + 64:ti * 65 + 65], 1.0)
            v.memset(vaB[:, ti * 65 + 64:ti * 65 + 65], 1.0)
    p.v(idcast)
    m_idc = p.mark("v")

    ev_ctx = {}
    exp_mark = {}
    sv_done = {}

    for hp in range(16):
        slot = hp % 2

        def kvload(y, hp=hp, slot=slot):
            if hp >= 1:
                y.wait_ge(sMM, sv_done[(hp - 1, "smm")])
            else:
                y.wait_ge(p.auto["v"], m_idc)
            y.dma_start(out=khp[:, :],
                        in_=kf_d[hp * 128:(hp + 1) * 128, :]).then_inc(sKL[0], 16)
            if hp >= 1:
                y.wait_ge(sMM, sv_done[(hp - 1, "pvmm")])
            for hh, va in ((0, vaA), (1, vaB)):
                y.dma_start(
                    out=va.ap().rearrange("p (ti e) -> p ti e", e=65)[:, :, 0:64],
                    in_=vt_d[:, (2 * hp + hh) * 64:(2 * hp + hh + 1) * 64]
                    .rearrange("(ti p) o -> p ti o", p=128)
                ).then_inc(sVA[0], 16)
        p.y(kvload)

        def msq(s, hp=hp, slot=slot):
            s.wait_ge(sQL, 16)
            s.wait_ge(sLc, NLC)
            s.activation(out=qsq[:, :], in_=qsb[:, hp * TPC:(hp + 1) * TPC],
                         func=AF.Square)
            s.wait_ge(sKL[0], 16 * (hp + 1))
            s.activation(out=sqs[:, :], in_=khp[:, :], func=AF.Square)
        p.s(msq)
        m_msq = p.mark("s")

        def mnorm_mm(t, hp=hp, slot=slot):
            t.wait_ge(p.auto["s"], m_msq)
            if hp >= 1:
                t.wait_ge(p.auto["s"], exp_mark.get((hp - 1, "last"), 0))
            for h in range(2):
                MM(t, SA2[0:65, h * 512:(h + 1) * 512], oab_sb[:, :],
                   qsq[:, h * 512:(h + 1) * 512])
        p.t(mnorm_mm)
        mm_q = mm_cnt[0]

        def mq_red(v, hp=hp):
            v.wait_ge(sMM, mm_q)
            v.reduce_max(nrm[0:1, 0:1], SA2[0:1, 0:1024], axis=AX.X)
            v.reduce_max(nrm[64:65, 0:1], SA2[64:65, 0:1024], axis=AX.X)
        p.v(mq_red)
        m_nq = p.mark("v")

        knred_mark = {}
        for j in range(8):
            def knorm_mm(t, hp=hp, j=j):
                if j < 2:
                    t.wait_ge(p.auto["v"], m_nq)
                else:
                    t.wait_ge(p.auto["v"], knred_mark[j - 2])
                MM(t, SA2[0:65, (j % 2) * 512:(j % 2) * 512 + 512], oab_sb[:, :],
                   sqs[:, j * 512:(j + 1) * 512])
            p.t(knorm_mm)
            mm_kj = mm_cnt[0]

            def kn_red(v, hp=hp, j=j, mm_kj=mm_kj):
                v.wait_ge(sMM, mm_kj)
                for r in (slice(0, 1), slice(64, 65)):
                    v.reduce_max(nrm[r, 2 + (j % 2):3 + (j % 2)],
                                 SA2[r, (j % 2) * 512:(j % 2) * 512 + 512],
                                 axis=AX.X)
                    if j == 0:
                        v.tensor_copy(out=nrm[r, 4:5], in_=nrm[r, 2:3])
                    else:
                        v.tensor_max(nrm[r, 4:5], nrm[r, 4:5],
                                     nrm[r, 2 + (j % 2):3 + (j % 2)])
                knred_mark[j] = p.cnt["v"]
            p.v(kn_red)

        def kn_fin(v, hp=hp):
            for r in (slice(0, 1), slice(64, 65)):
                v.tensor_mul(nrm[r, 5:6], nrm[r, 0:1], nrm[r, 4:5])
        p.v(kn_fin)
        m_kn = p.mark("v")

        def msqrt(s, hp=hp):
            s.wait_mark("v", m_kn)
            s.activation(out=nrm[0:1, 6:7], in_=nrm[0:1, 5:6], func=AF.Sqrt)
            s.activation(out=nrm[64:65, 6:7], in_=nrm[64:65, 5:6], func=AF.Sqrt)
        p.s(msqrt)
        m_ms = p.mark("s")

        def mneg(v, hp=hp):
            v.wait_mark("s", m_ms)
            v.tensor_scalar_mul(nrm[0:1, 7:8], nrm[0:1, 6:7], -1.0)
            v.tensor_scalar_mul(nrm[64:65, 7:8], nrm[64:65, 6:7], -1.0)
        p.v(mneg)
        m_mn = p.mark("v")

        def mbcast(t, hp=hp):
            t.wait_ge(p.auto["v"], m_mn)
            MM(t, SA2[:, 0:1], or_sb[:, :], nrm[0:1, 7:8])
            MM(t, SA2[:, 512:513], or64_sb[64:65, :], nrm[64:65, 7:8])
        p.t(mbcast)
        mm_mb = mm_cnt[0]

        def mevac(v, hp=hp):
            v.wait_ge(sMM, mm_mb)
            v.tensor_copy(out=mcols[:, 0:1], in_=SA2[:, 0:1])
            v.tensor_copy(out=mcols[:, 1:2], in_=SA2[:, 512:513])
        p.v(mevac)
        m_mc = p.mark("v")

        for tq in range(2):
            for tkp in range(16):
                li = (hp * 2 + tq) * 16 + tkp

                def smm(t, hp=hp, slot=slot, tq=tq, tkp=tkp, li=li):
                    t.wait_ge(p.auto["v"], m_mc)
                    if tkp >= 1:
                        t.wait_ge(p.auto["s"], exp_mark[(li - 1, "e")])
                    elif (hp, tq) != (0, 0):
                        t.wait_ge(p.auto["s"], exp_mark[(li - 1, "e")])
                    for u in range(2):
                        tkt = 2 * tkp + u
                        for hh, bank in ((0, SA2), (1, SB2)):
                            MM(t, bank[:, u * 512:(u + 1) * 512],
                               khp[hh * 64:(hh + 1) * 64,
                                   tkt * 128:(tkt + 1) * 128],
                               qsb[hh * 64:(hh + 1) * 64,
                                   hp * TPC + tq * 512: hp * TPC + (tq + 1) * 512]
                               if False else
                               qsb[:, hp * TPC + tq * 512:
                                   hp * TPC + (tq + 1) * 512][hh * 64:(hh + 1) * 64, :],
                               start=True, stop=True)
                    sv_done[(hp, "smm")] = mm_cnt[0]
                p.t(smm)
                mm_s = mm_cnt[0]

                def expo(s, hp=hp, tq=tq, tkp=tkp, li=li, mm_s=mm_s):
                    s.wait_ge(sMM, mm_s)
                    if li >= 2:
                        s.wait_ge(sMM, sv_done.get((li - 2, "pv"), 0))
                    s.activation(out=ptA[:, (tkp % 2) * 1024:(tkp % 2 + 1) * 1024],
                                 in_=SA2[:, :], func=AF.Exp,
                                 bias=mcols[:, 0:1])
                    s.activation(out=ptB[:, (tkp % 2) * 1024:(tkp % 2 + 1) * 1024],
                                 in_=SB2[:, :], func=AF.Exp,
                                 bias=mcols[:, 1:2])
                    exp_mark[(li, "e")] = p.cnt["s"]
                    exp_mark[(hp, "last")] = p.cnt["s"]
                p.s(expo)

                def pvmm(t, hp=hp, slot=slot, tq=tq, tkp=tkp, li=li):
                    t.wait_ge(p.auto["s"], exp_mark[(li, "e")])
                    t.wait_ge(sVA[0], 32 * (hp + 1))
                    if tkp == 0:
                        prev = (hp, 0) if tq == 1 else (hp - 1, 1)
                        t.wait_ge(p.auto["v"], ev_ctx.get(prev, 0))
                    for u in range(2):
                        tkt = 2 * tkp + u
                        for hh, va, pt in ((0, vaA, ptA), (1, vaB, ptB)):
                            MM(t, P[2 + hh][0:65, :],
                               va[:, tkt * 65:(tkt + 1) * 65],
                               pt[:, (tkp % 2) * 1024 + u * 512:
                                  (tkp % 2) * 1024 + (u + 1) * 512],
                               start=(tkt == 0), stop=(tkt == 31))
                    sv_done[(li, "pv")] = mm_cnt[0]
                    sv_done[(hp, "pvmm")] = mm_cnt[0]
                p.t(pvmm)

            mm_pv = mm_cnt[0]

            def lrec(v, hp=hp, tq=tq, mm_pv=mm_pv):
                v.wait_ge(sMM, mm_pv)
                for hh in range(2):
                    v.tensor_copy(out=rl_sb[0:1, hh * 512:(hh + 1) * 512],
                                  in_=P[2 + hh][64:65, :])
                v.reciprocal(rl_sb[0:1, :], rl_sb[0:1, :])
            p.v(lrec)
            m_lr = p.mark("v")

            def lbcast(t, hp=hp, tq=tq, m_lr=m_lr):
                t.wait_ge(p.auto["v"], m_lr)
                prev = (hp, 0) if tq == 1 else (hp - 1, 1)
                t.wait_ge(p.auto["v"], ev_ctx.get(prev, 0))
                for hh in range(2):
                    MM(t, P[hh][:, :], or_sb[:, :],
                       rl_sb[0:1, hh * 512:(hh + 1) * 512])
            p.t(lbcast)
            mm_lb = mm_cnt[0]

            def ctxev(v, hp=hp, tq=tq, mm_lb=mm_lb):
                idx = hp * 2 + tq
                v.wait_ge(sMM, mm_lb)
                for hh in range(2):
                    v.tensor_copy(out=rlb[:, hh * 512:(hh + 1) * 512],
                                  in_=P[hh][:, :])
                cs = slice(hp * TPC + tq * 512, hp * TPC + (tq + 1) * 512)
                v.tensor_mul(ctxf[0:64, cs], P[2][0:64, :], rlb[0:64, 0:512])
                if idx >= 1:
                    v.wait_ge(sCtxB, 16 * idx)
                v.tensor_mul(ctmp[:, :], P[3][0:64, :], rlb[0:64, 512:1024])
                ev_ctx[(hp, tq)] = p.cnt["v"]
                ev_ctx[hp] = p.cnt["v"]
            p.v(ctxev)

            def ctxb(y, hp=hp, tq=tq):
                y.wait_ge(p.auto["v"], ev_ctx[(hp, tq)])
                cs = slice(hp * TPC + tq * 512, hp * TPC + (tq + 1) * 512)
                y.dma_start(out=ctxf[64:128, cs],
                            in_=ctmp[:, :]).then_inc(sCtxB, 16)
            p.y(ctxb)

    phA.close()

    phB = ExitStack()
    r2bf = phB.enter_context(nc.sbuf_tensor("r2bf", [128, NDC * TPC], BF16))

    def bwait(v):
        v.wait_ge(sLx2, 16)
        v.wait_ge(sCtxB, 16 * 32)
        v.tensor_copy(out=lnscr[:, 0:1], in_=ctxf[:, 0:1])
    p.v(bwait)
    tq_bufs = (chb, chb_bf, cmx, statq, tmx, t4, trow, tb, tbj, idxq, scq,
               rscq)
    m_r2 = _tender_quant(p, PL, MM, mm_cnt, sMM, id_sb, or_sb, lnscr, tq_bufs,
                         ctxf, r2bf, 0)

    phC = ExitStack()
    wtb = phC.enter_context(nc.sbuf_tensor("wtb", [128, 2 * 2048], BF16))
    corr_s = phC.enter_context(nc.sbuf_tensor("corr_s", [4, 2048], F32))
    wt_mark = {}
    for dc in range(NDC):
        slot = dc % 2

        def owload(g, dc=dc, slot=slot):
            if dc < 2:
                g.wait_ge(p.auto["v"], m_r2)
            if dc >= 2:
                g.wait_ge(sMM, wt_mark.get(dc - 2, 0))
            g.dma_start(out=wtb[:, slot * 2048:(slot + 1) * 2048],
                        in_=owt_d[dc * 128:(dc + 1) * 128, :])
        p.g(owload)
        gm = p.mark("g")

        def cmm(t, dc=dc, slot=slot, gm=gm):
            p.wait_mark(t, "g", gm)
            t.wait_ge(p.auto["v"], m_r2)
            for og in range(4):
                MM(t, P[og][0:4, :], chb_bf[:, dc * 4:(dc + 1) * 4],
                   wtb[:, slot * 2048 + og * 512:slot * 2048 + (og + 1) * 512],
                   start=(dc == 0), stop=(dc == NDC - 1))
            wt_mark[dc] = mm_cnt[0]
        p.t(cmm)

    def cevac(s):
        s.wait_ge(sMM, wt_mark[NDC - 1])
        for og in range(4):
            s.activation(out=corr_s[:, og * 512:(og + 1) * 512],
                         in_=P[og][0:4, :], func=AF.Copy)
    p.s(cevac)
    m_ce = p.mark("s")

    def ctr(t, m_ce=m_ce):
        t.wait_ge(p.auto["s"], m_ce)
        last = None
        for oc in range(NDC):
            last = t.transpose(SA2[:, oc * 4:(oc + 1) * 4],
                               corr_s[0:4, oc * 128:(oc + 1) * 128],
                               id_sb[0:4, 0:4])
        last.then_inc(sMM, 1)
        mm_cnt[0] += 1
    p.t(ctr)
    mm_ct = mm_cnt[0]

    def cbias(v, mm_ct=mm_ct):
        v.wait_ge(sMM, mm_ct)
        for oc in range(NDC):
            v.tensor_scalar(out=cbo[:, oc * 4:(oc + 1) * 4],
                            in0=SA2[:, oc * 4:(oc + 1) * 4],
                            scalar1=obc_sb[:, oc:oc + 1], scalar2=None,
                            op0=ALU.add)
    p.v(cbias)
    m_cbo = p.mark("v")

    oq_tile = phC.enter_context(nc.sbuf_tensor("oq_tile", [128, 2 * 2048], BF16))
    op_mark = {}
    for oc in range(NDC):
        slot = oc % 2

        def oqload(y, oc=oc, slot=slot):
            if oc < 2:
                y.wait_ge(p.auto["v"], m_r2)
            if oc >= 2:
                y.wait_ge(sMM, op_mark.get((oc - 2, "mm"), 0))
            y.dma_start(
                out=oq_tile[:, slot * 2048:(slot + 1) * 2048],
                in_=oqit.rearrange("(c p) o -> p c o", p=128)
                [:, :, oc * 128:(oc + 1) * 128]).then_inc(sOQ[slot], 16)
        p.y(oqload)

        def omm(t, oc=oc, slot=slot):
            t.wait_ge(sOQ[slot], 16 * (oc // 2 + 1))
            t.wait_ge(p.auto["v"], m_r2)
            if oc >= 2:
                t.wait_ge(p.auto["v"], op_mark.get((oc - 2, "ev"), 0))
            for h in range(2):
                for dc in range(NDC):
                    MM(t, P[2 * (oc % 2) + h][:, :],
                       oq_tile[:, slot * 2048 + dc * 128:slot * 2048 + (dc + 1) * 128],
                       r2bf[:, dc * TPC + h * 512:dc * TPC + (h + 1) * 512],
                       start=(dc == 0), stop=(dc == NDC - 1))
            op_mark[(oc, "mm")] = mm_cnt[0]
        p.t(omm)

        def oev(v, oc=oc):
            v.wait_ge(sMM, op_mark[(oc, "mm")])
            v.wait_mark("v", m_cbo)
            for h in range(2):
                for c2 in range(2):
                    ch = 2 * h + c2
                    cs = slice(oc * TPC + ch * 256, oc * TPC + (ch + 1) * 256)
                    v.tensor_scalar(out=lnscr[:, 0:256],
                                    in0=P[2 * (oc % 2) + h][:, c2 * 256:(c2 + 1) * 256],
                                    scalar1=osc_sb[:, oc:oc + 1],
                                    scalar2=cbo[:, oc * 4 + ch:oc * 4 + ch + 1],
                                    op0=ALU.mult, op1=ALU.add)
                    v.tensor_add(x2sb[:, cs], x2sb[:, cs], lnscr[:, 0:256])
            op_mark[(oc, "ev")] = p.cnt["v"]
        p.v(oev)
    phC.close()
    phB.close()
    phCtx.close()

    xqf = E(nc.sbuf_tensor("xqf", [128, NDC * TPC], BF16))
    phD = ExitStack()
    ED = phD.enter_context
    x3n = ED(nc.sbuf_tensor("x3n", [128, NDC * TPC], F32))
    xtt = ED(nc.sbuf_tensor("xtt", [128, 2048], F32))
    sqq = ED(nc.sbuf_tensor("sqq", [128, 2048], F32))
    xqt = ED(nc.sbuf_tensor("xqt", [128, 2048], BF16))

    def dwait(v):
        v.wait_ge(p.auto["v"], op_mark[(NDC - 1, "ev")])
        v.tensor_copy(out=lnscr[:, 0:1], in_=x2sb[:, 0:1])
    p.v(dwait)
    m_ln3v = _ln_fm(p, PL, MM, mm_cnt, sMM, cst_sb, or_sb, rows, lnscr,
                    x2sb, x3n, 0)

    RMS_EPS = 1e-6
    AQ_EPS = 1e-5
    xq_mark = {}
    for tt in range(NTT):
        def etr(t, tt=tt):
            if tt == 0:
                t.wait_ge(p.auto["v"], m_ln3v)
            if tt >= 1:
                t.wait_ge(p.auto["s"], xq_mark[(tt - 1, "bev")])
            last = None
            for k in range(NDC):
                last = t.transpose(
                    P[k // 4][:, (k % 4) * 128:(k % 4) * 128 + 128],
                    x3n[:, k * TPC + tt * 128:k * TPC + (tt + 1) * 128],
                    id_sb[:, :])
            last.then_inc(sMM, 1)
            mm_cnt[0] += 1
        p.t(etr)
        mm_et = mm_cnt[0]

        def eev(s, tt=tt, mm_et=mm_et):
            s.wait_ge(sMM, mm_et)
            for b in range(4):
                s.activation(out=xtt[:, b * 512:(b + 1) * 512],
                             in_=P[b][:, :], func=AF.Copy)
            s.activation(out=sqq[:, :], in_=xtt[:, :], func=AF.Square)
        p.s(eev)
        m_eev = p.mark("s")

        def equant(v, tt=tt, m_eev=m_eev):
            c = lambda k: statq[:, k:k + 1]
            v.wait_mark("s", m_eev)
            v.reduce_sum(c(0), sqq[:, :], axis=AX.X)
            v.tensor_scalar_mul(c(0), c(0), 1.0 / D)
            v.reduce_max(c(1), xtt[:, :], axis=AX.X, apply_absolute_value=True)
            xq_mark[(tt, "st")] = p.cnt["v"]
        p.v(equant)
        m_est = p.mark("v")

        def erstd(s, tt=tt, m_est=m_est):
            c = lambda k: statq[:, k:k + 1]
            s.wait_mark("v", m_est)
            s.activation(out=c(2), in_=c(0), func=AF.Ln, bias=RMS_EPS)
            s.activation(out=c(2), in_=c(2), func=AF.Exp, scale=-0.5)
        p.s(erstd)
        m_ers = p.mark("s")

        def equant2(v, tt=tt, m_ers=m_ers):
            c = lambda k: statq[:, k:k + 1]
            v.wait_mark("s", m_ers)
            v.tensor_mul(c(3), c(1), c(2))
            v.tensor_scalar_max(c(3), c(3), AQ_EPS)
            v.reciprocal(c(4), c(3))
            v.tensor_scalar_mul(c(4), c(4), 127.0)
            v.tensor_scalar_mul(c(5), c(3), 1.0 / 127.0)
            v.tensor_mul(c(6), c(2), c(4))
            v.tensor_scalar_mul(sqq[:, :], xtt[:, :], c(6))
            v.tensor_scalar_add(xtt[:, :], sqq[:, :], LAD8_C)
            _emit_ladder(v, [(xtt[:, 0:1024], xqt[:, 0:1024]),
                             (xtt[:, 1024:2048], xqt[:, 1024:2048])], LAD8)
            v.scalar_tensor_tensor(out=sqq[:, :], in0=xtt[:, :], scalar=-1.0,
                                   in1=sqq[:, :], op0=ALU.mult, op1=ALU.add)
            v.tensor_scalar(out=xqt[:, :], in0=sqq[:, :], scalar1=0.5,
                            scalar2=c(5), op0=ALU.add, op1=ALU.mult)
            xq_mark[(tt, "q")] = p.cnt["v"]
        p.v(equant2)

        def ebtr(t, tt=tt):
            t.wait_ge(p.auto["v"], xq_mark[(tt, "q")])
            pbf0 = P[0][:, :].bitcast(BF16)
            pbf1 = P[1][:, :].bitcast(BF16)
            last = None
            for k in range(NDC):
                dstp = pbf0 if k < 8 else pbf1
                last = t.transpose(dstp[:, (k % 8) * 128:(k % 8) * 128 + 128],
                                   xqt[:, k * 128:(k + 1) * 128], id_bf[:, :])
            last.then_inc(sMM, 1)
            mm_cnt[0] += 1
        p.t(ebtr)
        mm_eb = mm_cnt[0]

        def ebev(s, tt=tt, mm_eb=mm_eb):
            s.wait_ge(sMM, mm_eb)
            pbf0 = P[0][:, :].bitcast(BF16)
            pbf1 = P[1][:, :].bitcast(BF16)
            for k in range(NDC):
                dstp = pbf0 if k < 8 else pbf1
                s.activation(out=xqf[:, k * TPC + tt * 128:k * TPC + (tt + 1) * 128],
                             in_=dstp[:, (k % 8) * 128:(k % 8) * 128 + 128],
                             func=AF.Copy)
            xq_mark[(tt, "bev")] = p.cnt["s"]
        p.s(ebev)
    phD.close()

    phF = ExitStack()
    EF = phF.enter_context
    wld = EF(nc.sbuf_tensor("wld", [128, 2 * 4096], F32))
    wab = EF(nc.sbuf_tensor("wab", [128, 4096], F32))
    wacc = EF(nc.sbuf_tensor("wacc", [128, 4], F32))
    wsrc = [w1t_d, w2t_d, w3t_d]
    wtiles = []
    for w, srcd in enumerate(wsrc):
        R, C = srcd.shape
        Ct = min(4096, C)
        for a in range(R // 128):
            for b in range(C // Ct):
                wtiles.append((w, a, b * Ct, Ct))
    f_mark = {}
    wl_cnt = [0, 0]

    def finit(v):
        v.wait_ge(p.auto["s"], xq_mark[(NTT - 1, "bev")])
        v.memset(wacc[:, :], 0.0)
    p.v(finit)
    m_fi = p.mark("v")

    for fi, (w, a_, coff, Ct) in enumerate(wtiles):
        slot = fi % 2

        def fload(y, w=w, a_=a_, coff=coff, Ct=Ct, fi=fi, slot=slot):
            if fi == 0:
                y.wait_ge(p.auto["v"], m_fi)
            if fi >= 2:
                y.wait_ge(p.auto["s"], f_mark[fi - 2])
            y.dma_start(out=wld[:, slot * 4096:slot * 4096 + Ct],
                        in_=wsrc[w][a_ * 128:(a_ + 1) * 128, coff:coff + Ct]
                        ).then_inc(sWL[slot], 16)
            wl_cnt[slot] += 16
        p.y(fload)
        need = wl_cnt[slot]

        def fabs(s, fi=fi, slot=slot, Ct=Ct, need=need):
            s.wait_ge(sWL[slot], need)
            if fi >= 1:
                s.wait_mark("v", f_mark.get(("acc", fi - 1), 0))
            s.activation(out=wab[:, 0:Ct], in_=wld[:, slot * 4096:slot * 4096 + Ct],
                         func=AF.Abs)
            f_mark[fi] = p.cnt["s"]
        p.s(fabs)

        def facc(v, w=w, fi=fi, Ct=Ct):
            v.wait_mark("s", f_mark[fi])
            v.reduce_sum(statq[:, 0:1], wab[:, 0:Ct], axis=AX.X)
            v.tensor_add(wacc[:, w:w + 1], wacc[:, w:w + 1], statq[:, 0:1])
            f_mark[("acc", fi)] = p.cnt["v"]
        p.v(facc)
    m_facc = p.mark("v")

    def fsum_mm(t):
        t.wait_ge(p.auto["v"], m_facc)
        MM(t, P[2][0:1, 0:4], cst_sb[:, 3:4], wacc[:, 0:4])
    p.t(fsum_mm)
    mm_fs = mm_cnt[0]

    def fscale(v):
        v.wait_ge(sMM, mm_fs)
        v.tensor_scalar_mul(rows[0:1, 0:4], P[2][0:1, 0:4], 1.0 / (D * 4 * D))
        v.tensor_scalar_max(rows[0:1, 0:4], rows[0:1, 0:4], 1e-5)
        v.reciprocal(rows[0:1, 4:8], rows[0:1, 0:4])
    p.v(fscale)
    m_fsc = p.mark("v")

    def fbcast(t):
        t.wait_ge(p.auto["v"], m_fsc)
        MM(t, P[2][:, 0:4], or_sb[:, :], rows[0:1, 4:8])
        MM(t, P[3][:, 0:4], or_sb[:, :], rows[0:1, 0:4])
    p.t(fbcast)
    mm_fb = mm_cnt[0]

    def fcols(v):
        v.wait_ge(sMM, mm_fb)
        v.tensor_copy(out=wsc[:, 0:4], in_=P[2][:, 0:4])
        v.tensor_copy(out=wsc[:, 4:8], in_=P[3][:, 0:4])
    p.v(fcols)
    m_wsc = p.mark("v")

    c1s = EF(nc.sbuf_tensor("c1s", [128, 4096], F32))
    c2s = EF(nc.sbuf_tensor("c2s", [128, 4096], F32))
    qts = EF(nc.sbuf_tensor("qts", [128, 4096], BF16))
    g_mark = {}
    w1_cnt = [0, 0]
    wq_cnt = [0, 0]

    for gi, (w, a_, coff, Ct) in enumerate(wtiles):
        slot = gi % 2

        def gload(y, w=w, a_=a_, coff=coff, Ct=Ct, gi=gi, slot=slot):
            if gi == 0:
                y.wait_ge(p.auto["v"], m_wsc)
            if gi >= 2:
                y.wait_ge(p.auto["v"], g_mark[("q", gi - 2)])
            y.dma_start(out=wld[:, slot * 4096:slot * 4096 + Ct],
                        in_=wsrc[w][a_ * 128:(a_ + 1) * 128, coff:coff + Ct]
                        ).then_inc(sW1[slot], 16)
            w1_cnt[slot] += 16
        p.y(gload)
        need1 = w1_cnt[slot]

        def gc2(g, w=w, gi=gi, slot=slot, Ct=Ct, need1=need1):
            g.wait_ge(sW1[slot], need1)
            if gi >= 1:
                g.wait_mark("v", g_mark.get(("q", gi - 1), 0))
            g.tensor_scalar(out=c2s[:, 0:Ct], in0=wld[:, slot * 4096:slot * 4096 + Ct],
                            scalar1=wsc[:, w:w + 1], scalar2=-0.5,
                            op0=ALU.mult, op1=ALU.is_lt)
            g_mark[("c2", gi)] = p.cnt["g"]
        p.g(gc2)

        def gc1(v, w=w, gi=gi, slot=slot, Ct=Ct, need1=need1):
            v.wait_ge(sW1[slot], need1)
            if gi >= 1:
                v.wait_ge(sWQ[0], wq_cnt[0])
                v.wait_ge(sWQ[1], wq_cnt[1])
            v.tensor_scalar(out=c1s[:, 0:Ct], in0=wld[:, slot * 4096:slot * 4096 + Ct],
                            scalar1=wsc[:, w:w + 1], scalar2=0.5,
                            op0=ALU.mult, op1=ALU.is_gt)
            v.wait_mark("g", g_mark[("c2", gi)])
            v.tensor_sub(qts[:, 0:Ct], c1s[:, 0:Ct], c2s[:, 0:Ct])
            g_mark[("q", gi)] = p.cnt["v"]
        p.v(gc1)

        def gout(y, w=w, a_=a_, coff=coff, Ct=Ct, gi=gi, slot=slot):
            y.wait_ge(p.auto["v"], g_mark[("q", gi)])
            y.dma_start(out=wq_scr[w].ap()[a_ * 128:(a_ + 1) * 128, coff:coff + Ct],
                        in_=qts[:, 0:Ct]).then_inc(sWQ[slot], 16)
            wq_cnt[slot] += 16
        p.y(gout)
    phF.close()

    HOC = 64
    QH = 4
    TQH = 256
    hbf = E(nc.sbuf_tensor("hbf", [128, HOC * TQH], BF16))
    sHW = [p.sem("sHW0"), p.sem("sHW1")]
    sW3 = [p.sem("sW30"), p.sem("sW31")]
    h_mark = {}
    m_xq_done = xq_mark[(NTT - 1, "bev")]

    for qh in range(QH):
        phG = ExitStack()
        wqt = phG.enter_context(nc.sbuf_tensor(f"wqt{qh}", [128, 2 * 4096], BF16))
        t1s = phG.enter_context(nc.sbuf_tensor(f"t1s{qh}", [128, TQH], F32))

        for oc in range(HOC):
            oi = qh * HOC + oc
            slot = oi % 2

            def hwload(y, oc=oc, oi=oi, slot=slot):
                if oi == 0:
                    y.wait_ge(sWQ[0], 16 * 64)
                    y.wait_ge(sWQ[1], 16 * 64)
                if oi == 0 and qh > 0:
                    pass
                if oi >= 2:
                    y.wait_ge(p.auto["v"], h_mark[("hev", oi - 2)])
                for wi in range(2):
                    flat = wq_scr[wi].ap().rearrange("(c p) o -> p c o", p=128)
                    y.dma_start(
                        out=wqt[:, slot * 4096 + wi * 2048:
                                slot * 4096 + (wi + 1) * 2048],
                        in_=flat[:, :, oc * 128:(oc + 1) * 128]
                    ).then_inc(sHW[slot], 16)
            p.y(hwload)

            def hmm(t, qh=qh, oc=oc, oi=oi, slot=slot):
                t.wait_ge(sHW[slot], 32 * (oi // 2 + 1))
                if oi == 0:
                    t.wait_ge(p.auto["s"], m_xq_done)
                if oi >= 2:
                    t.wait_ge(p.auto["v"], h_mark[("hev", oi - 2)])
                if oc < 2 and qh > 0:
                    t.wait_ge(sMM, h_mark[("btrlast", qh - 1)])
                for wi in range(2):
                    for ic in range(NDC):
                        MM(t, P[2 * (oc % 2) + wi][:, 0:TQH],
                           wqt[:, slot * 4096 + wi * 2048 + ic * 128:
                               slot * 4096 + wi * 2048 + (ic + 1) * 128],
                           xqf[:, ic * TPC + qh * TQH:ic * TPC + (qh + 1) * TQH],
                           start=(ic == 0), stop=(ic == NDC - 1))
                h_mark[("mm", oi)] = mm_cnt[0]
            p.t(hmm)

            def hsilu(s, oc=oc, oi=oi):
                s.wait_ge(sMM, h_mark[("mm", oi)])
                if oi > 0:
                    s.wait_mark("v", h_mark[("hev", oi - 1)])
                s.activation(out=t1s[:, :], in_=P[2 * (oc % 2)][:, 0:TQH],
                             func=AF.Silu, scale=wsc[:, 4:5])
            p.s(hsilu)
            m_hs = p.mark("s")

            def hev(v, oc=oc, oi=oi, m_hs=m_hs):
                v.wait_mark("s", m_hs)
                v.scalar_tensor_tensor(
                    out=hbf[:, oc * TQH:(oc + 1) * TQH],
                    in0=P[2 * (oc % 2) + 1][:, 0:TQH], scalar=wsc[:, 5:6],
                    in1=t1s[:, :], op0=ALU.mult, op1=ALU.mult)
                h_mark[("hev", oi)] = p.cnt["v"]
            p.v(hev)
        phG.close()
        mG_v = p.mark("v")

        phQ = ExitStack()
        htt = phQ.enter_context(nc.sbuf_tensor(f"htt{qh}", [128, 8192], BF16))
        hq2 = phQ.enter_context(nc.sbuf_tensor(f"hq2{qh}", [128, 4096], F32))
        msc = phQ.enter_context(nc.sbuf_tensor(f"msc{qh}", [128, 4096], F32))

        for tl in range(2):
            for rnd in range(2):
                def hftr(t, qh=qh, tl=tl, rnd=rnd):
                    if (tl, rnd) == (0, 0):
                        t.wait_ge(p.auto["v"], mG_v)
                    else:
                        t.wait_ge(p.auto["s"], h_mark[("fev", qh, tl, rnd - 1)
                                                      if rnd == 1 else
                                                      ("bev", qh, tl - 1, 3)])
                    last = None
                    for k in range(32):
                        hc = rnd * 32 + k
                        pb = P[k // 8][:, :].bitcast(BF16)
                        last = t.transpose(
                            pb[:, (k % 8) * 128:(k % 8) * 128 + 128],
                            hbf[:, hc * TQH + tl * 128:hc * TQH + tl * 128 + 128],
                            id_bf[:, :])
                    last.then_inc(sMM, 1)
                    mm_cnt[0] += 1
                    h_mark[("ftr", qh, tl, rnd)] = mm_cnt[0]
                p.t(hftr)

                def hfev(s, qh=qh, tl=tl, rnd=rnd):
                    s.wait_ge(sMM, h_mark[("ftr", qh, tl, rnd)])
                    for b in range(4):
                        pb = P[b][:, :].bitcast(BF16)
                        s.activation(out=htt[:, rnd * 4096 + b * 1024:
                                             rnd * 4096 + (b + 1) * 1024],
                                     in_=pb[:, :], func=AF.Copy)
                    h_mark[("fev", qh, tl, rnd)] = p.cnt["s"]
                p.s(hfev)

            def hsq1(s, qh=qh, tl=tl):
                s.wait_ge(p.auto["s"], 0)
                s.activation(out=hq2[:, :], in_=htt[:, 0:4096], func=AF.Square)
                s.activation(out=msc[:, :], in_=htt[:, 4096:8192], func=AF.Square)
            p.s(hsq1)
            m_hq1 = p.mark("s")

            def hqstat(v, qh=qh, tl=tl, m_hq1=m_hq1):
                c = lambda k: statq[:, k:k + 1]
                v.wait_mark("s", m_hq1)
                v.reduce_sum(c(0), hq2[:, :], axis=AX.X)
                v.reduce_sum(c(1), msc[:, :], axis=AX.X)
                v.tensor_add(c(0), c(0), c(1))
                v.tensor_scalar_mul(c(0), c(0), 1.0 / (4 * D))
                v.reduce_max(c(1), htt[:, 0:4096], axis=AX.X,
                             apply_absolute_value=True)
                v.reduce_max(c(2), htt[:, 4096:8192], axis=AX.X,
                             apply_absolute_value=True)
                v.tensor_max(c(1), c(1), c(2))
            p.v(hqstat)
            m_hst = p.mark("v")

            def hrstd(s, qh=qh, tl=tl, m_hst=m_hst):
                c = lambda k: statq[:, k:k + 1]
                s.wait_mark("v", m_hst)
                s.activation(out=c(2), in_=c(0), func=AF.Ln, bias=RMS_EPS)
                s.activation(out=c(2), in_=c(2), func=AF.Exp, scale=-0.5)
            p.s(hrstd)
            m_hrs = p.mark("s")

            def hqops(v, qh=qh, tl=tl, m_hrs=m_hrs):
                c = lambda k: statq[:, k:k + 1]
                v.wait_mark("s", m_hrs)
                v.tensor_mul(c(3), c(1), c(2))
                v.tensor_scalar_max(c(3), c(3), AQ_EPS)
                v.reciprocal(c(4), c(3))
                v.tensor_scalar_mul(c(4), c(4), 127.0)
                v.tensor_scalar_mul(c(5), c(3), 1.0 / 127.0)
                v.tensor_mul(c(6), c(2), c(4))
                for half in range(2):
                    h0 = half * 4096
                    hs_ = slice(h0, h0 + 4096)
                    v.tensor_scalar_mul(hq2[:, :], htt[:, hs_], c(6))
                    v.tensor_scalar_add(msc[:, :], hq2[:, :], LAD8_C)
                    _emit_ladder(v, [(msc[:, 0:2048], htt[:, h0:h0 + 2048]),
                                     (msc[:, 2048:4096],
                                      htt[:, h0 + 2048:h0 + 4096])], LAD8)
                    v.scalar_tensor_tensor(out=hq2[:, :], in0=msc[:, :],
                                           scalar=-1.0, in1=hq2[:, :],
                                           op0=ALU.mult, op1=ALU.add)
                    v.tensor_scalar(out=htt[:, hs_], in0=hq2[:, :], scalar1=0.5,
                                    scalar2=c(5), op0=ALU.add, op1=ALU.mult)
                h_mark[("q", qh, tl)] = p.cnt["v"]
            p.v(hqops)

            for rnd in range(4):
                def hbtr(t, qh=qh, tl=tl, rnd=rnd):
                    if rnd == 0:
                        t.wait_ge(p.auto["v"], h_mark[("q", qh, tl)])
                    else:
                        t.wait_ge(p.auto["s"], h_mark[("bev", qh, tl, rnd - 1)])
                    last = None
                    for k in range(16):
                        hc = rnd * 16 + k
                        pb = P[k // 8][:, :].bitcast(BF16)
                        last = t.transpose(
                            pb[:, (k % 8) * 128:(k % 8) * 128 + 128],
                            htt[:, hc * 128:(hc + 1) * 128], id_bf[:, :])
                    last.then_inc(sMM, 1)
                    mm_cnt[0] += 1
                    h_mark[("btr", qh, tl, rnd)] = mm_cnt[0]
                p.t(hbtr)

                def hbev(s, qh=qh, tl=tl, rnd=rnd):
                    s.wait_ge(sMM, h_mark[("btr", qh, tl, rnd)])
                    for k in range(16):
                        hc = rnd * 16 + k
                        pb = P[k // 8][:, :].bitcast(BF16)
                        s.activation(
                            out=hbf[:, hc * TQH + tl * 128:hc * TQH + tl * 128 + 128],
                            in_=pb[:, (k % 8) * 128:(k % 8) * 128 + 128],
                            func=AF.Copy)
                    h_mark[("bev", qh, tl, rnd)] = p.cnt["s"]
                p.s(hbev)
        h_mark[("btrlast", qh)] = h_mark[("btr", qh, 1, 3)]
        h_mark[("bevlast", qh)] = h_mark[("bev", qh, 1, 3)]
        phQ.close()

        phI = ExitStack()
        w3tl = phI.enter_context(nc.sbuf_tensor(f"w3tl{qh}", [128, 2 * 8192], BF16))
        for ec in range(NDC):
            ei = qh * NDC + ec
            slot = ei % 2

            def w3load(y, qh=qh, ec=ec, ei=ei, slot=slot):
                if ec < 2:
                    y.wait_ge(sMM, h_mark[("mm", qh * HOC + HOC - 1)])
                    y.wait_ge(sMM, h_mark[("btrlast", qh)])
                    y.wait_ge(p.auto["s"], h_mark[("bevlast", qh)])
                if ei >= 2:
                    y.wait_ge(sMM, h_mark.get(("w3mm", ei - 2), 0))
                flat = wq_scr[2].ap().rearrange("(c p) o -> p c o", p=128)
                y.dma_start(out=w3tl[:, slot * 8192:(slot + 1) * 8192],
                            in_=flat[:, :, ec * 128:(ec + 1) * 128]
                            ).then_inc(sW3[slot], 16)
            p.y(w3load)

            def w3mm(t, qh=qh, ec=ec, ei=ei, slot=slot):
                t.wait_ge(sW3[slot], 16 * (ei // 2 + 1))
                t.wait_ge(p.auto["s"], h_mark[("bevlast", qh)])
                if ei >= 2:
                    t.wait_ge(p.auto["v"], h_mark.get(("x4", ei - 2), 0))
                bank = SA2 if ec % 2 == 0 else SB2
                for hc in range(64):
                    MM(t, bank[:, 0:TQH],
                       w3tl[:, slot * 8192 + hc * 128:slot * 8192 + (hc + 1) * 128],
                       hbf[:, hc * TQH:(hc + 1) * TQH],
                       start=(hc == 0), stop=(hc == 63))
                h_mark[("w3mm", ei)] = mm_cnt[0]
            p.t(w3mm)

            def x4ev(v, qh=qh, ec=ec, ei=ei):
                v.wait_ge(sMM, h_mark[("w3mm", ei)])
                if qh > 0:
                    v.wait_ge(sOut[0], 128 * qh)
                    v.wait_ge(sOut[1], 128 * qh)
                bank = SA2 if ec % 2 == 0 else SB2
                cs = slice(ec * TPC + qh * TQH, ec * TPC + (qh + 1) * TQH)
                v.scalar_tensor_tensor(out=x2sb[:, cs], in0=bank[:, 0:TQH],
                                       scalar=wsc[:, 6:7], in1=x2sb[:, cs],
                                       op0=ALU.mult, op1=ALU.add)
                h_mark[("x4", ei)] = p.cnt["v"]
            p.v(x4ev)

            def x4out(y, qh=qh, ec=ec, ei=ei):
                y.wait_ge(p.auto["v"], h_mark[("x4", ei)])
                y.dma_start(out=o_x4[ec * 128:(ec + 1) * 128,
                                     qh * TQH:(qh + 1) * TQH],
                            in_=x2sb[:, ec * TPC + qh * TQH:
                                     ec * TPC + (qh + 1) * TQH]
                            ).then_inc(sOut[ec % 2], 16)
            p.y(x4out)
        phI.close()

    def finish(y):
        y.wait_ge(sOut[0], 16 * 32)
        y.wait_ge(sOut[1], 16 * 32)
    p.y(finish)

    st.close()
    return nc


_NC_CACHE = {}


def _get_nc(name):
    if name not in _NC_CACHE:
        _NC_CACHE[name] = globals()["_build_" + name]()
    return _NC_CACHE[name]


def _run(name, in_maps, cores):
    import time
    t0 = time.time()
    r = run_bass_kernel_spmd(_get_nc(name), in_maps, core_ids=cores).results
    if os.environ.get("KTIME"):
        print(f"[ktime] launch {name}: {time.time() - t0:.1f}s",
              file=sys.stderr, flush=True)
    return r


def _cols(v):
    v = np.asarray(v)
    return np.ascontiguousarray(v.reshape(-1, 128).T)


def _t(a):
    return np.ascontiguousarray(np.asarray(a).T)


def kernel(**inputs):
    x = np.asarray(inputs["L0_attn_x"])
    xf = np.ascontiguousarray(x.reshape(8192, D))
    ident = np.eye(128, dtype=np.float32)
    onesrow = np.ones((1, 128), np.float32)
    oab = np.zeros((128, 65), np.float32)
    oab[0:64, 0] = 1.0
    oab[64:128, 64] = 1.0
    or64 = np.ones((65, 128), np.float32)
    cA = _make_consts(CONSTS_A)
    cC = _make_consts(CONSTS_C)
    cD = _make_consts(CONSTS_D)
    cores = list(range(NCORES))

    bdtc = _cols(inputs["L0_attn_bdt"])
    inA = [dict(xr=np.ascontiguousarray(xf[i * TPC:(i + 1) * TPC]),
                wx=np.asarray(inputs["L0_attn_Wx"]),
                wdt=np.asarray(inputs["L0_attn_Wdt"]),
                bdtc=bdtc, ident=ident, constv=cA) for i in cores]
    rA = _run("A", inA, cores)

    dt_g = [np.concatenate([rA[4 * b + j]["o_dt"] for j in range(4)], axis=1)
            for b in range(2)]
    dtu_g = [np.concatenate([rA[4 * b + j]["o_dtu"] for j in range(4)], axis=1)
             for b in range(2)]
    b_nt = [np.concatenate([rA[4 * b + j]["o_bc"][0:16] for j in range(4)],
                           axis=1) for b in range(2)]
    c_nt = [np.concatenate([rA[4 * b + j]["o_bc"][16:32] for j in range(4)],
                           axis=1) for b in range(2)]

    alog = np.asarray(inputs["L0_attn_A_log"])
    masks = _mk_masks()
    inB = []
    for i in cores:
        b, q = divmod(i, 4)
        rs = slice(512 * q, 512 * (q + 1))
        inB.append(dict(
            dt_in=np.ascontiguousarray(dt_g[b][rs]),
            dtu_in=np.ascontiguousarray(dtu_g[b][rs]),
            b_in=np.ascontiguousarray(b_nt[b]),
            c_in=np.ascontiguousarray(c_nt[b]),
            alogl=_alog_lanes(alog[rs]), masks=masks))
    rB = _run("B", inB, cores)
    y_g = [np.concatenate([rB[4 * b + q]["o_y"] for q in range(4)], axis=0)
           for b in range(2)]

    p1t = _t(inputs["L0_ffn_proj1"])
    p2t = _t(inputs["L0_ffn_proj2"])
    inC1 = []
    for i in cores:
        b, q = divmod(i, 4)
        inC1.append(dict(
            xr=np.ascontiguousarray(xf[i * TPC:(i + 1) * TPC]),
            yin=np.ascontiguousarray(y_g[b][:, TPC * q:TPC * (q + 1)]),
            dpc=_cols(inputs["L0_attn_Dp"]),
            p1t=p1t, b1c=_cols(inputs["L0_ffn_proj1_bias"]),
            p2t=p2t, b2c=_cols(inputs["L0_ffn_proj2_bias"]),
            ident=ident, onesrow=onesrow, constv=cC))
    rC1 = _run("C1", inC1, cores)

    qw = np.asarray(inputs["L1_attn_q_weight"])
    kw = np.asarray(inputs["L1_attn_k_weight"])
    vw = np.asarray(inputs["L1_attn_v_weight"])
    ow = np.asarray(inputs["L1_attn_o_weight"])
    qwt, kwt, vwt = _t(qw), _t(kw), _t(vw)
    inC2 = []
    for i in cores:
        inC2.append(dict(
            x2in=np.ascontiguousarray(rC1[i]["o_x2"]),
            qw=qw, kw=kw, vw=vw, ow=ow, qwt=qwt, kwt=kwt, vwt=vwt,
            qbc=_cols(inputs["L1_attn_q_bias"]),
            kbc=_cols(inputs["L1_attn_k_bias"]),
            vbc=_cols(inputs["L1_attn_v_bias"]),
            ident=ident, onesrow=onesrow, constv=cC))
    rC2 = _run("C2", inC2, cores)

    oqit = _t(rC2[0]["o_oqi"])
    osc = np.asarray(rC2[0]["o_osc"])
    owt = _t(ow)
    w1t = _t(inputs["L1_ffn_w1"])
    w2t = _t(inputs["L1_ffn_w2"])
    w3t = _t(inputs["L1_ffn_w3"])
    inD = []
    for i in cores:
        b, q = divmod(i, 4)
        kf = np.concatenate([rC2[4 * b + j]["o_k"] for j in range(4)], axis=1)
        vt = np.concatenate([rC2[4 * b + j]["o_v"] for j in range(4)], axis=0)
        inD.append(dict(
            x2f=np.ascontiguousarray(rC1[i]["o_x2"]),
            qf=np.ascontiguousarray(rC2[i]["o_q"]),
            kf=np.ascontiguousarray(kf), vt=np.ascontiguousarray(vt),
            oqit=oqit, osc=osc, owt=owt,
            obc=_cols(inputs["L1_attn_o_bias"]),
            w1t=w1t, w2t=w2t, w3t=w3t,
            ident=ident, onesrow=onesrow, oab=oab, or64=or64, constv=cD))
    rD = _run("D", inD, cores)

    out = np.concatenate([np.asarray(rD[i]["o_x4"]).T for i in cores], axis=0)
    return out.reshape(2, T, D).astype(np.float32)


# version 05
# speedup vs baseline: 1.4959x; 1.4959x over previous
import os
import sys
import numpy as np
import ml_dtypes
from contextlib import ExitStack

import concourse.bass as bass
import concourse.mybir as mybir
from concourse.bass_utils import run_bass_kernel_spmd

F32 = mybir.dt.float32
BF16 = mybir.dt.bfloat16
I32 = mybir.dt.int32
AF = mybir.ActivationFunctionType
ALU = mybir.AluOpType
AX = mybir.AxisListType
BF = ml_dtypes.bfloat16

NCORES = 8
D = 2048
TPC = 1024
T = 4096
NST = 16
DTR = 64
NDC = D // 128
NTT = TPC // 128
LN_EPS = 1e-5


def _make_consts(vals):
    a = np.zeros((128, max(1, len(vals))), np.float32)
    for i, v in enumerate(vals):
        a[:, i] = v
    return a


LAD4 = (8.0, 4.0, 2.0, 1.0)
LAD4_C = 8.5
LAD8 = (128.0, 64.0, 32.0, 16.0, 8.0, 4.0, 2.0, 1.0)
LAD8_C = 128.5


def _emit_ladder(v, chunks, powers):
    assert len(chunks) >= 2
    for pw in powers:
        for m, g in chunks:
            v.tensor_scalar(out=g, in0=m, scalar1=pw, scalar2=-pw,
                            op0=ALU.is_ge, op1=ALU.mult)
        for m, g in chunks:
            v.scalar_tensor_tensor(out=m, in0=g, scalar=1.0, in1=m,
                                   op0=ALU.mult, op1=ALU.add)


class _Serial:

    def __init__(self, eng, sem, prog, key):
        self._eng, self._sem, self._prog, self._key = eng, sem, prog, key

    def wait_ge(self, sem, val):
        if val > 0:
            self._eng.wait_ge(sem, val)

    def wait_mark(self, key, mark):
        self._prog.wait_mark(self._eng, key, mark)

    def __getattr__(self, name):
        f = getattr(self._eng, name)
        if not callable(f):
            return f

        def wrapped(*a, **k):
            is_dma = name in ("dma_start", "dma_start_transpose")
            key = self._key
            dkey = key + "d"
            cnt = self._prog.cnt[key]
            if cnt > 0:
                self._eng.wait_ge(self._sem, cnt)
            if dkey in self._prog.auto and self._prog.cnt[dkey] > 0:
                self._eng.wait_ge(self._prog.auto[dkey], self._prog.cnt[dkey])
            inst = f(*a, **k)
            if is_dma and dkey in self._prog.auto:
                inst.then_inc(self._prog.auto[dkey], 16)
                self._prog.cnt[dkey] += 16
            else:
                inst.then_inc(self._sem, 16 if is_dma else 1)
                self._prog.cnt[key] += 16 if is_dma else 1
            return inst

        return wrapped


class Prog:

    AUTOKEYS = ("v", "s", "g", "gd")

    def __init__(self, nc, stack):
        self.nc = nc
        self.block = stack.enter_context(nc.Block())
        self.auto = {k: stack.enter_context(nc.semaphore(f"auto_{k}"))
                     for k in self.AUTOKEYS}
        self.cnt = {k: 0 for k in self.AUTOKEYS}
        self._nsem = 0
        self._stack = stack

    def sem(self, name=None):
        self._nsem += 1
        return self._stack.enter_context(
            self.nc.semaphore(name or f"s{self._nsem}"))

    def v(self, fn):
        self.block.vector(lambda e: fn(_Serial(e, self.auto["v"], self, "v")))

    def s(self, fn):
        self.block.scalar(lambda e: fn(_Serial(e, self.auto["s"], self, "s")))

    def g(self, fn):
        self.block.gpsimd(lambda e: fn(_Serial(e, self.auto["g"], self, "g")))

    def t(self, fn):
        self.block.tensor(fn)

    def y(self, fn):
        self.block.sync(fn)

    def mark(self, key):
        if key == "g":
            return (self.cnt["g"], self.cnt["gd"])
        return self.cnt[key]

    def wait_mark(self, eng, key, mark):
        if key == "g":
            a, b = mark if isinstance(mark, tuple) else (mark, 0)
            if a > 0:
                eng.wait_ge(self.auto["g"], a)
            if b > 0:
                eng.wait_ge(self.auto["gd"], b)
        elif mark > 0:
            eng.wait_ge(self.auto[key], mark)


CONSTS_A = [LN_EPS, -0.5, -1.0, 1.0]


def _build_A():
    nc = bass.Bass("TRN2", target_bir_lowering=False)
    xr = nc.dram_tensor("xr", [TPC, D], F32, kind="ExternalInput").ap()
    wx = nc.dram_tensor("wx", [D, 96], F32, kind="ExternalInput").ap()
    wdt = nc.dram_tensor("wdt", [DTR, D], F32, kind="ExternalInput").ap()
    bdtc = nc.dram_tensor("bdtc", [128, NDC], F32, kind="ExternalInput").ap()
    ident = nc.dram_tensor("ident", [128, 128], F32, kind="ExternalInput").ap()
    constv = nc.dram_tensor("constv", [128, len(CONSTS_A)], F32,
                            kind="ExternalInput").ap()
    o_dt = nc.dram_tensor("o_dt", [D, TPC], F32, kind="ExternalOutput").ap()
    o_dtu = nc.dram_tensor("o_dtu", [D, TPC], F32, kind="ExternalOutput").ap()
    o_bc = nc.dram_tensor("o_bc", [32, TPC], F32, kind="ExternalOutput").ap()

    st = ExitStack()
    E = st.enter_context
    xtm = E(nc.sbuf_tensor("xtm", [128, 2 * D], F32))
    sqt = E(nc.sbuf_tensor("sqt", [128, 2 * D], F32))
    x0t = E(nc.sbuf_tensor("x0t", [128, 2 * D], F32))
    x0n = E(nc.sbuf_tensor("x0n", [128, NDC * TPC], F32))
    x0nb = E(nc.sbuf_tensor("x0nb", [128, NDC * TPC], BF16))
    statc = E(nc.sbuf_tensor("statc", [128, NTT * 8], F32))
    wx_sb = E(nc.sbuf_tensor("wx_sb", [128, NDC * 96], BF16))
    wdt_sb = E(nc.sbuf_tensor("wdt_sb", [DTR, D], BF16))
    bdt_sb = E(nc.sbuf_tensor("bdt_sb", [128, NDC], F32))
    id_sb = E(nc.sbuf_tensor("id_sb", [128, 128], F32))
    cst_sb = E(nc.sbuf_tensor("cst_sb", [128, len(CONSTS_A)], F32))
    delta_bf = E(nc.sbuf_tensor("delta_bf", [DTR, TPC], BF16))
    bc_sb = E(nc.sbuf_tensor("bc_sb", [32, TPC], F32))
    absz = E(nc.sbuf_tensor("absz", [128, TPC], F32))
    l1p = E(nc.sbuf_tensor("l1p", [128, TPC], F32))
    relu = E(nc.sbuf_tensor("relu", [128, TPC], F32))
    dt_sb = E(nc.sbuf_tensor("dt_sb", [128, 2 * TPC], F32))
    dtu_sb = E(nc.sbuf_tensor("dtu_sb", [128, 2 * TPC], F32))

    P = [E(nc.psum_tensor(f"P{i}", [128, 512], F32)) for i in range(8)]

    for i, v in enumerate(CONSTS_A):
        nc.const_aps.register(F32, v, cst_sb[:, i:i + 1])

    p = Prog(nc, st)
    sLc = p.sem("sLc")
    sLx = [p.sem("sLx0"), p.sem("sLx1")]
    sLw = p.sem("sLw")
    sTr = p.sem("sTr")
    sXz = p.sem("sXz")
    sDtM = p.sem("sDtM")
    sOutD = [p.sem("sOutD0"), p.sem("sOutD1")]
    sOutB = p.sem("sOutB")

    def consts(y):
        y.dma_start(out=bdt_sb[:, :], in_=bdtc).then_inc(sLc, 16)
        y.dma_start(out=id_sb[:, :], in_=ident).then_inc(sLc, 16)
        y.dma_start(out=cst_sb[:, :], in_=constv).then_inc(sLc, 16)
    p.y(consts)

    def wloads(g):
        g.dma_start(out=wx_sb[:, :], in_=wx.rearrange("(c p) n -> p c n", p=128))
        g.dma_start(out=wdt_sb[:, :], in_=wdt)
    p.g(wloads)
    m_wload = p.mark("g")

    aff_mark = [0] * NTT
    red_mark = [0] * NTT
    evac_mark = [0] * NTT

    for tt in range(NTT):
        sl = slice((tt % 2) * D, (tt % 2) * D + D)

        def load(y, tt=tt, sl=sl):
            if tt >= 2:
                y.wait_ge(p.auto["v"], aff_mark[tt - 2])
            y.dma_start(out=xtm[:, sl],
                        in_=xr[tt * 128:(tt + 1) * 128, :]).then_inc(sLx[tt % 2], 16)
        p.y(load)

        def square(s, tt=tt, sl=sl):
            s.wait_ge(sLc, 48)
            s.wait_ge(sLx[tt % 2], 16 * (tt // 2 + 1))
            if tt >= 2:
                s.wait_mark("v", red_mark[tt - 2])
            s.activation(out=sqt[:, sl], in_=xtm[:, sl], func=AF.Square)
        p.s(square)
        m_sq = p.mark("s")

        def stats(v, tt=tt, sl=sl):
            c = lambda k: statc[:, tt * 8 + k: tt * 8 + k + 1]
            v.wait_mark("s", m_sq)
            v.reduce_sum(c(0), xtm[:, sl], axis=AX.X)
            v.reduce_sum(c(1), sqt[:, sl], axis=AX.X)
            red_mark[tt] = p.cnt["v"]
            v.tensor_scalar_mul(c(2), c(0), 1.0 / D)
            v.tensor_scalar_mul(c(3), c(1), 1.0 / D)
            v.tensor_mul(c(7), c(2), c(2))
            v.tensor_sub(c(4), c(3), c(7))
        p.v(stats)
        red_mark[tt] = red_mark[tt]
        m_var = p.mark("v")

        def rstd(s, tt=tt):
            c = lambda k: statc[:, tt * 8 + k: tt * 8 + k + 1]
            s.wait_mark("v", m_var)
            s.activation(out=c(5), in_=c(4), func=AF.Ln, bias=LN_EPS)
            s.activation(out=c(6), in_=c(5), func=AF.Exp, scale=-0.5)
        p.s(rstd)
        m_rstd = p.mark("s")

        def affine(v, tt=tt, sl=sl):
            c = lambda k: statc[:, tt * 8 + k: tt * 8 + k + 1]
            v.wait_mark("s", m_rstd)
            if tt >= 2:
                v.wait_ge(sTr, tt - 1)
            v.tensor_scalar(out=x0t[:, sl], in0=xtm[:, sl], scalar1=c(2),
                            scalar2=c(6), op0=ALU.subtract, op1=ALU.mult)
            aff_mark[tt] = p.cnt["v"]
        p.v(affine)

        def transposes(t, tt=tt):
            t.wait_ge(p.auto["v"], aff_mark[tt])
            t.wait_ge(sLc, 48)
            if tt >= 2:
                t.wait_ge(p.auto["s"], evac_mark[tt - 2])
            last = None
            for k in range(NDC):
                bank = 4 * (tt % 2) + k // 4
                last = t.transpose(
                    P[bank][:, (k % 4) * 128:(k % 4) * 128 + 128],
                    x0t[:, (tt % 2) * D + k * 128:(tt % 2) * D + (k + 1) * 128],
                    id_sb[:, :])
            last.then_inc(sTr, 1)
        p.t(transposes)

        def evacs(s, tt=tt):
            s.wait_ge(sTr, tt + 1)
            for k in range(NDC):
                bank = 4 * (tt % 2) + k // 4
                s.activation(
                    out=x0n[:, k * TPC + tt * 128: k * TPC + (tt + 1) * 128],
                    in_=P[bank][:, (k % 4) * 128:(k % 4) * 128 + 128],
                    func=AF.Copy)
            evac_mark[tt] = p.cnt["s"]
        p.s(evacs)

    def casts(s):
        for dc in range(NDC):
            s.activation(out=x0nb[:, dc * TPC:(dc + 1) * TPC],
                         in_=x0n[:, dc * TPC:(dc + 1) * TPC], func=AF.Copy)
    p.s(casts)
    m_cast = p.mark("s")

    def xz_mm(t):
        t.wait_ge(p.auto["s"], m_cast)
        p.wait_mark(t, "g", m_wload)
        for dc in range(NDC):
            for h in range(2):
                mm = t.matmul(
                    P[h][0:96, :], wx_sb[:, dc * 96:(dc + 1) * 96],
                    x0nb[:, dc * TPC + h * 512: dc * TPC + (h + 1) * 512],
                    start=(dc == 0), stop=(dc == NDC - 1))
        mm.then_inc(sXz, 1)
    p.t(xz_mm)

    def delta_evac(s):
        s.wait_ge(sXz, 1)
        for h in range(2):
            s.activation(out=delta_bf[:, h * 512:(h + 1) * 512],
                         in_=P[h][0:DTR, :], func=AF.Copy)
        s.activation(out=bc_sb[:, 0:512], in_=P[0][64:96, :], func=AF.Copy)
        s.activation(out=bc_sb[:, 512:1024], in_=P[1][64:96, :], func=AF.Copy)
    p.s(delta_evac)
    m_dbf = p.mark("s")

    abs_mark = [[0, 0] for _ in range(NDC)]
    relu_mark = [[0, 0] for _ in range(NDC)]
    dtl_mark = [[0, 0] for _ in range(NDC)]
    dtu_mark = [0] * NDC

    for dc in range(NDC):
        def dt_mm(t, dc=dc):
            if dc == 0:
                t.wait_ge(p.auto["s"], m_dbf)
            for h in range(2):
                if dc > 0:
                    t.wait_ge(p.auto["s"], abs_mark[dc - 1][h])
                    t.wait_ge(p.auto["v"], relu_mark[dc - 1][h])
                t.matmul(P[2 + h][:, :], wdt_sb[:, dc * 128:(dc + 1) * 128],
                         delta_bf[:, h * 512:(h + 1) * 512],
                         start=True, stop=True).then_inc(sDtM, 1)
        p.t(dt_mm)

        def softplus(s, dc=dc):
            for h in range(2):
                s.wait_ge(sDtM, 2 * dc + h + 1)
                if dc > 0:
                    s.wait_mark("v", dtl_mark[dc - 1][h])
                hs = slice(h * 512, (h + 1) * 512)
                s.activation(out=absz[:, hs], in_=P[2 + h][:, :], func=AF.Abs,
                             bias=bdt_sb[:, dc:dc + 1])
                abs_mark[dc][h] = p.cnt["s"]
                s.activation(out=absz[:, hs], in_=absz[:, hs], func=AF.Exp,
                             scale=-1.0)
                s.activation(out=l1p[:, hs], in_=absz[:, hs], func=AF.Ln,
                             bias=1.0)
        p.s(softplus)
        m_sp = [abs_mark[dc][0] + 2, abs_mark[dc][1] + 2]

        def dt_ep(v, dc=dc, m_sp=m_sp):
            dsl0 = (dc % 2) * TPC
            for h in range(2):
                hs = slice(h * 512, (h + 1) * 512)
                v.wait_ge(sDtM, 2 * dc + h + 1)
                v.tensor_scalar(out=relu[:, hs], in0=P[2 + h][:, :],
                                scalar1=bdt_sb[:, dc:dc + 1], scalar2=0.0,
                                op0=ALU.add, op1=ALU.max)
                relu_mark[dc][h] = p.cnt["v"]
                v.wait_mark("s", m_sp[h])
                if dc >= 2 and h == 0:
                    v.wait_ge(sOutD[dc % 2], 32 * (dc // 2))
                v.tensor_add(dt_sb[:, dsl0 + h * 512: dsl0 + (h + 1) * 512],
                             relu[:, hs], l1p[:, hs])
                dtl_mark[dc][h] = p.cnt["v"]
            v.tensor_mul(dtu_sb[:, dsl0:dsl0 + TPC], dt_sb[:, dsl0:dsl0 + TPC],
                         x0n[:, dc * TPC:(dc + 1) * TPC])
            dtu_mark[dc] = p.cnt["v"]
        p.v(dt_ep)

        def dt_out(y, dc=dc):
            y.wait_ge(p.auto["v"], dtu_mark[dc])
            dsl = slice((dc % 2) * TPC, (dc % 2) * TPC + TPC)
            y.dma_start(out=o_dt[dc * 128:(dc + 1) * 128, :],
                        in_=dt_sb[:, dsl]).then_inc(sOutD[dc % 2], 16)
            y.dma_start(out=o_dtu[dc * 128:(dc + 1) * 128, :],
                        in_=dtu_sb[:, dsl]).then_inc(sOutD[dc % 2], 16)
        p.y(dt_out)

    def finish(y):
        y.wait_ge(p.auto["s"], m_dbf)
        y.dma_start(out=o_bc, in_=bc_sb[:, :]).then_inc(sOutB, 16)
        y.wait_ge(sOutD[0], 32 * (NDC // 2))
        y.wait_ge(sOutD[1], 32 * (NDC // 2))
        y.wait_ge(sOutB, 16)
    p.y(finish)

    st.close()
    return nc


CONSTS_W = [1.0]


def _build_W():
    nc = bass.Bass("TRN2", target_bir_lowering=False)
    wr4 = nc.dram_tensor("wr4", [1024, D], F32, kind="ExternalInput").ap()
    wt4 = nc.dram_tensor("wt4", [D, 1024], F32, kind="ExternalInput").ap()
    w1s = nc.dram_tensor("w1s", [D, 1024], F32, kind="ExternalInput").ap()
    w2s = nc.dram_tensor("w2s", [D, 1024], F32, kind="ExternalInput").ap()
    w3s = nc.dram_tensor("w3s", [1024, D], F32, kind="ExternalInput").ap()
    constv = nc.dram_tensor("constv", [128, 1], F32, kind="ExternalInput").ap()
    o_q4 = nc.dram_tensor("o_q4", [1024, D], BF16, kind="ExternalOutput").ap()
    o_s4 = nc.dram_tensor("o_s4", [128, 8], F32, kind="ExternalOutput").ap()
    o_wbf = nc.dram_tensor("o_wbf", [D, 1024], BF16, kind="ExternalOutput").ap()
    o_msum = nc.dram_tensor("o_msum", [1, 4], F32, kind="ExternalOutput").ap()

    st = ExitStack()
    E = st.enter_context
    wrow = E(nc.sbuf_tensor("wrow", [128, 2 * D], F32))
    wqr = E(nc.sbuf_tensor("wqr", [128, D], F32))
    qir = E(nc.sbuf_tensor("qir", [128, 2 * D], BF16))
    wscl = E(nc.sbuf_tensor("wscl", [128, 8], F32))
    rbuf = E(nc.sbuf_tensor("rbuf", [128, 1], F32))
    cbuf = E(nc.sbuf_tensor("cbuf", [128, 2 * 1024], BF16))
    mld = E(nc.sbuf_tensor("mld", [128, 2 * 2048], F32))
    mab = E(nc.sbuf_tensor("mab", [128, 2048], F32))
    macc = E(nc.sbuf_tensor("macc", [128, 4], F32))
    mrow = E(nc.sbuf_tensor("mrow", [1, 4], F32))
    cst_sb = E(nc.sbuf_tensor("cst_sb", [128, 1], F32))
    P0 = E(nc.psum_tensor("P0", [128, 512], F32))

    for i, v in enumerate(CONSTS_W):
        nc.const_aps.register(F32, v, cst_sb[:, i:i + 1])

    p = Prog(nc, st)
    sLc = p.sem("sLc")
    sWr = [p.sem("sWr0"), p.sem("sWr1")]
    sQo = [p.sem("sQo0"), p.sem("sQo1")]
    sCo = [p.sem("sCo0"), p.sem("sCo1")]
    sMl = [p.sem("sMl0"), p.sem("sMl1")]
    sMM = p.sem("sMM")
    sOutS = p.sem("sOutS")
    sOutM = p.sem("sOutM")

    def consts(y):
        y.dma_start(out=cst_sb[:, :], in_=constv).then_inc(sLc, 16)
    p.y(consts)

    def init(v):
        v.wait_ge(sLc, 16)
        v.memset(macc[:, :], 0.0)
    p.v(init)

    q_mark = {}
    for rb in range(8):
        slot = rb % 2

        def qload(y, rb=rb, slot=slot):
            if rb >= 2:
                y.wait_ge(sQo[slot], 16 * (rb // 2))
            y.dma_start(out=wrow[:, slot * D:(slot + 1) * D],
                        in_=wr4[rb * 128:(rb + 1) * 128, :]
                        ).then_inc(sWr[slot], 16)
        p.y(qload)

        def quant(v, rb=rb, slot=slot):
            ss = slice(slot * D, (slot + 1) * D)
            v.wait_ge(sWr[slot], 16 * (rb // 2 + 1))
            if rb >= 2:
                v.wait_ge(sQo[slot], 16 * (rb // 2))
            c = wscl[:, rb:rb + 1]
            v.reduce_max(c, wrow[:, ss], axis=AX.X, apply_absolute_value=True)
            v.tensor_scalar_mul(c, c, 1.0 / 7.0)
            v.reciprocal(rbuf[:, 0:1], c)
            v.tensor_scalar_mul(wqr[:, :], wrow[:, ss], rbuf[:, 0:1])
            v.tensor_scalar_add(wrow[:, ss], wqr[:, :], LAD4_C)
            h0 = slot * D
            _emit_ladder(v, [(wrow[:, h0:h0 + 1024], qir[:, h0:h0 + 1024]),
                             (wrow[:, h0 + 1024:h0 + 2048],
                              qir[:, h0 + 1024:h0 + 2048])], LAD4)
            v.scalar_tensor_tensor(out=wqr[:, :], in0=wrow[:, ss], scalar=-1.0,
                                   in1=wqr[:, :], op0=ALU.mult, op1=ALU.add)
            v.tensor_scalar_add(qir[:, ss], wqr[:, :], 0.5)
            q_mark[rb] = p.cnt["v"]
        p.v(quant)

        def qout(y, rb=rb, slot=slot):
            y.wait_ge(p.auto["v"], q_mark[rb])
            y.dma_start(out=o_q4[rb * 128:(rb + 1) * 128, :],
                        in_=qir[:, slot * D:(slot + 1) * D]
                        ).then_inc(sQo[slot], 16)
        p.y(qout)

    def sclout(y):
        y.wait_ge(p.auto["v"], q_mark[7])
        y.dma_start(out=o_s4, in_=wscl[:, :]).then_inc(sOutS, 16)
    p.y(sclout)

    c_mark = {}
    for ct in range(16):
        slot = ct % 2

        def cload(g, ct=ct, slot=slot):
            if ct >= 2:
                g.wait_ge(sCo[slot], 16 * (ct // 2))
            g.dma_start(out=cbuf[:, slot * 1024:(slot + 1) * 1024],
                        in_=wt4[ct * 128:(ct + 1) * 128, :])
            c_mark[ct] = p.cnt["gd"]
        p.g(cload)

        def cout(y, ct=ct, slot=slot):
            y.wait_ge(p.auto["gd"], c_mark[ct])
            y.dma_start(out=o_wbf[ct * 128:(ct + 1) * 128, :],
                        in_=cbuf[:, slot * 1024:(slot + 1) * 1024]
                        ).then_inc(sCo[slot], 16)
        p.y(cout)

    mtiles = []
    for w, (src, R, C) in enumerate(((w1s, D, 1024), (w2s, D, 1024),
                                     (w3s, 1024, D))):
        for a in range(R // 128):
            mtiles.append((w, src, a, C))
    mk = {}
    ml_cnt = [0, 0]
    for mi, (w, src, a, C) in enumerate(mtiles):
        slot = mi % 2

        def mload(y, src=src, a=a, C=C, mi=mi, slot=slot):
            if mi >= 2:
                y.wait_ge(p.auto["s"], mk[("ab", mi - 2)])
            y.dma_start(out=mld[:, slot * 2048:slot * 2048 + C],
                        in_=src[a * 128:(a + 1) * 128, :]
                        ).then_inc(sMl[slot], 16)
            ml_cnt[slot] += 16
        p.y(mload)
        need = ml_cnt[slot]

        def mabs(s, mi=mi, slot=slot, C=C, need=need):
            s.wait_ge(sMl[slot], need)
            if mi >= 1:
                s.wait_mark("v", mk.get(("acc", mi - 1), 0))
            s.activation(out=mab[:, 0:C],
                         in_=mld[:, slot * 2048:slot * 2048 + C], func=AF.Abs)
            mk[("ab", mi)] = p.cnt["s"]
        p.s(mabs)

        def macc_(v, w=w, mi=mi, C=C):
            v.wait_mark("s", mk[("ab", mi)])
            v.reduce_sum(rbuf[:, 0:1], mab[:, 0:C], axis=AX.X)
            v.tensor_add(macc[:, w:w + 1], macc[:, w:w + 1], rbuf[:, 0:1])
            mk[("acc", mi)] = p.cnt["v"]
        p.v(macc_)
    m_macc = p.mark("v")

    def msum_mm(t):
        t.wait_ge(p.auto["v"], m_macc)
        t.wait_ge(sLc, 16)
        t.matmul(P0[0:1, 0:4], cst_sb[:, 0:1], macc[:, 0:4], start=True,
                 stop=True).then_inc(sMM, 1)
    p.t(msum_mm)

    def mevac(v):
        v.wait_ge(sMM, 1)
        v.tensor_copy(out=mrow[0:1, 0:4], in_=P0[0:1, 0:4])
    p.v(mevac)
    m_mrow = p.mark("v")

    def mout(y):
        y.wait_ge(p.auto["v"], m_mrow)
        y.dma_start(out=o_msum, in_=mrow[0:1, :]).then_inc(sOutM, 16)
    p.y(mout)

    def finish(y):
        y.wait_ge(sQo[0], 16 * 4)
        y.wait_ge(sQo[1], 16 * 4)
        y.wait_ge(sCo[0], 16 * 8)
        y.wait_ge(sCo[1], 16 * 8)
        y.wait_ge(sOutS, 16)
        y.wait_ge(sOutM, 16)
    p.y(finish)

    st.close()
    return nc


def _mk_masks():
    m = np.zeros((128, 16, 128), np.float32)
    for lb in range(16):
        for j in range(8):
            for n in range(16):
                m[j * 16 + n, lb, 8 * lb + j] = 1.0
    return m.reshape(128, 16 * 128).astype(BF)


def _alog_lanes(alog_slice):
    a = alog_slice.reshape(4, 16, 8, 16)
    return np.ascontiguousarray(a.transpose(2, 3, 0, 1).reshape(128, 64))


def _build_B():
    TC = 1024
    NTC = T // TC
    nc = bass.Bass("TRN2", target_bir_lowering=False)
    dt_in = nc.dram_tensor("dt_in", [512, T], F32, kind="ExternalInput").ap()
    dtu_in = nc.dram_tensor("dtu_in", [512, T], F32, kind="ExternalInput").ap()
    b_in = nc.dram_tensor("b_in", [NST, T], F32, kind="ExternalInput").ap()
    c_in = nc.dram_tensor("c_in", [NST, T], F32, kind="ExternalInput").ap()
    alogl = nc.dram_tensor("alogl", [128, 64], F32, kind="ExternalInput").ap()
    masks = nc.dram_tensor("masks", [128, 16 * 128], BF16,
                           kind="ExternalInput").ap()
    o_y = nc.dram_tensor("o_y", [512, T], F32, kind="ExternalOutput").ap()

    st = ExitStack()
    E = st.enter_context
    dtr = E(nc.sbuf_tensor("dtr", [128, 2 * 4 * TC], F32))
    dur = E(nc.sbuf_tensor("dur", [128, 2 * 4 * TC], F32))
    a_sb = E(nc.sbuf_tensor("a_sb", [128, 4 * TC], F32))
    b_sb = E(nc.sbuf_tensor("b_sb", [128, 4 * TC], F32))
    h_sb = E(nc.sbuf_tensor("h_sb", [128, 2 * TC], F32))
    tmp_sb = E(nc.sbuf_tensor("tmp_sb", [128, 4 * TC], BF16))
    b16 = E(nc.sbuf_tensor("b16", [128, TC], F32))
    c16 = E(nc.sbuf_tensor("c16", [128, TC], F32))
    nal = E(nc.sbuf_tensor("nal", [128, 64], F32))
    msk = E(nc.sbuf_tensor("msk", [128, 16 * 128], BF16))
    stt = E(nc.sbuf_tensor("stt", [128, 64], F32))
    y_sb = E(nc.sbuf_tensor("y_sb", [128, 2 * TC], F32))

    Py = [E(nc.psum_tensor(f"Py{i}", [128, 512], F32)) for i in range(4)]

    p = Prog(nc, st)
    sLc = p.sem("sLc")
    sRepD = [p.sem("sRepD0"), p.sem("sRepD1")]
    sRepU = [p.sem("sRepU0"), p.sem("sRepU1")]
    sB16 = p.sem("sB16")
    sYmm = p.sem("sYmm")
    sOutY = [p.sem("sOutY0"), p.sem("sOutY1")]

    def consts(y):
        y.dma_start(out=nal[:, :], in_=alogl).then_inc(sLc, 16)
        y.dma_start(out=msk[:, :], in_=masks).then_inc(sLc, 16)
    p.y(consts)

    def negate(s):
        s.wait_ge(sLc, 32)
        s.activation(out=nal[:, :], in_=nal[:, :], func=AF.Exp)
        s.mul(nal[:, :], nal[:, :], -1.0)
    p.s(negate)
    m_nal = p.mark("s")

    a_mark = {}
    b_mark = {}
    tmp_mark = {}
    scan_mark = {}
    mmcnt_after = {}
    ev_mark = {}
    mm_cnt = 0
    b16_last_v = 0
    b16_last_g = 0

    for tc in range(NTC):
        tsl = slice(tc * TC, (tc + 1) * TC)

        def b16_load(y, tc=tc, tsl=tsl):
            if tc > 0:
                y.wait_ge(p.auto["g"], b16_last_g)
                y.wait_ge(p.auto["v"], b16_last_v)
            y.dma_start(out=b16[:, :],
                        in_=b_in[None, :, tsl].broadcast_to([8, NST, TC])
                        ).then_inc(sB16, 16)
            y.dma_start(out=c16[:, :],
                        in_=c_in[None, :, tsl].broadcast_to([8, NST, TC])
                        ).then_inc(sB16, 16)
        p.y(b16_load)

        for cc in range(4):
            for gr in range(4):
                gidx = (tc * 4 + cc) * 4 + gr
                slot = gidx % 2

                def rep_load(y, tc=tc, cc=cc, gr=gr, gidx=gidx, slot=slot):
                    if gidx >= 2:
                        pg = gidx - 2
                        y.wait_ge(p.auto["s"], a_mark[(pg, 3)])
                        y.wait_ge(p.auto["g"], b_mark[(pg, 3)])
                    r0 = cc * 128 + gr * 32
                    for src, dst, sem in ((dt_in, dtr, sRepD[slot]),
                                          (dtu_in, dur, sRepU[slot])):
                        for k in range(4):
                            v = src[r0 + 8 * k:r0 + 8 * k + 8,
                                    tc * TC:(tc + 1) * TC]
                            v = v[:, None, :].broadcast_to([8, NST, TC])
                            y.dma_start(
                                out=dst[:, (slot * 4 + k) * TC:
                                        (slot * 4 + k + 1) * TC],
                                in_=v).then_inc(sem, 16)
                p.y(rep_load)

                def a_ops(s, tc=tc, cc=cc, gr=gr, gidx=gidx, slot=slot):
                    s.wait_mark("s", m_nal)
                    s.wait_ge(sRepD[slot], 64 * (gidx // 2 + 1))
                    for k in range(4):
                        li = gidx * 4 + k
                        if li >= 4:
                            s.wait_mark("v", scan_mark.get(li - 4, 0))
                        col = cc * 16 + gr * 4 + k
                        s.activation(
                            out=a_sb[:, (li % 4) * TC:(li % 4 + 1) * TC],
                            in_=dtr[:, (slot * 4 + k) * TC:(slot * 4 + k + 1) * TC],
                            func=AF.Exp, scale=nal[:, col:col + 1])
                        a_mark[(gidx, k)] = p.cnt["s"]
                p.s(a_ops)

                def b_ops(g, tc=tc, gidx=gidx, slot=slot):
                    g.wait_ge(sRepU[slot], 64 * (gidx // 2 + 1))
                    g.wait_ge(sB16, 32 * (tc + 1))
                    for k in range(4):
                        li = gidx * 4 + k
                        if li >= 4:
                            g.wait_mark("v", scan_mark.get(li - 4, 0))
                        g.tensor_tensor(
                            out=b_sb[:, (li % 4) * TC:(li % 4 + 1) * TC],
                            in0=dur[:, (slot * 4 + k) * TC:(slot * 4 + k + 1) * TC],
                            in1=b16[:, :], op=ALU.mult)
                        b_mark[(gidx, k)] = p.cnt["g"]
                p.g(b_ops)

                def v_ops(v, tc=tc, cc=cc, gr=gr, gidx=gidx):
                    nonlocal b16_last_v
                    for k in range(4):
                        li = gidx * 4 + k
                        col = cc * 16 + gr * 4 + k
                        v.wait_mark("s", a_mark[(gidx, k)])
                        v.wait_mark("g", b_mark[(gidx, k)])
                        asl = a_sb[:, (li % 4) * TC:(li % 4 + 1) * TC]
                        bsl = b_sb[:, (li % 4) * TC:(li % 4 + 1) * TC]
                        hsl = h_sb[:, (li % 2) * TC:(li % 2 + 1) * TC]
                        init = 0.0 if tc == 0 else stt[:, col:col + 1]
                        v.tensor_tensor_scan(out=hsl, data0=asl, data1=bsl,
                                             initial=init, op0=ALU.mult,
                                             op1=ALU.add)
                        scan_mark[li] = p.cnt["v"]
                        v.tensor_copy(out=stt[:, col:col + 1],
                                      in_=hsl[:, TC - 1:TC])
                        if li >= 4:
                            v.wait_ge(sYmm, mmcnt_after.get(li - 4, 0))
                        v.tensor_tensor(
                            out=tmp_sb[:, (li % 4) * TC:(li % 4 + 1) * TC],
                            in0=hsl, in1=c16[:, :], op=ALU.mult)
                        tmp_mark[li] = p.cnt["v"]
                        b16_last_v = p.cnt["v"]
                p.v(v_ops)

                def t_ops(t, tc=tc, cc=cc, gr=gr, gidx=gidx):
                    nonlocal mm_cnt
                    pb = 2 * (cc % 2)
                    for k in range(4):
                        li = gidx * 4 + k
                        lb = gr * 4 + k
                        t.wait_ge(p.auto["v"], tmp_mark[li])
                        if lb == 0:
                            t.wait_ge(p.auto["s"], ev_mark.get((tc - 1, cc), 0)
                                      if tc > 0 else 0)
                        for h in range(2):
                            t.matmul(
                                Py[pb + h][:, :],
                                msk[:, lb * 128:(lb + 1) * 128],
                                tmp_sb[:, (li % 4) * TC + h * 512:
                                       (li % 4) * TC + (h + 1) * 512],
                                start=(lb == 0), stop=(lb == 15)
                            ).then_inc(sYmm, 1)
                            mm_cnt += 1
                        mmcnt_after[li] = mm_cnt
                p.t(t_ops)

            def y_evac(s, tc=tc, cc=cc):
                pb = 2 * (cc % 2)
                s.wait_ge(sYmm, mm_cnt)
                s.wait_ge(sOutY[cc % 2], 16 * (tc * 2 + cc // 2))
                for h in range(2):
                    s.activation(out=y_sb[:, (cc % 2) * TC + h * 512:
                                          (cc % 2) * TC + (h + 1) * 512],
                                 in_=Py[pb + h][:, :], func=AF.Copy)
                ev_mark[(tc, cc)] = p.cnt["s"]
            p.s(y_evac)

            def y_out(y, tc=tc, cc=cc):
                y.wait_ge(p.auto["s"], ev_mark[(tc, cc)])
                y.dma_start(
                    out=o_y[cc * 128:(cc + 1) * 128, tc * TC:(tc + 1) * TC],
                    in_=y_sb[:, (cc % 2) * TC:(cc % 2 + 1) * TC]
                ).then_inc(sOutY[cc % 2], 16)
            p.y(y_out)

        def b16_done(g, tc=tc):
            nonlocal b16_last_g
            b16_last_g = p.cnt["g"]
        b16_done(None)

    def finish(y):
        y.wait_ge(sOutY[0], 16 * 8)
        y.wait_ge(sOutY[1], 16 * 8)
    p.y(finish)

    st.close()
    return nc




def _ln_fm(p, P, MM, mm_cnt, sMM, cst_sb, or_sb, rows, lnscr, src_sb, dst_sb,
           row0, extra_wait_mark=None, first_sq_waits=()):
    m_src = p.mark("v")
    R = lambda k: rows[0:1, (row0 + k) * TPC:(row0 + k) * TPC + TPC]

    def s1mm(t):
        t.wait_ge(p.auto["v"], m_src)
        if extra_wait_mark is not None:
            t.wait_ge(p.auto["s"], extra_wait_mark)
        for dc in range(NDC):
            for h in range(2):
                MM(t, P[h][0:1, :], cst_sb[:, 3:4],
                   src_sb[:, dc * TPC + h * 512:dc * TPC + (h + 1) * 512],
                   start=(dc == 0), stop=(dc == NDC - 1))
    p.t(s1mm)

    sqm = {}
    mm_after = {}
    for dc in range(NDC):
        def sq(s, dc=dc):
            if dc == 0:
                for sem_, val_ in first_sq_waits:
                    s.wait_ge(sem_, val_)
                s.wait_ge(p.auto["v"], m_src)
            if dc >= 1:
                s.wait_ge(sMM, mm_after[dc - 1])
            s.activation(out=lnscr[:, 0:TPC],
                         in_=src_sb[:, dc * TPC:(dc + 1) * TPC], func=AF.Square)
            sqm[dc] = p.cnt["s"]
        p.s(sq)

        def s2mm(t, dc=dc):
            t.wait_ge(p.auto["s"], sqm[dc])
            for h in range(2):
                MM(t, P[2 + h][0:1, :], cst_sb[:, 3:4],
                   lnscr[:, h * 512:(h + 1) * 512],
                   start=(dc == 0), stop=(dc == NDC - 1))
            mm_after[dc] = mm_cnt[0]
        p.t(s2mm)
    mm_s2 = mm_cnt[0]

    def rowstats(v):
        v.wait_ge(sMM, mm_s2)
        for h in range(2):
            hs = slice(h * 512, (h + 1) * 512)
            v.tensor_scalar_mul(R(0)[:, hs], P[h][0:1, :], 1.0 / D)
            v.tensor_scalar_mul(R(1)[:, hs], P[2 + h][0:1, :], 1.0 / D)
        v.tensor_mul(R(2), R(0), R(0))
        v.tensor_sub(R(1), R(1), R(2))
    p.v(rowstats)
    m_vr = p.mark("v")

    def rowrstd(s):
        s.wait_mark("v", m_vr)
        s.activation(out=R(1), in_=R(1), func=AF.Ln, bias=LN_EPS)
        s.activation(out=R(1), in_=R(1), func=AF.Exp, scale=-0.5)
    p.s(rowrstd)
    m_rr = p.mark("s")

    def rowmr(v):
        v.wait_mark("s", m_rr)
        v.tensor_mul(R(2), R(0), R(1))
    p.v(rowmr)
    m_mr = p.mark("v")

    def bcast(t):
        t.wait_ge(p.auto["v"], m_mr)
        for h in range(2):
            hs = slice(h * 512, (h + 1) * 512)
            MM(t, P[4 + h][:, :], or_sb[:, :], R(1)[:, hs])
            MM(t, P[6 + h][:, :], or_sb[:, :], R(2)[:, hs])
    p.t(bcast)
    mm_bc = mm_cnt[0]

    def norm(v):
        v.wait_ge(sMM, mm_bc)
        for dc in range(NDC):
            for h in range(2):
                cs = slice(dc * TPC + h * 512, dc * TPC + (h + 1) * 512)
                scr = lnscr[:, h * 512:(h + 1) * 512]
                v.tensor_mul(scr, src_sb[:, cs], P[4 + h][:, :])
                v.tensor_sub(dst_sb[:, cs], scr, P[6 + h][:, :])
    p.v(norm)
    return p.mark("v")


LN2C = 0.6931471805599453
CONSTS_C = [LN_EPS, -0.5, LN2C, 1.0]


def _build_C1():
    nc = bass.Bass("TRN2", target_bir_lowering=False)
    inp = lambda n, sh, dt=F32: nc.dram_tensor(n, sh, dt, kind="ExternalInput").ap()
    xr = inp("xr", [TPC, D])
    yin = inp("yin", [D, TPC])
    dpc = inp("dpc", [128, NDC])
    p1t = inp("p1t", [D, 4 * D])
    b1c = inp("b1c", [128, 64])
    p2t = inp("p2t", [4 * D, D])
    b2c = inp("b2c", [128, NDC])
    ident = inp("ident", [128, 128])
    onesrow = inp("onesrow", [1, 128])
    constv = inp("constv", [128, len(CONSTS_C)])
    msums = inp("msums", [8, 4])
    w1ts = inp("w1ts", [D, TPC])
    w2ts = inp("w2ts", [D, TPC])
    w3ts = inp("w3ts", [TPC, D])
    o_x2 = nc.dram_tensor("o_x2", [D, TPC], F32, kind="ExternalOutput").ap()
    o_w1q = nc.dram_tensor("o_w1q", [D, TPC], BF16, kind="ExternalOutput").ap()
    o_w2q = nc.dram_tensor("o_w2q", [D, TPC], BF16, kind="ExternalOutput").ap()
    o_w3q = nc.dram_tensor("o_w3q", [TPC, D], BF16, kind="ExternalOutput").ap()
    o_wm = nc.dram_tensor("o_wm", [1, 4], F32, kind="ExternalOutput").ap()

    st = ExitStack()
    E = st.enter_context
    x1 = E(nc.sbuf_tensor("x1", [128, NDC * TPC], F32))
    x1n = E(nc.sbuf_tensor("x1n", [128, NDC * TPC], BF16))
    lnscr = E(nc.sbuf_tensor("lnscr", [128, TPC], F32))
    rows = E(nc.sbuf_tensor("rows", [1, 4 * TPC], F32))
    statc = E(nc.sbuf_tensor("statc", [128, NTT * 8], F32))
    dp_sb = E(nc.sbuf_tensor("dp_sb", [128, NDC], F32))
    b1_sb = E(nc.sbuf_tensor("b1_sb", [128, 64], F32))
    b2_sb = E(nc.sbuf_tensor("b2_sb", [128, NDC], F32))
    id_sb = E(nc.sbuf_tensor("id_sb", [128, 128], F32))
    or_sb = E(nc.sbuf_tensor("or_sb", [1, 128], F32))
    cst_sb = E(nc.sbuf_tensor("cst_sb", [128, len(CONSTS_C)], F32))
    msb = E(nc.sbuf_tensor("msb", [8, 4], F32))
    wsm = E(nc.sbuf_tensor("wsm", [128, 4], F32))
    P = [E(nc.psum_tensor(f"P{i}", [128, 512], F32)) for i in range(8)]

    for i, v in enumerate(CONSTS_C):
        nc.const_aps.register(F32, v, cst_sb[:, i:i + 1])

    p = Prog(nc, st)
    sLc = p.sem("sLc")
    sLx = [p.sem("sLx0"), p.sem("sLx1")]
    sLy = [p.sem("sLy0"), p.sem("sLy1")]
    sTr = p.sem("sTr")
    sMM = p.sem("sMM")
    sOut = [p.sem("sOut0"), p.sem("sOut1")]
    sTl = [p.sem("sTl0"), p.sem("sTl1")]
    sTq = [p.sem("sTq0"), p.sem("sTq1")]
    sWm = p.sem("sWm")
    mm_cnt = [0]

    def MM(t, out_ap, lhs, rhs, start=True, stop=True):
        t.matmul(out_ap, lhs, rhs, start=start, stop=stop).then_inc(sMM, 1)
        mm_cnt[0] += 1

    def consts(y):
        for dst, srcd in ((dp_sb, dpc), (b1_sb, b1c), (b2_sb, b2c),
                          (id_sb, ident), (or_sb, onesrow), (cst_sb, constv),
                          (msb, msums)):
            y.dma_start(out=dst[:, :], in_=srcd).then_inc(sLc, 16)
    p.y(consts)
    NLC = 16 * 7

    ph1 = ExitStack()
    E1 = ph1.enter_context
    xtm = E1(nc.sbuf_tensor("xtm", [128, 2 * D], F32))
    sqt = E1(nc.sbuf_tensor("sqt", [128, 2 * D], F32))
    x0t = E1(nc.sbuf_tensor("x0t", [128, 2 * D], F32))
    y_t = E1(nc.sbuf_tensor("y_t", [128, 2 * D], F32))

    aff_mark = [0] * NTT
    red_mark = [0] * NTT
    ev1_mark = [0] * NTT

    for tt in range(NTT):
        sl = slice((tt % 2) * D, (tt % 2) * D + D)

        def load(y, tt=tt, sl=sl):
            if tt >= 2:
                y.wait_ge(p.auto["v"], aff_mark[tt - 2])
                y.wait_ge(sTr, 2 * (tt - 1))
            y.dma_start(out=xtm[:, sl],
                        in_=xr[tt * 128:(tt + 1) * 128, :]).then_inc(sLx[tt % 2], 16)
            if tt >= 2:
                y.wait_ge(p.auto["v"], ev1_mark[tt - 2])
            y.dma_start(
                out=y_t[:, sl],
                in_=yin.rearrange("(c p) t -> p c t", p=128)
                [:, :, tt * 128:(tt + 1) * 128]).then_inc(sLy[tt % 2], 16)
        p.y(load)

        def square(s, tt=tt, sl=sl):
            s.wait_ge(sLc, NLC)
            s.wait_ge(sLx[tt % 2], 16 * (tt // 2 + 1))
            if tt >= 2:
                s.wait_mark("v", red_mark[tt - 2])
            s.activation(out=sqt[:, sl], in_=xtm[:, sl], func=AF.Square)
        p.s(square)
        m_sq = p.mark("s")

        def stats(v, tt=tt, sl=sl):
            c = lambda k: statc[:, tt * 8 + k: tt * 8 + k + 1]
            v.wait_mark("s", m_sq)
            v.reduce_sum(c(0), xtm[:, sl], axis=AX.X)
            v.reduce_sum(c(1), sqt[:, sl], axis=AX.X)
            red_mark[tt] = p.cnt["v"]
            v.tensor_scalar_mul(c(2), c(0), 1.0 / D)
            v.tensor_scalar_mul(c(3), c(1), 1.0 / D)
            v.tensor_mul(c(7), c(2), c(2))
            v.tensor_sub(c(4), c(3), c(7))
        p.v(stats)
        m_var = p.mark("v")

        def rstd(s, tt=tt):
            c = lambda k: statc[:, tt * 8 + k: tt * 8 + k + 1]
            s.wait_mark("v", m_var)
            s.activation(out=c(5), in_=c(4), func=AF.Ln, bias=LN_EPS)
            s.activation(out=c(6), in_=c(5), func=AF.Exp, scale=-0.5)
        p.s(rstd)
        m_rstd = p.mark("s")

        def affine(v, tt=tt, sl=sl):
            c = lambda k: statc[:, tt * 8 + k: tt * 8 + k + 1]
            v.wait_mark("s", m_rstd)
            if tt >= 2:
                v.wait_ge(sTr, 2 * (tt - 1))
            v.tensor_scalar(out=x0t[:, sl], in0=xtm[:, sl], scalar1=c(2),
                            scalar2=c(6), op0=ALU.subtract, op1=ALU.mult)
            aff_mark[tt] = p.cnt["v"]
        p.v(affine)

        def transposes(t, tt=tt):
            t.wait_ge(p.auto["v"], aff_mark[tt])
            t.wait_ge(sLc, NLC)
            if tt >= 1:
                t.wait_ge(p.auto["v"], ev1_mark[tt - 1])
            last = None
            for k in range(NDC):
                last = t.transpose(
                    P[k // 4][:, (k % 4) * 128:(k % 4) * 128 + 128],
                    xtm[:, (tt % 2) * D + k * 128:(tt % 2) * D + (k + 1) * 128],
                    id_sb[:, :])
            for k in range(NDC):
                last = t.transpose(
                    P[4 + k // 4][:, (k % 4) * 128:(k % 4) * 128 + 128],
                    x0t[:, (tt % 2) * D + k * 128:(tt % 2) * D + (k + 1) * 128],
                    id_sb[:, :])
            last.then_inc(sTr, 2)
        p.t(transposes)

        def ev1(v, tt=tt):
            v.wait_ge(sTr, 2 * (tt + 1))
            v.wait_ge(sLy[tt % 2], 16 * (tt // 2 + 1))
            for k in range(NDC):
                xb = P[k // 4][:, (k % 4) * 128:(k % 4) * 128 + 128]
                nb = P[4 + k // 4][:, (k % 4) * 128:(k % 4) * 128 + 128]
                cs = slice(k * TPC + tt * 128, k * TPC + (tt + 1) * 128)
                v.scalar_tensor_tensor(
                    out=x1[:, cs], in0=nb, scalar=dp_sb[:, k:k + 1],
                    in1=y_t[:, (tt % 2) * D + k * 128:(tt % 2) * D + (k + 1) * 128],
                    op0=ALU.mult, op1=ALU.add)
                v.tensor_add(x1[:, cs], x1[:, cs], xb)
            ev1_mark[tt] = p.cnt["v"]
        p.v(ev1)

    ph1.close()

    ln1_mark = _ln_fm(p, P, MM, mm_cnt, sMM, cst_sb, or_sb, rows, lnscr,
                      x1, x1n, 0)

    ph2 = ExitStack()
    E2 = ph2.enter_context
    hh = E2(nc.sbuf_tensor("hh", [128, 64 * 512], BF16))
    w1 = E2(nc.sbuf_tensor("w1", [128, 2 * 2048], BF16))
    w2 = E2(nc.sbuf_tensor("w2", [128, 2 * 4096], BF16))

    HOC = 64
    wl_mark = {}
    mmw_mark = {}
    ge_mark = {}
    w2l_mark = {}
    mm2_mark = {}
    ev2_mark = {}

    for h in range(2):
        for oc in range(HOC):
            widx = h * HOC + oc
            ws_ = widx % 2

            def wload(g, oc=oc, widx=widx, ws_=ws_):
                if widx == 0:
                    g.wait_ge(p.auto["v"], ln1_mark)
                if widx >= 2:
                    g.wait_ge(sMM, mmw_mark.get(widx - 2, 0))
                g.dma_start(
                    out=w1[:, ws_ * 2048:ws_ * 2048 + 2048],
                    in_=p1t.rearrange("(c p) o -> p c o", p=128)
                    [:, :, oc * 128:(oc + 1) * 128])
                wl_mark[widx] = p.cnt["gd"]
            p.g(wload)

            def p1mm(t, h=h, oc=oc, widx=widx, ws_=ws_):
                t.wait_ge(p.auto["gd"], wl_mark[widx])
                t.wait_ge(p.auto["v"], ln1_mark)
                if widx >= 2:
                    t.wait_ge(p.auto["s"], ge_mark.get(widx - 2, 0))
                for dc in range(NDC):
                    MM(t, P[widx % 2][:, :],
                       w1[:, ws_ * 2048 + dc * 128: ws_ * 2048 + (dc + 1) * 128],
                       x1n[:, dc * TPC + h * 512: dc * TPC + (h + 1) * 512],
                       start=(dc == 0), stop=(dc == NDC - 1))
                mmw_mark[widx] = mm_cnt[0]
            p.t(p1mm)

            def gelu_ev(s, oc=oc, widx=widx):
                s.wait_ge(sMM, mmw_mark[widx])
                s.activation(out=hh[:, oc * 512:(oc + 1) * 512],
                             in_=P[widx % 2][:, :], func=AF.Gelu,
                             bias=b1_sb[:, oc:oc + 1])
                ge_mark[widx] = p.cnt["s"]
            p.s(gelu_ev)

        for ec in range(NDC):
            eidx = h * NDC + ec
            for kh in range(2):
                ki = 2 * eidx + kh

                def w2load(g, ec=ec, kh=kh, ki=ki):
                    if ki >= 2:
                        g.wait_ge(sMM, mm2_mark.get(ki - 2, 0))
                    g.dma_start(
                        out=w2[:, (ki % 2) * 4096:(ki % 2) * 4096 + 4096],
                        in_=p2t.rearrange("(c p) o -> p c o", p=128)
                        [:, kh * 32:(kh + 1) * 32, ec * 128:(ec + 1) * 128])
                    w2l_mark[ki] = p.cnt["gd"]
                p.g(w2load)

                def p2mm(t, h=h, ec=ec, eidx=eidx, kh=kh, ki=ki):
                    t.wait_ge(p.auto["gd"], w2l_mark[ki])
                    t.wait_ge(p.auto["s"], ge_mark[h * HOC + HOC - 1])
                    if kh == 0 and eidx >= 2:
                        t.wait_ge(p.auto["v"], ev2_mark.get(eidx - 2, 0))
                    for j in range(32):
                        hc = kh * 32 + j
                        MM(t, P[2 + eidx % 2][:, :],
                           w2[:, (ki % 2) * 4096 + j * 128:(ki % 2) * 4096 + (j + 1) * 128],
                           hh[:, hc * 512:(hc + 1) * 512],
                           start=(hc == 0), stop=(hc == 63))
                    mm2_mark[ki] = mm_cnt[0]
                p.t(p2mm)

            def x2ev(v, h=h, ec=ec, eidx=eidx):
                v.wait_ge(sMM, mm2_mark[2 * eidx + 1])
                cs = slice(ec * TPC + h * 512, ec * TPC + (h + 1) * 512)
                v.scalar_tensor_tensor(out=x1[:, cs], in0=P[2 + eidx % 2][:, :],
                                       scalar=b2_sb[:, ec:ec + 1], in1=x1[:, cs],
                                       op0=ALU.add, op1=ALU.add)
                ev2_mark[eidx] = p.cnt["v"]
            p.v(x2ev)

            def x2out(y, h=h, ec=ec, eidx=eidx):
                y.wait_ge(p.auto["v"], ev2_mark[eidx])
                y.dma_start(
                    out=o_x2[ec * 128:(ec + 1) * 128, h * 512:(h + 1) * 512],
                    in_=x1[:, ec * TPC + h * 512:ec * TPC + (h + 1) * 512]
                ).then_inc(sOut[eidx % 2], 16)
            p.y(x2out)

    ph2.close()
    mm_ffn = mm_cnt[0]

    def wmsum(t):
        t.wait_ge(sLc, NLC)
        MM(t, P[4][0:1, 0:4], cst_sb[0:8, 3:4], msb[0:8, 0:4])
    p.t(wmsum)
    mm_ws = mm_cnt[0]

    def wmev(v):
        v.wait_ge(sMM, mm_ws)
        v.tensor_scalar_mul(rows[0:1, 0:4], P[4][0:1, 0:4], 1.0 / (D * 4 * D))
        v.tensor_scalar_max(rows[0:1, 0:4], rows[0:1, 0:4], 1e-5)
        v.reciprocal(rows[0:1, 4:8], rows[0:1, 0:4])
    p.v(wmev)
    m_wmr = p.mark("v")

    def wmout(y):
        y.wait_ge(p.auto["v"], m_wmr)
        y.dma_start(out=o_wm, in_=rows[0:1, 0:4]).then_inc(sWm, 16)
    p.y(wmout)

    def wmbc(t):
        t.wait_ge(p.auto["v"], m_wmr)
        MM(t, P[5][:, 0:4], or_sb[:, :], rows[0:1, 4:8])
    p.t(wmbc)
    mm_wb = mm_cnt[0]

    def wmcols(v):
        v.wait_ge(sMM, mm_wb)
        v.tensor_copy(out=wsm[:, :], in_=P[5][:, 0:4])
    p.v(wmcols)
    m_wsm = p.mark("v")

    pht = ExitStack()
    Et = pht.enter_context
    twld = Et(nc.sbuf_tensor("twld", [128, 2 * 2048], F32))
    tc1 = Et(nc.sbuf_tensor("tc1", [128, 2048], F32))
    tc2 = Et(nc.sbuf_tensor("tc2", [128, 2048], F32))
    tqt = Et(nc.sbuf_tensor("tqt", [128, 2 * 2048], BF16))

    ttiles = []
    for w, (src, dst, R, C) in enumerate(((w1ts, o_w1q, D, TPC),
                                          (w2ts, o_w2q, D, TPC),
                                          (w3ts, o_w3q, TPC, D))):
        for a in range(R // 128):
            ttiles.append((w, src, dst, a, C))
    t_mark = {}
    for ti, (w, src, dst, a, C) in enumerate(ttiles):
        slot = ti % 2

        def tl(y, src=src, a=a, C=C, ti=ti, slot=slot):
            if ti < 2:
                y.wait_ge(sMM, mm_ffn)
            if ti >= 2:
                y.wait_ge(sTq[slot], 16 * (ti // 2))
            y.dma_start(out=twld[:, slot * 2048:slot * 2048 + C],
                        in_=src[a * 128:(a + 1) * 128, :]
                        ).then_inc(sTl[slot], 16)
        p.y(tl)

        def tq(v, w=w, C=C, ti=ti, slot=slot):
            v.wait_ge(sTl[slot], 16 * (ti // 2 + 1))
            if ti == 0:
                v.wait_mark("v", m_wsm)
            if ti >= 2:
                v.wait_ge(sTq[slot], 16 * (ti // 2))
            s0 = slot * 2048
            H = C // 2
            sc = wsm[:, w:w + 1]
            for h0 in (0, H):
                v.tensor_scalar(out=tc1[:, h0:h0 + H],
                                in0=twld[:, s0 + h0:s0 + h0 + H],
                                scalar1=sc, scalar2=0.5,
                                op0=ALU.mult, op1=ALU.is_gt)
            for h0 in (0, H):
                v.tensor_scalar(out=tc2[:, h0:h0 + H],
                                in0=twld[:, s0 + h0:s0 + h0 + H],
                                scalar1=sc, scalar2=-0.5,
                                op0=ALU.mult, op1=ALU.is_lt)
            for h0 in (0, H):
                v.tensor_sub(tqt[:, s0 + h0:s0 + h0 + H],
                             tc1[:, h0:h0 + H], tc2[:, h0:h0 + H])
            t_mark[ti] = p.cnt["v"]
        p.v(tq)

        def tout(y, dst=dst, a=a, C=C, ti=ti, slot=slot):
            y.wait_ge(p.auto["v"], t_mark[ti])
            y.dma_start(out=dst[a * 128:(a + 1) * 128, :],
                        in_=tqt[:, slot * 2048:slot * 2048 + C]
                        ).then_inc(sTq[slot], 16)
        p.y(tout)
    NTI = len(ttiles)
    pht.close()

    def finish(y):
        y.wait_ge(sOut[0], 16 * NDC)
        y.wait_ge(sOut[1], 16 * NDC)
        y.wait_ge(sTq[0], 16 * ((NTI + 1) // 2))
        y.wait_ge(sTq[1], 16 * (NTI // 2))
        y.wait_ge(sWm, 16)
    p.y(finish)

    st.close()
    return nc




def _tender_quant(p, P, MM, mm_cnt, sMM, id_sb, or_sb, lnscr, bufs,
                  src_sb, dst_bf, start_mm_wait):
    (chb, chb_bf, cmx, statq, tmx, t4, trow, tb, tbj, idxq, scq, rscq) = bufs
    CH = 4
    for ch in range(CH):
        for dc in range(NDC):
            def stat1(v, ch=ch, dc=dc):
                if ch == 0 and dc == 0:
                    v.wait_ge(sMM, start_mm_wait)
                cs = slice(dc * TPC + ch * 256, dc * TPC + (ch + 1) * 256)
                c0 = statq[:, 0:1]; c1 = statq[:, 1:2]; c2 = statq[:, 2:3]
                cb = chb[:, dc * 4 + ch:dc * 4 + ch + 1]
                v.reduce_max(c0, src_sb[:, cs], axis=AX.X)
                v.tensor_reduce(c1, src_sb[:, cs], axis=AX.X, op=ALU.min)
                v.tensor_add(c2, c0, c1)
                v.tensor_scalar_mul(cb, c2, 0.5)
                v.tensor_scalar(out=lnscr[:, 0:256], in0=src_sb[:, cs],
                                scalar1=cb, scalar2=None, op0=ALU.subtract)
                v.reduce_max(cmx[:, ch * NDC + dc:ch * NDC + dc + 1],
                             lnscr[:, 0:256], axis=AX.X,
                             apply_absolute_value=True)
            p.v(stat1)

        def tmax1(v, ch=ch):
            v.reduce_max(tmx[:, ch:ch + 1],
                         cmx[:, ch * NDC:(ch + 1) * NDC], axis=AX.X)
        p.v(tmax1)
    m_tmx = p.mark("v")

    def tmaxtr(t):
        t.wait_ge(p.auto["v"], m_tmx)
        t.transpose(P[0][0:4, 0:128], tmx[:, 0:4], id_sb[:, :]).then_inc(sMM, 1)
        mm_cnt[0] += 1
    p.t(tmaxtr)
    mm_t1 = mm_cnt[0]

    def tmaxred(v):
        v.wait_ge(sMM, mm_t1)
        v.reduce_max(t4[0:4, 0:1], P[0][0:4, 0:128], axis=AX.X)
    p.v(tmaxred)
    m_t4 = p.mark("v")

    def tmaxtr2(t):
        t.wait_ge(p.auto["v"], m_t4)
        t.transpose(P[1][0:1, 0:4], t4[0:4, 0:1],
                    id_sb[0:4, 0:4]).then_inc(sMM, 1)
        mm_cnt[0] += 1
    p.t(tmaxtr2)
    mm_t2 = mm_cnt[0]

    def trowev(v):
        v.wait_ge(sMM, mm_t2)
        v.tensor_copy(out=trow[0:1, :], in_=P[1][0:1, 0:4])
    p.v(trowev)
    m_trow = p.mark("v")

    def tbcast(t):
        t.wait_ge(p.auto["v"], m_trow)
        MM(t, P[1][:, 0:4], or_sb[:, :], trow[0:1, :])
    p.t(tbcast)
    mm_t3 = mm_cnt[0]

    def thr(v):
        v.wait_ge(sMM, mm_t3)
        v.tensor_copy(out=tb[:, :], in_=P[1][:, 0:4])
        for ch in range(CH):
            for j in range(7):
                v.tensor_scalar_mul(tbj[:, ch * 7 + j:ch * 7 + j + 1],
                                    tb[:, ch:ch + 1], 2.0 ** (j - 7))
        v.memset(idxq[:, :], 0.0)
        for ch in range(CH):
            for j in range(7):
                v.scalar_tensor_tensor(
                    out=idxq[:, ch * NDC:(ch + 1) * NDC],
                    in0=cmx[:, ch * NDC:(ch + 1) * NDC],
                    scalar=tbj[:, ch * 7 + j:ch * 7 + j + 1],
                    in1=idxq[:, ch * NDC:(ch + 1) * NDC],
                    op0=ALU.is_gt, op1=ALU.add)
    p.v(thr)
    m_idx = p.mark("v")

    def p2i(s):
        s.wait_mark("v", m_idx)
        s.activation(out=idxq[:, :], in_=idxq[:, :], func=AF.Exp, scale=LN2C)
    p.s(p2i)
    m_p2i = p.mark("s")

    def mkscale(v):
        v.wait_mark("s", m_p2i)
        for ch in range(CH):
            v.tensor_scalar(out=scq[:, ch * NDC:(ch + 1) * NDC],
                            in0=idxq[:, ch * NDC:(ch + 1) * NDC],
                            scalar1=tb[:, ch:ch + 1], scalar2=1.0 / 896.0,
                            op0=ALU.mult, op1=ALU.mult)
        v.tensor_scalar_max(scq[:, :], scq[:, :], 1e-9)
        v.reciprocal(rscq[:, :], scq[:, :])
        v.tensor_copy(out=chb_bf[:, :], in_=chb[:, :])
    p.v(mkscale)

    for dc in range(NDC):
        for ch in range(CH):
            def rq(v, dc=dc, ch=ch):
                cs = slice(dc * TPC + ch * 256, dc * TPC + (ch + 1) * 256)
                t0 = lnscr[:, 0:256]
                m0 = lnscr[:, 256:512]
                v.tensor_scalar(out=t0, in0=src_sb[:, cs],
                                scalar1=chb[:, dc * 4 + ch:dc * 4 + ch + 1],
                                scalar2=rscq[:, ch * NDC + dc:ch * NDC + dc + 1],
                                op0=ALU.subtract, op1=ALU.mult)
                v.tensor_scalar_add(m0, t0, LAD4_C)
                _emit_ladder(v, [(lnscr[:, 256:384], lnscr[:, 512:640]),
                                 (lnscr[:, 384:512], lnscr[:, 640:768])],
                             LAD4)
                v.scalar_tensor_tensor(out=t0, in0=m0, scalar=-1.0, in1=t0,
                                       op0=ALU.mult, op1=ALU.add)
                v.tensor_scalar(out=dst_bf[:, cs], in0=t0, scalar1=0.5,
                                scalar2=scq[:, ch * NDC + dc:ch * NDC + dc + 1],
                                op0=ALU.add, op1=ALU.mult)
            p.v(rq)
    return p.mark("v")


def _build_C2():
    nc = bass.Bass("TRN2", target_bir_lowering=False)
    inp = lambda n, sh, dt=F32: nc.dram_tensor(n, sh, dt, kind="ExternalInput").ap()
    x2in = inp("x2in", [D, TPC])
    qit_d = [inp("qit0", [D, D], BF16), inp("qit1", [D, D], BF16),
             inp("qit2", [D, D], BF16)]
    wtr_d = [inp("wtbf0", [D, D], BF16), inp("wtbf1", [D, D], BF16),
             inp("wtbf2", [D, D], BF16)]
    winsc = inp("winsc", [128, 4 * NDC])
    bcols = [inp("qbc", [128, NDC]), inp("kbc", [128, NDC]),
             inp("vbc", [128, NDC])]
    ident = inp("ident", [128, 128])
    onesrow = inp("onesrow", [1, 128])
    constv = inp("constv", [128, len(CONSTS_C)])
    o_q = nc.dram_tensor("o_q", [D, TPC], BF16, kind="ExternalOutput").ap()
    o_k = nc.dram_tensor("o_k", [D, TPC], BF16, kind="ExternalOutput").ap()
    o_v = nc.dram_tensor("o_v", [TPC, D], BF16, kind="ExternalOutput").ap()

    SC = 0.125
    st = ExitStack()
    E = st.enter_context
    x2 = E(nc.sbuf_tensor("x2", [128, NDC * TPC], F32))
    resbf = E(nc.sbuf_tensor("resbf", [128, NDC * TPC], BF16))
    lnscr = E(nc.sbuf_tensor("lnscr", [128, TPC], F32))
    rows = E(nc.sbuf_tensor("rows", [1, 4 * TPC], F32))
    id_sb = E(nc.sbuf_tensor("id_sb", [128, 128], F32))
    or_sb = E(nc.sbuf_tensor("or_sb", [1, 128], F32))
    cst_sb = E(nc.sbuf_tensor("cst_sb", [128, len(CONSTS_C)], F32))
    bq_sb = E(nc.sbuf_tensor("bq_sb", [128, 3 * NDC], F32))
    chb = E(nc.sbuf_tensor("chb", [128, NDC * 4], F32))
    chb_bf = E(nc.sbuf_tensor("chb_bf", [128, NDC * 4], BF16))
    cmx = E(nc.sbuf_tensor("cmx", [128, 4 * NDC], F32))
    statq = E(nc.sbuf_tensor("statq", [128, 16], F32))
    tmx = E(nc.sbuf_tensor("tmx", [128, 4], F32))
    t4 = E(nc.sbuf_tensor("t4", [4, 1], F32))
    trow = E(nc.sbuf_tensor("trow", [1, 4], F32))
    tb = E(nc.sbuf_tensor("tb", [128, 4], F32))
    tbj = E(nc.sbuf_tensor("tbj", [128, 4 * 7], F32))
    idxq = E(nc.sbuf_tensor("idxq", [128, 4 * NDC], F32))
    scq = E(nc.sbuf_tensor("scq", [128, 4 * NDC], F32))
    rscq = E(nc.sbuf_tensor("rscq", [128, 4 * NDC], F32))
    ws = E(nc.sbuf_tensor("ws", [128, 4 * NDC], F32))
    wsx = E(nc.sbuf_tensor("wsx", [128, 3 * NDC], F32))
    cbw = E(nc.sbuf_tensor("cbw", [128, 3 * NDC * 4], F32))
    corr_s = E(nc.sbuf_tensor("corr_s", [4, 2048], F32))
    qv_sb = E(nc.sbuf_tensor("qv_sb", [128, 2 * TPC], BF16))
    vtm = E(nc.sbuf_tensor("vtm", [128, 2 * TPC], BF16))
    id_bf = E(nc.sbuf_tensor("id_bf", [128, 128], BF16))
    P = [E(nc.psum_tensor(f"P{i}", [128, 512], F32)) for i in range(8)]

    for i, v in enumerate(CONSTS_C):
        nc.const_aps.register(F32, v, cst_sb[:, i:i + 1])

    p = Prog(nc, st)
    sLc = p.sem("sLc")
    sLx2 = p.sem("sLx2")
    sMM = p.sem("sMM")
    sTi = p.sem("sTi")
    sOutQ = [p.sem("sOutQ0"), p.sem("sOutQ1")]
    sOutV = [p.sem("sOutV0"), p.sem("sOutV1")]
    mm_cnt = [0]

    def MM(t, out_ap, lhs, rhs, start=True, stop=True):
        t.matmul(out_ap, lhs, rhs, start=start, stop=stop).then_inc(sMM, 1)
        mm_cnt[0] += 1

    def consts(y):
        for dst, srcd in ((id_sb, ident), (or_sb, onesrow), (cst_sb, constv),
                          (ws, winsc)):
            y.dma_start(out=dst[:, :], in_=srcd).then_inc(sLc, 16)
        for i in range(3):
            y.dma_start(out=bq_sb[:, i * NDC:(i + 1) * NDC],
                        in_=bcols[i]).then_inc(sLc, 16)
        y.dma_start(out=x2[:, :],
                    in_=x2in.rearrange("(c p) t -> p c t", p=128)
                    ).then_inc(sLx2, 16)
    p.y(consts)
    NLC = 16 * 7

    def x2wait(v):
        v.wait_ge(sLx2, 16)
        v.wait_ge(sLc, NLC)
        v.tensor_copy(out=id_bf[:, :], in_=id_sb[:, :])
    p.v(x2wait)

    _ln_fm(p, P, MM, mm_cnt, sMM, cst_sb, or_sb, rows, lnscr, x2, x2, 0,
           first_sq_waits=((sLx2, 16), (sLc, NLC)))
    mm_ln = mm_cnt[0]

    tq_bufs = (chb, chb_bf, cmx, statq, tmx, t4, trow, tb, tbj, idxq, scq,
               rscq)
    _tender_quant(p, P, MM, mm_cnt, sMM, id_sb, or_sb, lnscr, tq_bufs,
                  x2, resbf, mm_ln)
    m_res = p.mark("v")

    m_wq = p.mark("v")

    def wsx_prep(v):
        v.wait_ge(sLc, NLC)
        v.tensor_scalar_mul(wsx[:, 0:NDC], ws[:, 0:NDC], SC)
        v.tensor_copy(out=wsx[:, NDC:3 * NDC], in_=ws[:, NDC:3 * NDC])
    p.v(wsx_prep)

    ph_c = ExitStack()
    wtb = ph_c.enter_context(nc.sbuf_tensor("wtb", [128, 2 * 2048], BF16))
    wt_mark = {}
    cbw_done = {}
    for w in range(3):
        for dc in range(NDC):
            ci = w * NDC + dc
            slot = ci % 2

            def wtload(g, w=w, dc=dc, ci=ci, slot=slot):
                if ci < 2:
                    g.wait_ge(p.auto["v"], m_wq)
                if ci >= 2:
                    g.wait_ge(sMM, wt_mark.get(ci - 2, 0))
                g.dma_start(out=wtb[:, slot * 2048:(slot + 1) * 2048],
                            in_=wtr_d[w][dc * 128:(dc + 1) * 128, :])
            p.g(wtload)
            gm = p.mark("g")

            def cmm(t, w=w, dc=dc, ci=ci, slot=slot, gm=gm):
                p.wait_mark(t, "g", gm)
                t.wait_ge(p.auto["v"], m_res)
                if dc == 0:
                    t.wait_ge(p.auto["v"], cbw_done.get(w - 1, 0))
                for og in range(4):
                    MM(t, P[2 + og][0:4, :], chb_bf[:, dc * 4:(dc + 1) * 4],
                       wtb[:, slot * 2048 + og * 512:slot * 2048 + (og + 1) * 512],
                       start=(dc == 0), stop=(dc == NDC - 1))
                wt_mark[ci] = mm_cnt[0]
            p.t(cmm)

        def cevac(s, w=w):
            s.wait_ge(sMM, wt_mark[w * NDC + NDC - 1])
            for og in range(4):
                s.activation(out=corr_s[:, og * 512:(og + 1) * 512],
                             in_=P[2 + og][0:4, :], func=AF.Copy)
        p.s(cevac)
        m_ce = p.mark("s")

        def ctr(t, w=w, m_ce=m_ce):
            t.wait_ge(p.auto["s"], m_ce)
            if w > 0:
                t.wait_ge(p.auto["v"], cbw_done[w - 1])
            last = None
            for oc in range(NDC):
                last = t.transpose(P[6][:, oc * 4:(oc + 1) * 4],
                                   corr_s[0:4, oc * 128:(oc + 1) * 128],
                                   id_sb[0:4, 0:4])
            last.then_inc(sMM, 1)
            mm_cnt[0] += 1
        p.t(ctr)
        mm_ct = mm_cnt[0]

        def cbias(v, w=w, mm_ct=mm_ct):
            v.wait_ge(sMM, mm_ct)
            for oc in range(NDC):
                v.tensor_scalar(
                    out=cbw[:, (w * NDC + oc) * 4:(w * NDC + oc) * 4 + 4],
                    in0=P[6][:, oc * 4:(oc + 1) * 4],
                    scalar1=bq_sb[:, w * NDC + oc:w * NDC + oc + 1],
                    scalar2=SC if w == 0 else 1.0,
                    op0=ALU.add, op1=ALU.mult)
            cbw_done[w] = p.cnt["v"]
        p.v(cbias)
    ph_c.close()
    g_corr_end = p.mark("g")

    ph_q = ExitStack()
    qiT = ph_q.enter_context(nc.sbuf_tensor("qiT", [128, NDC * 2048], BF16))
    ep_mark = {}
    vtr_mark = {}
    for w in range(3):
        def tload(y, w=w):
            if w == 0:
                p.wait_mark(y, "g", g_corr_end)
                y.wait_ge(p.auto["v"], cbw_done[2])
            if w > 0:
                for oc_ in (NDC - 2, NDC - 1):
                    y.wait_ge(sMM, ep_mark.get((w - 1, oc_, "mm"), 0))
            for dc in range(NDC):
                y.dma_start(
                    out=qiT[:, dc * 2048:(dc + 1) * 2048],
                    in_=qit_d[w][dc * 128:(dc + 1) * 128, :]
                ).then_inc(sTi, 16)
        p.y(tload)
        m_ti = 16 * NDC * (w + 1)

        for oc in range(NDC):
            bk = [4 + 2 * (oc % 2), 5 + 2 * (oc % 2)]

            def qmm(t, w=w, oc=oc, bk=bk):
                t.wait_ge(sTi, m_ti)
                t.wait_ge(p.auto["v"], m_res)
                if oc >= 2:
                    t.wait_ge(p.auto["v"], ep_mark.get((w, oc - 2, "v"), 0))
                elif oc < 2 and w > 0:
                    t.wait_ge(p.auto["v"],
                              ep_mark.get((w - 1, NDC - 2 + oc, "v"), 0))
                for dc in range(NDC):
                    for h in range(2):
                        MM(t, P[bk[h]][:, :],
                           qiT[:, dc * 2048 + oc * 128:dc * 2048 + (oc + 1) * 128],
                           resbf[:, dc * TPC + h * 512:dc * TPC + (h + 1) * 512],
                           start=(dc == 0), stop=(dc == NDC - 1))
                ep_mark[(w, oc, "mm")] = mm_cnt[0]
            p.t(qmm)

            if w < 2:
                def epi(v, w=w, oc=oc, bk=bk):
                    v.wait_ge(sMM, ep_mark[(w, oc, "mm")])
                    if (w * NDC + oc) >= 2:
                        pw, poc = divmod(w * NDC + oc - 2, NDC)
                        v.wait_ge(sOutQ[oc % 2], 16 * ((w * NDC + oc) // 2))
                    for h in range(2):
                        for c2 in range(2):
                            ch = 2 * h + c2
                            v.tensor_scalar(
                                out=qv_sb[:, (oc % 2) * TPC + ch * 256:
                                          (oc % 2) * TPC + (ch + 1) * 256],
                                in0=P[bk[h]][:, c2 * 256:(c2 + 1) * 256],
                                scalar1=wsx[:, w * NDC + oc:w * NDC + oc + 1],
                                scalar2=cbw[:, (w * NDC + oc) * 4 + ch:
                                            (w * NDC + oc) * 4 + ch + 1],
                                op0=ALU.mult, op1=ALU.add)
                    ep_mark[(w, oc, "v")] = p.cnt["v"]
                p.v(epi)

                def qkout(y, w=w, oc=oc):
                    y.wait_ge(p.auto["v"], ep_mark[(w, oc, "v")])
                    dst = o_q if w == 0 else o_k
                    y.dma_start(out=dst[oc * 128:(oc + 1) * 128, :],
                                in_=qv_sb[:, (oc % 2) * TPC:(oc % 2 + 1) * TPC]
                                ).then_inc(sOutQ[oc % 2], 16)
                p.y(qkout)
            else:
                def epiv(v, w=w, oc=oc, bk=bk):
                    v.wait_ge(sMM, ep_mark[(w, oc, "mm")])
                    if oc >= 2:
                        v.wait_ge(sMM, ep_mark[(2, oc - 2, "vtr")])
                    else:
                        v.wait_ge(sOutQ[0], 16 * 16)
                        v.wait_ge(sOutQ[1], 16 * 16)
                    for h in range(2):
                        for c2 in range(2):
                            ch = 2 * h + c2
                            v.tensor_scalar(
                                out=qv_sb[:, (oc % 2) * TPC + ch * 256:
                                          (oc % 2) * TPC + (ch + 1) * 256],
                                in0=P[bk[h]][:, c2 * 256:(c2 + 1) * 256],
                                scalar1=wsx[:, w * NDC + oc:w * NDC + oc + 1],
                                scalar2=cbw[:, (w * NDC + oc) * 4 + ch:
                                            (w * NDC + oc) * 4 + ch + 1],
                                op0=ALU.mult, op1=ALU.add)
                    ep_mark[(w, oc, "v")] = p.cnt["v"]
                p.v(epiv)

                def vtr(t, oc=oc):
                    t.wait_ge(p.auto["v"], ep_mark[(2, oc, "v")])
                    if oc > 0:
                        t.wait_ge(p.auto["s"], vtr_mark[oc - 1])
                    pbf = P[0][:, :].bitcast(BF16)
                    last = None
                    for tt in range(8):
                        last = t.transpose(
                            pbf[:, tt * 128:(tt + 1) * 128],
                            qv_sb[:, (oc % 2) * TPC + tt * 128:
                                  (oc % 2) * TPC + (tt + 1) * 128],
                            id_bf[:, :])
                    last.then_inc(sMM, 1)
                    mm_cnt[0] += 1
                    ep_mark[(2, oc, "vtr")] = mm_cnt[0]
                p.t(vtr)
                mm_vt = mm_cnt[0]

                def vev(s, oc=oc, mm_vt=mm_vt):
                    s.wait_ge(sMM, mm_vt)
                    if oc >= 2:
                        s.wait_ge(sOutV[oc % 2], 16 * (oc // 2))
                    pbf = P[0][:, :].bitcast(BF16)
                    s.activation(out=vtm[:, (oc % 2) * TPC:(oc % 2 + 1) * TPC],
                                 in_=pbf[:, :], func=AF.Copy)
                    vtr_mark[oc] = p.cnt["s"]
                p.s(vev)

                def vout(y, oc=oc):
                    y.wait_ge(p.auto["s"], vtr_mark[oc])
                    y.dma_start(
                        out=o_v[:, oc * 128:(oc + 1) * 128].rearrange(
                            "(tt p) o -> p tt o", p=128),
                        in_=vtm[:, (oc % 2) * TPC:(oc % 2 + 1) * TPC].rearrange(
                            "p (tt o) -> p tt o", o=128)
                    ).then_inc(sOutV[oc % 2], 16)
                p.y(vout)
    ph_q.close()

    def finish(y):
        y.wait_ge(sOutQ[0], 16 * 16)
        y.wait_ge(sOutQ[1], 16 * 16)
        y.wait_ge(sOutV[0], 16 * 8)
        y.wait_ge(sOutV[1], 16 * 8)
    p.y(finish)

    st.close()
    return nc


CONSTS_D = [LN_EPS, -0.5, LN2C, 1.0, 1e-6]


def _build_D():
    nc = bass.Bass("TRN2", target_bir_lowering=False)
    inp = lambda n, sh, dt=F32: nc.dram_tensor(n, sh, dt, kind="ExternalInput").ap()
    x2f = inp("x2f", [D, TPC])
    qf_d = inp("qf", [D, TPC], BF16)
    kf_d = inp("kf", [D, T], BF16)
    vt_d = inp("vt", [T, D], BF16)
    oqit = inp("oqit", [D, D], BF16)
    osc_d = inp("osc", [128, NDC])
    owt_d = inp("owbf", [D, D], BF16)
    obc_d = inp("obc", [128, NDC])
    wq_scr = [inp("w1qt", [D, 4 * D], BF16), inp("w2qt", [D, 4 * D], BF16),
              inp("w3qt", [4 * D, D], BF16)]
    wmb_d = inp("wmb", [128, 4])
    ident = inp("ident", [128, 128])
    onesrow = inp("onesrow", [1, 128])
    oab_d = inp("oab", [128, 65])
    or64_d = inp("or64", [65, 128])
    constv = inp("constv", [128, len(CONSTS_D)])
    o_x4 = nc.dram_tensor("o_x4", [D, TPC], F32, kind="ExternalOutput").ap()

    st = ExitStack()
    E = st.enter_context
    x2sb = E(nc.sbuf_tensor("x2sb", [128, NDC * TPC], F32))
    lnscr = E(nc.sbuf_tensor("lnscr", [128, TPC], F32))
    rows = E(nc.sbuf_tensor("rows", [1, 3 * TPC], F32))
    id_sb = E(nc.sbuf_tensor("id_sb", [128, 128], F32))
    id_bf = E(nc.sbuf_tensor("id_bf", [128, 128], BF16))
    or_sb = E(nc.sbuf_tensor("or_sb", [1, 128], F32))
    oab_sb = E(nc.sbuf_tensor("oab_sb", [128, 65], F32))
    or64_sb = E(nc.sbuf_tensor("or64_sb", [65, 128], F32))
    cst_sb = E(nc.sbuf_tensor("cst_sb", [128, len(CONSTS_D)], F32))
    osc_sb = E(nc.sbuf_tensor("osc_sb", [128, NDC], F32))
    obc_sb = E(nc.sbuf_tensor("obc_sb", [128, NDC], F32))
    chb = E(nc.sbuf_tensor("chb", [128, NDC * 4], F32))
    chb_bf = E(nc.sbuf_tensor("chb_bf", [128, NDC * 4], BF16))
    cmx = E(nc.sbuf_tensor("cmx", [128, 4 * NDC], F32))
    statq = E(nc.sbuf_tensor("statq", [128, 16], F32))
    tmx = E(nc.sbuf_tensor("tmx", [128, 4], F32))
    t4 = E(nc.sbuf_tensor("t4", [4, 1], F32))
    trow = E(nc.sbuf_tensor("trow", [1, 4], F32))
    tb = E(nc.sbuf_tensor("tb", [128, 4], F32))
    tbj = E(nc.sbuf_tensor("tbj", [128, 4 * 7], F32))
    idxq = E(nc.sbuf_tensor("idxq", [128, 4 * NDC], F32))
    scq = E(nc.sbuf_tensor("scq", [128, 4 * NDC], F32))
    rscq = E(nc.sbuf_tensor("rscq", [128, 4 * NDC], F32))
    cbo = E(nc.sbuf_tensor("cbo", [128, NDC * 4], F32))
    wsc = E(nc.sbuf_tensor("wsc", [128, 8], F32))
    P = [E(nc.psum_tensor(f"P{i}", [128, 512], F32)) for i in range(4)]
    SA2 = E(nc.psum_tensor("SA2", [128, 1024], F32))
    SB2 = E(nc.psum_tensor("SB2", [128, 1024], F32))

    phCtx = ExitStack()
    ctxf = phCtx.enter_context(nc.sbuf_tensor("ctxf", [128, NDC * TPC], BF16))
    PL = P + [SA2[:, 0:512], SA2[:, 512:1024], SB2[:, 0:512], SB2[:, 512:1024]]
    for i, v in enumerate(CONSTS_D):
        nc.const_aps.register(F32, v, cst_sb[:, i:i + 1])

    p = Prog(nc, st)
    sLc = p.sem("sLc")
    sLx2 = p.sem("sLx2")
    sMM = p.sem("sMM")
    sKL = [p.sem("sKL0"), p.sem("sKL1")]
    sVA = [p.sem("sVA0"), p.sem("sVA1")]
    sQL = p.sem("sQL")
    sOQ = [p.sem("sOQ0"), p.sem("sOQ1")]
    sOW = [p.sem("sOW0"), p.sem("sOW1")]
    sOut = [p.sem("sOutD0"), p.sem("sOutD1")]
    sCtxB = p.sem("sCtxB")
    mm_cnt = [0]

    def MM(t, out_ap, lhs, rhs, start=True, stop=True):
        t.matmul(out_ap, lhs, rhs, start=start, stop=stop).then_inc(sMM, 1)
        mm_cnt[0] += 1

    def consts(y):
        for dst, srcd in ((id_sb, ident), (or_sb, onesrow), (cst_sb, constv),
                          (oab_sb, oab_d), (osc_sb, osc_d), (obc_sb, obc_d),
                          (or64_sb, or64_d)):
            y.dma_start(out=dst[:, :], in_=srcd).then_inc(sLc, 16)
        y.dma_start(out=wsc[:, 4:8], in_=wmb_d).then_inc(sLc, 16)
        y.dma_start(out=x2sb[:, :],
                    in_=x2f.rearrange("(c p) t -> p c t", p=128)
                    ).then_inc(sLx2, 16)
    p.y(consts)
    NLC = 16 * 8

    phA = ExitStack()
    EA = phA.enter_context
    qsb = EA(nc.sbuf_tensor("qsb", [128, NDC * TPC], BF16))
    khp = EA(nc.sbuf_tensor("khp", [128, T], BF16))
    vaA = EA(nc.sbuf_tensor("vaA", [128, 32 * 65], BF16))
    vaB = EA(nc.sbuf_tensor("vaB", [128, 32 * 65], BF16))
    ptA = EA(nc.sbuf_tensor("ptA", [128, 2 * 1024], BF16))
    ptB = EA(nc.sbuf_tensor("ptB", [128, 2 * 1024], BF16))
    sqs = EA(nc.sbuf_tensor("sqs", [128, T], F32))
    qsq = EA(nc.sbuf_tensor("qsq", [128, TPC], F32))
    nrm = EA(nc.sbuf_tensor("nrm", [128, 8], F32))
    mcols = EA(nc.sbuf_tensor("mcols", [128, 2], F32))
    rl_sb = EA(nc.sbuf_tensor("rl_sb", [1, 2 * 512], F32))
    rlb = EA(nc.sbuf_tensor("rlb", [128, 2 * 512], F32))
    ctmp = EA(nc.sbuf_tensor("ctmp", [64, 512], BF16))

    def qload(y):
        y.dma_start(out=qsb[:, :],
                    in_=qf_d.rearrange("(c p) t -> p c t", p=128)
                    ).then_inc(sQL, 16)
    p.y(qload)

    def idcast(v):
        v.wait_ge(sLc, NLC)
        v.tensor_copy(out=id_bf[:, :], in_=id_sb[:, :])
        for ti in range(32):
            v.memset(vaA[:, ti * 65 + 64:ti * 65 + 65], 1.0)
            v.memset(vaB[:, ti * 65 + 64:ti * 65 + 65], 1.0)
    p.v(idcast)
    m_idc = p.mark("v")

    ev_ctx = {}
    exp_mark = {}
    sv_done = {}

    for hp in range(16):
        slot = hp % 2

        def kvload(y, hp=hp, slot=slot):
            if hp >= 1:
                y.wait_ge(sMM, sv_done[(hp - 1, "smm")])
            else:
                y.wait_ge(p.auto["v"], m_idc)
            y.dma_start(out=khp[:, :],
                        in_=kf_d[hp * 128:(hp + 1) * 128, :]).then_inc(sKL[0], 16)
            if hp >= 1:
                y.wait_ge(sMM, sv_done[(hp - 1, "pvmm")])
            for hh, va in ((0, vaA), (1, vaB)):
                y.dma_start(
                    out=va.ap().rearrange("p (ti e) -> p ti e", e=65)[:, :, 0:64],
                    in_=vt_d[:, (2 * hp + hh) * 64:(2 * hp + hh + 1) * 64]
                    .rearrange("(ti p) o -> p ti o", p=128)
                ).then_inc(sVA[0], 16)
        p.y(kvload)

        def msq(s, hp=hp, slot=slot):
            s.wait_ge(sQL, 16)
            s.wait_ge(sLc, NLC)
            s.activation(out=qsq[:, :], in_=qsb[:, hp * TPC:(hp + 1) * TPC],
                         func=AF.Square)
            s.wait_ge(sKL[0], 16 * (hp + 1))
            s.activation(out=sqs[:, :], in_=khp[:, :], func=AF.Square)
        p.s(msq)
        m_msq = p.mark("s")

        def mnorm_mm(t, hp=hp, slot=slot):
            t.wait_ge(p.auto["s"], m_msq)
            if hp >= 1:
                t.wait_ge(p.auto["s"], exp_mark.get((hp - 1, "last"), 0))
            for h in range(2):
                MM(t, SA2[0:65, h * 512:(h + 1) * 512], oab_sb[:, :],
                   qsq[:, h * 512:(h + 1) * 512])
        p.t(mnorm_mm)
        mm_q = mm_cnt[0]

        def mq_red(v, hp=hp):
            v.wait_ge(sMM, mm_q)
            v.reduce_max(nrm[0:1, 0:1], SA2[0:1, 0:1024], axis=AX.X)
            v.reduce_max(nrm[64:65, 0:1], SA2[64:65, 0:1024], axis=AX.X)
        p.v(mq_red)
        m_nq = p.mark("v")

        knred_mark = {}
        for j in range(8):
            def knorm_mm(t, hp=hp, j=j):
                if j < 2:
                    t.wait_ge(p.auto["v"], m_nq)
                else:
                    t.wait_ge(p.auto["v"], knred_mark[j - 2])
                MM(t, SA2[0:65, (j % 2) * 512:(j % 2) * 512 + 512], oab_sb[:, :],
                   sqs[:, j * 512:(j + 1) * 512])
            p.t(knorm_mm)
            mm_kj = mm_cnt[0]

            def kn_red(v, hp=hp, j=j, mm_kj=mm_kj):
                v.wait_ge(sMM, mm_kj)
                for r in (slice(0, 1), slice(64, 65)):
                    v.reduce_max(nrm[r, 2 + (j % 2):3 + (j % 2)],
                                 SA2[r, (j % 2) * 512:(j % 2) * 512 + 512],
                                 axis=AX.X)
                    if j == 0:
                        v.tensor_copy(out=nrm[r, 4:5], in_=nrm[r, 2:3])
                    else:
                        v.tensor_max(nrm[r, 4:5], nrm[r, 4:5],
                                     nrm[r, 2 + (j % 2):3 + (j % 2)])
                knred_mark[j] = p.cnt["v"]
            p.v(kn_red)

        def kn_fin(v, hp=hp):
            for r in (slice(0, 1), slice(64, 65)):
                v.tensor_mul(nrm[r, 5:6], nrm[r, 0:1], nrm[r, 4:5])
        p.v(kn_fin)
        m_kn = p.mark("v")

        def msqrt(s, hp=hp):
            s.wait_mark("v", m_kn)
            s.activation(out=nrm[0:1, 6:7], in_=nrm[0:1, 5:6], func=AF.Sqrt)
            s.activation(out=nrm[64:65, 6:7], in_=nrm[64:65, 5:6], func=AF.Sqrt)
        p.s(msqrt)
        m_ms = p.mark("s")

        def mneg(v, hp=hp):
            v.wait_mark("s", m_ms)
            v.tensor_scalar_mul(nrm[0:1, 7:8], nrm[0:1, 6:7], -1.0)
            v.tensor_scalar_mul(nrm[64:65, 7:8], nrm[64:65, 6:7], -1.0)
        p.v(mneg)
        m_mn = p.mark("v")

        def mbcast(t, hp=hp):
            t.wait_ge(p.auto["v"], m_mn)
            MM(t, SA2[:, 0:1], or_sb[:, :], nrm[0:1, 7:8])
            MM(t, SA2[:, 512:513], or64_sb[64:65, :], nrm[64:65, 7:8])
        p.t(mbcast)
        mm_mb = mm_cnt[0]

        def mevac(v, hp=hp):
            v.wait_ge(sMM, mm_mb)
            v.tensor_copy(out=mcols[:, 0:1], in_=SA2[:, 0:1])
            v.tensor_copy(out=mcols[:, 1:2], in_=SA2[:, 512:513])
        p.v(mevac)
        m_mc = p.mark("v")

        for tq in range(2):
            for tkp in range(16):
                li = (hp * 2 + tq) * 16 + tkp

                def smm(t, hp=hp, slot=slot, tq=tq, tkp=tkp, li=li):
                    t.wait_ge(p.auto["v"], m_mc)
                    if tkp >= 1:
                        t.wait_ge(p.auto["s"], exp_mark[(li - 1, "e")])
                    elif (hp, tq) != (0, 0):
                        t.wait_ge(p.auto["s"], exp_mark[(li - 1, "e")])
                    for u in range(2):
                        tkt = 2 * tkp + u
                        for hh, bank in ((0, SA2), (1, SB2)):
                            MM(t, bank[:, u * 512:(u + 1) * 512],
                               khp[hh * 64:(hh + 1) * 64,
                                   tkt * 128:(tkt + 1) * 128],
                               qsb[hh * 64:(hh + 1) * 64,
                                   hp * TPC + tq * 512: hp * TPC + (tq + 1) * 512]
                               if False else
                               qsb[:, hp * TPC + tq * 512:
                                   hp * TPC + (tq + 1) * 512][hh * 64:(hh + 1) * 64, :],
                               start=True, stop=True)
                    sv_done[(hp, "smm")] = mm_cnt[0]
                p.t(smm)
                mm_s = mm_cnt[0]

                def expo(s, hp=hp, tq=tq, tkp=tkp, li=li, mm_s=mm_s):
                    s.wait_ge(sMM, mm_s)
                    if li >= 2:
                        s.wait_ge(sMM, sv_done.get((li - 2, "pv"), 0))
                    s.activation(out=ptA[:, (tkp % 2) * 1024:(tkp % 2 + 1) * 1024],
                                 in_=SA2[:, :], func=AF.Exp,
                                 bias=mcols[:, 0:1])
                    s.activation(out=ptB[:, (tkp % 2) * 1024:(tkp % 2 + 1) * 1024],
                                 in_=SB2[:, :], func=AF.Exp,
                                 bias=mcols[:, 1:2])
                    exp_mark[(li, "e")] = p.cnt["s"]
                    exp_mark[(hp, "last")] = p.cnt["s"]
                p.s(expo)

                def pvmm(t, hp=hp, slot=slot, tq=tq, tkp=tkp, li=li):
                    t.wait_ge(p.auto["s"], exp_mark[(li, "e")])
                    t.wait_ge(sVA[0], 32 * (hp + 1))
                    if tkp == 0:
                        prev = (hp, 0) if tq == 1 else (hp - 1, 1)
                        t.wait_ge(p.auto["v"], ev_ctx.get(prev, 0))
                    for u in range(2):
                        tkt = 2 * tkp + u
                        for hh, va, pt in ((0, vaA, ptA), (1, vaB, ptB)):
                            MM(t, P[2 + hh][0:65, :],
                               va[:, tkt * 65:(tkt + 1) * 65],
                               pt[:, (tkp % 2) * 1024 + u * 512:
                                  (tkp % 2) * 1024 + (u + 1) * 512],
                               start=(tkt == 0), stop=(tkt == 31))
                    sv_done[(li, "pv")] = mm_cnt[0]
                    sv_done[(hp, "pvmm")] = mm_cnt[0]
                p.t(pvmm)

            mm_pv = mm_cnt[0]

            def lrec(v, hp=hp, tq=tq, mm_pv=mm_pv):
                v.wait_ge(sMM, mm_pv)
                for hh in range(2):
                    v.tensor_copy(out=rl_sb[0:1, hh * 512:(hh + 1) * 512],
                                  in_=P[2 + hh][64:65, :])
                v.reciprocal(rl_sb[0:1, :], rl_sb[0:1, :])
            p.v(lrec)
            m_lr = p.mark("v")

            def lbcast(t, hp=hp, tq=tq, m_lr=m_lr):
                t.wait_ge(p.auto["v"], m_lr)
                prev = (hp, 0) if tq == 1 else (hp - 1, 1)
                t.wait_ge(p.auto["v"], ev_ctx.get(prev, 0))
                for hh in range(2):
                    MM(t, P[hh][:, :], or_sb[:, :],
                       rl_sb[0:1, hh * 512:(hh + 1) * 512])
            p.t(lbcast)
            mm_lb = mm_cnt[0]

            def ctxev(v, hp=hp, tq=tq, mm_lb=mm_lb):
                idx = hp * 2 + tq
                v.wait_ge(sMM, mm_lb)
                for hh in range(2):
                    v.tensor_copy(out=rlb[:, hh * 512:(hh + 1) * 512],
                                  in_=P[hh][:, :])
                cs = slice(hp * TPC + tq * 512, hp * TPC + (tq + 1) * 512)
                v.tensor_mul(ctxf[0:64, cs], P[2][0:64, :], rlb[0:64, 0:512])
                if idx >= 1:
                    v.wait_ge(sCtxB, 16 * idx)
                v.tensor_mul(ctmp[:, :], P[3][0:64, :], rlb[0:64, 512:1024])
                ev_ctx[(hp, tq)] = p.cnt["v"]
                ev_ctx[hp] = p.cnt["v"]
            p.v(ctxev)

            def ctxb(y, hp=hp, tq=tq):
                y.wait_ge(p.auto["v"], ev_ctx[(hp, tq)])
                cs = slice(hp * TPC + tq * 512, hp * TPC + (tq + 1) * 512)
                y.dma_start(out=ctxf[64:128, cs],
                            in_=ctmp[:, :]).then_inc(sCtxB, 16)
            p.y(ctxb)

    phA.close()

    phB = ExitStack()
    r2bf = phB.enter_context(nc.sbuf_tensor("r2bf", [128, NDC * TPC], BF16))

    def bwait(v):
        v.wait_ge(sLx2, 16)
        v.wait_ge(sCtxB, 16 * 32)
        v.tensor_copy(out=lnscr[:, 0:1], in_=ctxf[:, 0:1])
    p.v(bwait)
    tq_bufs = (chb, chb_bf, cmx, statq, tmx, t4, trow, tb, tbj, idxq, scq,
               rscq)
    m_r2 = _tender_quant(p, PL, MM, mm_cnt, sMM, id_sb, or_sb, lnscr, tq_bufs,
                         ctxf, r2bf, 0)

    phC = ExitStack()
    wtb = phC.enter_context(nc.sbuf_tensor("wtb", [128, 2 * 2048], BF16))
    corr_s = phC.enter_context(nc.sbuf_tensor("corr_s", [4, 2048], F32))
    wt_mark = {}
    for dc in range(NDC):
        slot = dc % 2

        def owload(g, dc=dc, slot=slot):
            if dc < 2:
                g.wait_ge(p.auto["v"], m_r2)
            if dc >= 2:
                g.wait_ge(sMM, wt_mark.get(dc - 2, 0))
            g.dma_start(out=wtb[:, slot * 2048:(slot + 1) * 2048],
                        in_=owt_d[dc * 128:(dc + 1) * 128, :])
        p.g(owload)
        gm = p.mark("g")

        def cmm(t, dc=dc, slot=slot, gm=gm):
            p.wait_mark(t, "g", gm)
            t.wait_ge(p.auto["v"], m_r2)
            for og in range(4):
                MM(t, P[og][0:4, :], chb_bf[:, dc * 4:(dc + 1) * 4],
                   wtb[:, slot * 2048 + og * 512:slot * 2048 + (og + 1) * 512],
                   start=(dc == 0), stop=(dc == NDC - 1))
            wt_mark[dc] = mm_cnt[0]
        p.t(cmm)

    def cevac(s):
        s.wait_ge(sMM, wt_mark[NDC - 1])
        for og in range(4):
            s.activation(out=corr_s[:, og * 512:(og + 1) * 512],
                         in_=P[og][0:4, :], func=AF.Copy)
    p.s(cevac)
    m_ce = p.mark("s")

    def ctr(t, m_ce=m_ce):
        t.wait_ge(p.auto["s"], m_ce)
        last = None
        for oc in range(NDC):
            last = t.transpose(SA2[:, oc * 4:(oc + 1) * 4],
                               corr_s[0:4, oc * 128:(oc + 1) * 128],
                               id_sb[0:4, 0:4])
        last.then_inc(sMM, 1)
        mm_cnt[0] += 1
    p.t(ctr)
    mm_ct = mm_cnt[0]

    def cbias(v, mm_ct=mm_ct):
        v.wait_ge(sMM, mm_ct)
        for oc in range(NDC):
            v.tensor_scalar(out=cbo[:, oc * 4:(oc + 1) * 4],
                            in0=SA2[:, oc * 4:(oc + 1) * 4],
                            scalar1=obc_sb[:, oc:oc + 1], scalar2=None,
                            op0=ALU.add)
    p.v(cbias)
    m_cbo = p.mark("v")

    oq_tile = phC.enter_context(nc.sbuf_tensor("oq_tile", [128, 2 * 2048], BF16))
    op_mark = {}
    for oc in range(NDC):
        slot = oc % 2

        def oqload(y, oc=oc, slot=slot):
            if oc < 2:
                y.wait_ge(p.auto["v"], m_r2)
            if oc >= 2:
                y.wait_ge(sMM, op_mark.get((oc - 2, "mm"), 0))
            y.dma_start(
                out=oq_tile[:, slot * 2048:(slot + 1) * 2048],
                in_=oqit.rearrange("(c p) o -> p c o", p=128)
                [:, :, oc * 128:(oc + 1) * 128]).then_inc(sOQ[slot], 16)
        p.y(oqload)

        def omm(t, oc=oc, slot=slot):
            t.wait_ge(sOQ[slot], 16 * (oc // 2 + 1))
            t.wait_ge(p.auto["v"], m_r2)
            if oc >= 2:
                t.wait_ge(p.auto["v"], op_mark.get((oc - 2, "ev"), 0))
            for h in range(2):
                for dc in range(NDC):
                    MM(t, P[2 * (oc % 2) + h][:, :],
                       oq_tile[:, slot * 2048 + dc * 128:slot * 2048 + (dc + 1) * 128],
                       r2bf[:, dc * TPC + h * 512:dc * TPC + (h + 1) * 512],
                       start=(dc == 0), stop=(dc == NDC - 1))
            op_mark[(oc, "mm")] = mm_cnt[0]
        p.t(omm)

        def oev(v, oc=oc):
            v.wait_ge(sMM, op_mark[(oc, "mm")])
            v.wait_mark("v", m_cbo)
            for h in range(2):
                for c2 in range(2):
                    ch = 2 * h + c2
                    cs = slice(oc * TPC + ch * 256, oc * TPC + (ch + 1) * 256)
                    v.tensor_scalar(out=lnscr[:, 0:256],
                                    in0=P[2 * (oc % 2) + h][:, c2 * 256:(c2 + 1) * 256],
                                    scalar1=osc_sb[:, oc:oc + 1],
                                    scalar2=cbo[:, oc * 4 + ch:oc * 4 + ch + 1],
                                    op0=ALU.mult, op1=ALU.add)
                    v.tensor_add(x2sb[:, cs], x2sb[:, cs], lnscr[:, 0:256])
            op_mark[(oc, "ev")] = p.cnt["v"]
        p.v(oev)
    phC.close()
    phB.close()
    phCtx.close()

    xqf = E(nc.sbuf_tensor("xqf", [128, NDC * TPC], BF16))
    phD = ExitStack()
    ED = phD.enter_context
    x3n = ED(nc.sbuf_tensor("x3n", [128, NDC * TPC], F32))
    xtt = ED(nc.sbuf_tensor("xtt", [128, 2048], F32))
    sqq = ED(nc.sbuf_tensor("sqq", [128, 2048], F32))
    xqt = ED(nc.sbuf_tensor("xqt", [128, 2048], BF16))

    def dwait(v):
        v.wait_ge(p.auto["v"], op_mark[(NDC - 1, "ev")])
        v.tensor_copy(out=lnscr[:, 0:1], in_=x2sb[:, 0:1])
    p.v(dwait)
    m_ln3v = _ln_fm(p, PL, MM, mm_cnt, sMM, cst_sb, or_sb, rows, lnscr,
                    x2sb, x3n, 0)

    RMS_EPS = 1e-6
    AQ_EPS = 1e-5
    xq_mark = {}
    for tt in range(NTT):
        def etr(t, tt=tt):
            if tt == 0:
                t.wait_ge(p.auto["v"], m_ln3v)
            if tt >= 1:
                t.wait_ge(p.auto["s"], xq_mark[(tt - 1, "bev")])
            last = None
            for k in range(NDC):
                last = t.transpose(
                    P[k // 4][:, (k % 4) * 128:(k % 4) * 128 + 128],
                    x3n[:, k * TPC + tt * 128:k * TPC + (tt + 1) * 128],
                    id_sb[:, :])
            last.then_inc(sMM, 1)
            mm_cnt[0] += 1
        p.t(etr)
        mm_et = mm_cnt[0]

        def eev(s, tt=tt, mm_et=mm_et):
            s.wait_ge(sMM, mm_et)
            for b in range(4):
                s.activation(out=xtt[:, b * 512:(b + 1) * 512],
                             in_=P[b][:, :], func=AF.Copy)
            s.activation(out=sqq[:, :], in_=xtt[:, :], func=AF.Square)
        p.s(eev)
        m_eev = p.mark("s")

        def equant(v, tt=tt, m_eev=m_eev):
            c = lambda k: statq[:, k:k + 1]
            v.wait_mark("s", m_eev)
            v.reduce_sum(c(0), sqq[:, :], axis=AX.X)
            v.tensor_scalar_mul(c(0), c(0), 1.0 / D)
            v.reduce_max(c(1), xtt[:, :], axis=AX.X, apply_absolute_value=True)
            xq_mark[(tt, "st")] = p.cnt["v"]
        p.v(equant)
        m_est = p.mark("v")

        def erstd(s, tt=tt, m_est=m_est):
            c = lambda k: statq[:, k:k + 1]
            s.wait_mark("v", m_est)
            s.activation(out=c(2), in_=c(0), func=AF.Ln, bias=RMS_EPS)
            s.activation(out=c(2), in_=c(2), func=AF.Exp, scale=-0.5)
        p.s(erstd)
        m_ers = p.mark("s")

        def equant2(v, tt=tt, m_ers=m_ers):
            c = lambda k: statq[:, k:k + 1]
            v.wait_mark("s", m_ers)
            v.tensor_mul(c(3), c(1), c(2))
            v.tensor_scalar_max(c(3), c(3), AQ_EPS)
            v.reciprocal(c(4), c(3))
            v.tensor_scalar_mul(c(4), c(4), 127.0)
            v.tensor_scalar_mul(c(5), c(3), 1.0 / 127.0)
            v.tensor_mul(c(6), c(2), c(4))
            v.tensor_scalar_mul(sqq[:, :], xtt[:, :], c(6))
            v.tensor_scalar_add(xtt[:, :], sqq[:, :], LAD8_C)
            _emit_ladder(v, [(xtt[:, 0:1024], xqt[:, 0:1024]),
                             (xtt[:, 1024:2048], xqt[:, 1024:2048])], LAD8)
            v.scalar_tensor_tensor(out=sqq[:, :], in0=xtt[:, :], scalar=-1.0,
                                   in1=sqq[:, :], op0=ALU.mult, op1=ALU.add)
            v.tensor_scalar(out=xqt[:, :], in0=sqq[:, :], scalar1=0.5,
                            scalar2=c(5), op0=ALU.add, op1=ALU.mult)
            xq_mark[(tt, "q")] = p.cnt["v"]
        p.v(equant2)

        def ebtr(t, tt=tt):
            t.wait_ge(p.auto["v"], xq_mark[(tt, "q")])
            pbf0 = P[0][:, :].bitcast(BF16)
            pbf1 = P[1][:, :].bitcast(BF16)
            last = None
            for k in range(NDC):
                dstp = pbf0 if k < 8 else pbf1
                last = t.transpose(dstp[:, (k % 8) * 128:(k % 8) * 128 + 128],
                                   xqt[:, k * 128:(k + 1) * 128], id_bf[:, :])
            last.then_inc(sMM, 1)
            mm_cnt[0] += 1
        p.t(ebtr)
        mm_eb = mm_cnt[0]

        def ebev(s, tt=tt, mm_eb=mm_eb):
            s.wait_ge(sMM, mm_eb)
            pbf0 = P[0][:, :].bitcast(BF16)
            pbf1 = P[1][:, :].bitcast(BF16)
            for k in range(NDC):
                dstp = pbf0 if k < 8 else pbf1
                s.activation(out=xqf[:, k * TPC + tt * 128:k * TPC + (tt + 1) * 128],
                             in_=dstp[:, (k % 8) * 128:(k % 8) * 128 + 128],
                             func=AF.Copy)
            xq_mark[(tt, "bev")] = p.cnt["s"]
        p.s(ebev)
    phD.close()

    HOC = 64
    QH = 4
    TQH = 256
    hbf = E(nc.sbuf_tensor("hbf", [128, HOC * TQH], BF16))
    sHW = [p.sem("sHW0"), p.sem("sHW1")]
    sW3 = [p.sem("sW30"), p.sem("sW31")]
    h_mark = {}
    m_xq_done = xq_mark[(NTT - 1, "bev")]

    for qh in range(QH):
        phG = ExitStack()
        wqt = phG.enter_context(nc.sbuf_tensor(f"wqt{qh}", [128, 2 * 4096], BF16))
        t1s = phG.enter_context(nc.sbuf_tensor(f"t1s{qh}", [128, TQH], F32))

        for oc in range(HOC):
            oi = qh * HOC + oc
            slot = oi % 2

            def hwload(y, qh=qh, oc=oc, oi=oi, slot=slot):
                if oi == 0 and qh == 0:
                    y.wait_ge(p.auto["s"], m_xq_done)
                if oi >= 2:
                    y.wait_ge(p.auto["v"], h_mark[("hev", oi - 2)])
                for wi in range(2):
                    flat = wq_scr[wi].rearrange("(c p) o -> p c o", p=128)
                    y.dma_start(
                        out=wqt[:, slot * 4096 + wi * 2048:
                                slot * 4096 + (wi + 1) * 2048],
                        in_=flat[:, :, oc * 128:(oc + 1) * 128]
                    ).then_inc(sHW[slot], 16)
            p.y(hwload)

            def hmm(t, qh=qh, oc=oc, oi=oi, slot=slot):
                t.wait_ge(sHW[slot], 32 * (oi // 2 + 1))
                if oi == 0:
                    t.wait_ge(p.auto["s"], m_xq_done)
                if oi >= 2:
                    t.wait_ge(p.auto["v"], h_mark[("hev", oi - 2)])
                if oc < 2 and qh > 0:
                    t.wait_ge(sMM, h_mark[("btrlast", qh - 1)])
                for wi in range(2):
                    for ic in range(NDC):
                        MM(t, P[2 * (oc % 2) + wi][:, 0:TQH],
                           wqt[:, slot * 4096 + wi * 2048 + ic * 128:
                               slot * 4096 + wi * 2048 + (ic + 1) * 128],
                           xqf[:, ic * TPC + qh * TQH:ic * TPC + (qh + 1) * TQH],
                           start=(ic == 0), stop=(ic == NDC - 1))
                h_mark[("mm", oi)] = mm_cnt[0]
            p.t(hmm)

            def hsilu(s, oc=oc, oi=oi):
                s.wait_ge(sMM, h_mark[("mm", oi)])
                if oi > 0:
                    s.wait_mark("v", h_mark[("hev", oi - 1)])
                s.activation(out=t1s[:, :], in_=P[2 * (oc % 2)][:, 0:TQH],
                             func=AF.Silu, scale=wsc[:, 4:5])
            p.s(hsilu)
            m_hs = p.mark("s")

            def hev(v, oc=oc, oi=oi, m_hs=m_hs):
                v.wait_mark("s", m_hs)
                v.scalar_tensor_tensor(
                    out=hbf[:, oc * TQH:(oc + 1) * TQH],
                    in0=P[2 * (oc % 2) + 1][:, 0:TQH], scalar=wsc[:, 5:6],
                    in1=t1s[:, :], op0=ALU.mult, op1=ALU.mult)
                h_mark[("hev", oi)] = p.cnt["v"]
            p.v(hev)
        phG.close()
        mG_v = p.mark("v")

        phQ = ExitStack()
        htt = phQ.enter_context(nc.sbuf_tensor(f"htt{qh}", [128, 8192], BF16))
        hq2 = phQ.enter_context(nc.sbuf_tensor(f"hq2{qh}", [128, 4096], F32))
        msc = phQ.enter_context(nc.sbuf_tensor(f"msc{qh}", [128, 4096], F32))

        for tl in range(2):
            for rnd in range(2):
                def hftr(t, qh=qh, tl=tl, rnd=rnd):
                    if (tl, rnd) == (0, 0):
                        t.wait_ge(p.auto["v"], mG_v)
                    else:
                        t.wait_ge(p.auto["s"], h_mark[("fev", qh, tl, rnd - 1)
                                                      if rnd == 1 else
                                                      ("bev", qh, tl - 1, 3)])
                    last = None
                    for k in range(32):
                        hc = rnd * 32 + k
                        pb = P[k // 8][:, :].bitcast(BF16)
                        last = t.transpose(
                            pb[:, (k % 8) * 128:(k % 8) * 128 + 128],
                            hbf[:, hc * TQH + tl * 128:hc * TQH + tl * 128 + 128],
                            id_bf[:, :])
                    last.then_inc(sMM, 1)
                    mm_cnt[0] += 1
                    h_mark[("ftr", qh, tl, rnd)] = mm_cnt[0]
                p.t(hftr)

                def hfev(s, qh=qh, tl=tl, rnd=rnd):
                    s.wait_ge(sMM, h_mark[("ftr", qh, tl, rnd)])
                    for b in range(4):
                        pb = P[b][:, :].bitcast(BF16)
                        s.activation(out=htt[:, rnd * 4096 + b * 1024:
                                             rnd * 4096 + (b + 1) * 1024],
                                     in_=pb[:, :], func=AF.Copy)
                    h_mark[("fev", qh, tl, rnd)] = p.cnt["s"]
                p.s(hfev)

            def hsq1(s, qh=qh, tl=tl):
                s.wait_ge(p.auto["s"], 0)
                s.activation(out=hq2[:, :], in_=htt[:, 0:4096], func=AF.Square)
                s.activation(out=msc[:, :], in_=htt[:, 4096:8192], func=AF.Square)
            p.s(hsq1)
            m_hq1 = p.mark("s")

            def hqstat(v, qh=qh, tl=tl, m_hq1=m_hq1):
                c = lambda k: statq[:, k:k + 1]
                v.wait_mark("s", m_hq1)
                v.reduce_sum(c(0), hq2[:, :], axis=AX.X)
                v.reduce_sum(c(1), msc[:, :], axis=AX.X)
                v.tensor_add(c(0), c(0), c(1))
                v.tensor_scalar_mul(c(0), c(0), 1.0 / (4 * D))
                v.reduce_max(c(1), htt[:, 0:4096], axis=AX.X,
                             apply_absolute_value=True)
                v.reduce_max(c(2), htt[:, 4096:8192], axis=AX.X,
                             apply_absolute_value=True)
                v.tensor_max(c(1), c(1), c(2))
            p.v(hqstat)
            m_hst = p.mark("v")

            def hrstd(s, qh=qh, tl=tl, m_hst=m_hst):
                c = lambda k: statq[:, k:k + 1]
                s.wait_mark("v", m_hst)
                s.activation(out=c(2), in_=c(0), func=AF.Ln, bias=RMS_EPS)
                s.activation(out=c(2), in_=c(2), func=AF.Exp, scale=-0.5)
            p.s(hrstd)
            m_hrs = p.mark("s")

            def hqops(v, qh=qh, tl=tl, m_hrs=m_hrs):
                c = lambda k: statq[:, k:k + 1]
                v.wait_mark("s", m_hrs)
                v.tensor_mul(c(3), c(1), c(2))
                v.tensor_scalar_max(c(3), c(3), AQ_EPS)
                v.reciprocal(c(4), c(3))
                v.tensor_scalar_mul(c(4), c(4), 127.0)
                v.tensor_scalar_mul(c(5), c(3), 1.0 / 127.0)
                v.tensor_mul(c(6), c(2), c(4))
                for half in range(2):
                    h0 = half * 4096
                    hs_ = slice(h0, h0 + 4096)
                    v.tensor_scalar_mul(hq2[:, :], htt[:, hs_], c(6))
                    v.tensor_scalar_add(msc[:, :], hq2[:, :], LAD8_C)
                    _emit_ladder(v, [(msc[:, 0:2048], htt[:, h0:h0 + 2048]),
                                     (msc[:, 2048:4096],
                                      htt[:, h0 + 2048:h0 + 4096])], LAD8)
                    v.scalar_tensor_tensor(out=hq2[:, :], in0=msc[:, :],
                                           scalar=-1.0, in1=hq2[:, :],
                                           op0=ALU.mult, op1=ALU.add)
                    v.tensor_scalar(out=htt[:, hs_], in0=hq2[:, :], scalar1=0.5,
                                    scalar2=c(5), op0=ALU.add, op1=ALU.mult)
                h_mark[("q", qh, tl)] = p.cnt["v"]
            p.v(hqops)

            for rnd in range(4):
                def hbtr(t, qh=qh, tl=tl, rnd=rnd):
                    if rnd == 0:
                        t.wait_ge(p.auto["v"], h_mark[("q", qh, tl)])
                    else:
                        t.wait_ge(p.auto["s"], h_mark[("bev", qh, tl, rnd - 1)])
                    last = None
                    for k in range(16):
                        hc = rnd * 16 + k
                        pb = P[k // 8][:, :].bitcast(BF16)
                        last = t.transpose(
                            pb[:, (k % 8) * 128:(k % 8) * 128 + 128],
                            htt[:, hc * 128:(hc + 1) * 128], id_bf[:, :])
                    last.then_inc(sMM, 1)
                    mm_cnt[0] += 1
                    h_mark[("btr", qh, tl, rnd)] = mm_cnt[0]
                p.t(hbtr)

                def hbev(s, qh=qh, tl=tl, rnd=rnd):
                    s.wait_ge(sMM, h_mark[("btr", qh, tl, rnd)])
                    for k in range(16):
                        hc = rnd * 16 + k
                        pb = P[k // 8][:, :].bitcast(BF16)
                        s.activation(
                            out=hbf[:, hc * TQH + tl * 128:hc * TQH + tl * 128 + 128],
                            in_=pb[:, (k % 8) * 128:(k % 8) * 128 + 128],
                            func=AF.Copy)
                    h_mark[("bev", qh, tl, rnd)] = p.cnt["s"]
                p.s(hbev)
        h_mark[("btrlast", qh)] = h_mark[("btr", qh, 1, 3)]
        h_mark[("bevlast", qh)] = h_mark[("bev", qh, 1, 3)]
        phQ.close()

        phI = ExitStack()
        w3tl = phI.enter_context(nc.sbuf_tensor(f"w3tl{qh}", [128, 2 * 8192], BF16))
        for ec in range(NDC):
            ei = qh * NDC + ec
            slot = ei % 2

            def w3load(y, qh=qh, ec=ec, ei=ei, slot=slot):
                if ec < 2:
                    y.wait_ge(sMM, h_mark[("mm", qh * HOC + HOC - 1)])
                    y.wait_ge(sMM, h_mark[("btrlast", qh)])
                    y.wait_ge(p.auto["s"], h_mark[("bevlast", qh)])
                if ei >= 2:
                    y.wait_ge(sMM, h_mark.get(("w3mm", ei - 2), 0))
                flat = wq_scr[2].rearrange("(c p) o -> p c o", p=128)
                y.dma_start(out=w3tl[:, slot * 8192:(slot + 1) * 8192],
                            in_=flat[:, :, ec * 128:(ec + 1) * 128]
                            ).then_inc(sW3[slot], 16)
            p.y(w3load)

            def w3mm(t, qh=qh, ec=ec, ei=ei, slot=slot):
                t.wait_ge(sW3[slot], 16 * (ei // 2 + 1))
                t.wait_ge(p.auto["s"], h_mark[("bevlast", qh)])
                if ei >= 2:
                    t.wait_ge(p.auto["v"], h_mark.get(("x4", ei - 2), 0))
                bank = SA2 if ec % 2 == 0 else SB2
                for hc in range(64):
                    MM(t, bank[:, 0:TQH],
                       w3tl[:, slot * 8192 + hc * 128:slot * 8192 + (hc + 1) * 128],
                       hbf[:, hc * TQH:(hc + 1) * TQH],
                       start=(hc == 0), stop=(hc == 63))
                h_mark[("w3mm", ei)] = mm_cnt[0]
            p.t(w3mm)

            def x4ev(v, qh=qh, ec=ec, ei=ei):
                v.wait_ge(sMM, h_mark[("w3mm", ei)])
                if qh > 0:
                    v.wait_ge(sOut[0], 128 * qh)
                    v.wait_ge(sOut[1], 128 * qh)
                bank = SA2 if ec % 2 == 0 else SB2
                cs = slice(ec * TPC + qh * TQH, ec * TPC + (qh + 1) * TQH)
                v.scalar_tensor_tensor(out=x2sb[:, cs], in0=bank[:, 0:TQH],
                                       scalar=wsc[:, 6:7], in1=x2sb[:, cs],
                                       op0=ALU.mult, op1=ALU.add)
                h_mark[("x4", ei)] = p.cnt["v"]
            p.v(x4ev)

            def x4out(y, qh=qh, ec=ec, ei=ei):
                y.wait_ge(p.auto["v"], h_mark[("x4", ei)])
                y.dma_start(out=o_x4[ec * 128:(ec + 1) * 128,
                                     qh * TQH:(qh + 1) * TQH],
                            in_=x2sb[:, ec * TPC + qh * TQH:
                                     ec * TPC + (qh + 1) * TQH]
                            ).then_inc(sOut[ec % 2], 16)
            p.y(x4out)
        phI.close()

    def finish(y):
        y.wait_ge(sOut[0], 16 * 32)
        y.wait_ge(sOut[1], 16 * 32)
    p.y(finish)

    st.close()
    return nc


_NC_CACHE = {}


def _get_nc(name):
    if name not in _NC_CACHE:
        _NC_CACHE[name] = globals()["_build_" + name]()
    return _NC_CACHE[name]


def _run(name, in_maps, cores):
    import time
    t0 = time.time()
    r = run_bass_kernel_spmd(_get_nc(name), in_maps, core_ids=cores).results
    if os.environ.get("KTIME"):
        print(f"[ktime] launch {name}: {time.time() - t0:.1f}s",
              file=sys.stderr, flush=True)
    return r


def _cols(v):
    v = np.asarray(v)
    return np.ascontiguousarray(v.reshape(-1, 128).T)


def _t(a):
    return np.ascontiguousarray(np.asarray(a).T)


def kernel(**inputs):
    inputs.get("L0_ffn_gate_w")
    x = np.asarray(inputs["L0_attn_x"])
    xf = np.ascontiguousarray(x.reshape(8192, D))
    ident = np.eye(128, dtype=np.float32)
    onesrow = np.ones((1, 128), np.float32)
    oab = np.zeros((128, 65), np.float32)
    oab[0:64, 0] = 1.0
    oab[64:128, 64] = 1.0
    or64 = np.ones((65, 128), np.float32)
    cA = _make_consts(CONSTS_A)
    cC = _make_consts(CONSTS_C)
    cD = _make_consts(CONSTS_D)
    cores = list(range(NCORES))

    qw = np.asarray(inputs["L1_attn_q_weight"])
    kw = np.asarray(inputs["L1_attn_k_weight"])
    vw = np.asarray(inputs["L1_attn_v_weight"])
    ow = np.asarray(inputs["L1_attn_o_weight"])
    w1t = _t(inputs["L1_ffn_w1"])
    w2t = _t(inputs["L1_ffn_w2"])
    w3t = _t(inputs["L1_ffn_w3"])
    qwt, kwt, vwt, owt = _t(qw), _t(kw), _t(vw), _t(ow)
    cW = _make_consts(CONSTS_W)
    inW = []
    for i in cores:
        sl = slice(256 * i, 256 * (i + 1))
        s2 = slice(TPC * i, TPC * (i + 1))
        inW.append(dict(
            wr4=np.ascontiguousarray(
                np.concatenate([qw[sl], kw[sl], vw[sl], ow[sl]], axis=0)),
            wt4=np.ascontiguousarray(
                np.concatenate([qwt[:, sl], kwt[:, sl], vwt[:, sl],
                                owt[:, sl]], axis=1)),
            w1s=np.ascontiguousarray(w1t[:, s2]),
            w2s=np.ascontiguousarray(w2t[:, s2]),
            w3s=np.ascontiguousarray(w3t[s2, :]),
            constv=cW))
    rW = _run("W", inW, cores)

    qint = [np.concatenate([rW[i]["o_q4"][256 * w:256 * (w + 1)]
                            for i in cores], axis=0) for w in range(4)]
    qitT = [_t(qint[w]) for w in range(4)]
    ws64 = np.zeros((128, 64), np.float32)
    for w in range(4):
        for i in cores:
            ws64[:, w * 16 + 2 * i:w * 16 + 2 * i + 2] = \
                np.asarray(rW[i]["o_s4"])[:, 2 * w:2 * w + 2]
    wtbf = [np.concatenate([np.asarray(rW[i]["o_wbf"])[:, 256 * w:256 * (w + 1)]
                            for i in cores], axis=1) for w in range(4)]
    msums = np.concatenate([np.asarray(rW[i]["o_msum"]) for i in cores],
                           axis=0)

    bdtc = _cols(inputs["L0_attn_bdt"])
    inA = [dict(xr=np.ascontiguousarray(xf[i * TPC:(i + 1) * TPC]),
                wx=np.asarray(inputs["L0_attn_Wx"]),
                wdt=np.asarray(inputs["L0_attn_Wdt"]),
                bdtc=bdtc, ident=ident, constv=cA) for i in cores]
    rA = _run("A", inA, cores)

    dt_g = [np.concatenate([rA[4 * b + j]["o_dt"] for j in range(4)], axis=1)
            for b in range(2)]
    dtu_g = [np.concatenate([rA[4 * b + j]["o_dtu"] for j in range(4)], axis=1)
             for b in range(2)]
    b_nt = [np.concatenate([rA[4 * b + j]["o_bc"][0:16] for j in range(4)],
                           axis=1) for b in range(2)]
    c_nt = [np.concatenate([rA[4 * b + j]["o_bc"][16:32] for j in range(4)],
                           axis=1) for b in range(2)]

    alog = np.asarray(inputs["L0_attn_A_log"])
    masks = _mk_masks()
    inB = []
    for i in cores:
        b, q = divmod(i, 4)
        rs = slice(512 * q, 512 * (q + 1))
        inB.append(dict(
            dt_in=np.ascontiguousarray(dt_g[b][rs]),
            dtu_in=np.ascontiguousarray(dtu_g[b][rs]),
            b_in=np.ascontiguousarray(b_nt[b]),
            c_in=np.ascontiguousarray(c_nt[b]),
            alogl=_alog_lanes(alog[rs]), masks=masks))
    rB = _run("B", inB, cores)
    y_g = [np.concatenate([rB[4 * b + q]["o_y"] for q in range(4)], axis=0)
           for b in range(2)]

    p1t = _t(inputs["L0_ffn_proj1"])
    p2t = _t(inputs["L0_ffn_proj2"])
    inC1 = []
    for i in cores:
        b, q = divmod(i, 4)
        inC1.append(dict(
            xr=np.ascontiguousarray(xf[i * TPC:(i + 1) * TPC]),
            yin=np.ascontiguousarray(y_g[b][:, TPC * q:TPC * (q + 1)]),
            dpc=_cols(inputs["L0_attn_Dp"]),
            p1t=p1t, b1c=_cols(inputs["L0_ffn_proj1_bias"]),
            p2t=p2t, b2c=_cols(inputs["L0_ffn_proj2_bias"]),
            msums=msums,
            w1ts=np.ascontiguousarray(w1t[:, TPC * i:TPC * (i + 1)]),
            w2ts=np.ascontiguousarray(w2t[:, TPC * i:TPC * (i + 1)]),
            w3ts=np.ascontiguousarray(w3t[TPC * i:TPC * (i + 1), :]),
            ident=ident, onesrow=onesrow, constv=cC))
    rC1 = _run("C1", inC1, cores)

    w1qt = np.concatenate([np.asarray(rC1[i]["o_w1q"]) for i in cores], axis=1)
    w2qt = np.concatenate([np.asarray(rC1[i]["o_w2q"]) for i in cores], axis=1)
    w3qt = np.concatenate([np.asarray(rC1[i]["o_w3q"]) for i in cores], axis=0)
    wmb = np.ascontiguousarray(
        np.broadcast_to(np.asarray(rC1[0]["o_wm"]), (128, 4)))

    inC2 = []
    for i in cores:
        inC2.append(dict(
            x2in=np.ascontiguousarray(rC1[i]["o_x2"]),
            qit0=qitT[0], qit1=qitT[1], qit2=qitT[2],
            wtbf0=wtbf[0], wtbf1=wtbf[1], wtbf2=wtbf[2],
            winsc=ws64,
            qbc=_cols(inputs["L1_attn_q_bias"]),
            kbc=_cols(inputs["L1_attn_k_bias"]),
            vbc=_cols(inputs["L1_attn_v_bias"]),
            ident=ident, onesrow=onesrow, constv=cC))
    rC2 = _run("C2", inC2, cores)

    oqit = qitT[3]
    osc = np.ascontiguousarray(ws64[:, 48:64])
    inD = []
    for i in cores:
        b, q = divmod(i, 4)
        kf = np.concatenate([rC2[4 * b + j]["o_k"] for j in range(4)], axis=1)
        vt = np.concatenate([rC2[4 * b + j]["o_v"] for j in range(4)], axis=0)
        inD.append(dict(
            x2f=np.ascontiguousarray(rC1[i]["o_x2"]),
            qf=np.ascontiguousarray(rC2[i]["o_q"]),
            kf=np.ascontiguousarray(kf), vt=np.ascontiguousarray(vt),
            oqit=oqit, osc=osc, owbf=wtbf[3],
            obc=_cols(inputs["L1_attn_o_bias"]),
            w1qt=w1qt, w2qt=w2qt, w3qt=w3qt, wmb=wmb,
            ident=ident, onesrow=onesrow, oab=oab, or64=or64, constv=cD))
    rD = _run("D", inD, cores)

    out = np.concatenate([np.asarray(rD[i]["o_x4"]).T for i in cores], axis=0)
    return out.reshape(2, T, D).astype(np.float32)


# version 06
# speedup vs baseline: 1.5311x; 1.0235x over previous
import os
import sys
import numpy as np
import ml_dtypes
from contextlib import ExitStack

import concourse.bass as bass
import concourse.mybir as mybir
from concourse.bass_utils import run_bass_kernel_spmd

F32 = mybir.dt.float32
BF16 = mybir.dt.bfloat16
I32 = mybir.dt.int32
AF = mybir.ActivationFunctionType
ALU = mybir.AluOpType
AX = mybir.AxisListType
BF = ml_dtypes.bfloat16

NCORES = 8
D = 2048
TPC = 1024
T = 4096
NST = 16
DTR = 64
NDC = D // 128
NTT = TPC // 128
LN_EPS = 1e-5


def _make_consts(vals):
    a = np.zeros((128, max(1, len(vals))), np.float32)
    for i, v in enumerate(vals):
        a[:, i] = v
    return a


LAD4 = (8.0, 4.0, 2.0, 1.0)
LAD4_C = 8.5
LAD8 = (128.0, 64.0, 32.0, 16.0, 8.0, 4.0, 2.0, 1.0)
LAD8_C = 128.5


def _emit_ladder(v, chunks, powers):
    assert len(chunks) >= 2
    for pw in powers:
        for m, g in chunks:
            v.tensor_scalar(out=g, in0=m, scalar1=pw, scalar2=-pw,
                            op0=ALU.is_ge, op1=ALU.mult)
        for m, g in chunks:
            v.scalar_tensor_tensor(out=m, in0=g, scalar=1.0, in1=m,
                                   op0=ALU.mult, op1=ALU.add)


class _Serial:

    def __init__(self, eng, sem, prog, key):
        self._eng, self._sem, self._prog, self._key = eng, sem, prog, key

    def wait_ge(self, sem, val):
        if val > 0:
            self._eng.wait_ge(sem, val)

    def wait_mark(self, key, mark):
        self._prog.wait_mark(self._eng, key, mark)

    def __getattr__(self, name):
        f = getattr(self._eng, name)
        if not callable(f):
            return f

        def wrapped(*a, **k):
            is_dma = name in ("dma_start", "dma_start_transpose")
            key = self._key
            dkey = key + "d"
            cnt = self._prog.cnt[key]
            if cnt > 0:
                self._eng.wait_ge(self._sem, cnt)
            if dkey in self._prog.auto and self._prog.cnt[dkey] > 0:
                self._eng.wait_ge(self._prog.auto[dkey], self._prog.cnt[dkey])
            inst = f(*a, **k)
            if is_dma and dkey in self._prog.auto:
                inst.then_inc(self._prog.auto[dkey], 16)
                self._prog.cnt[dkey] += 16
            else:
                inst.then_inc(self._sem, 16 if is_dma else 1)
                self._prog.cnt[key] += 16 if is_dma else 1
            return inst

        return wrapped


class Prog:

    AUTOKEYS = ("v", "s", "g", "gd")

    def __init__(self, nc, stack):
        self.nc = nc
        self.block = stack.enter_context(nc.Block())
        self.auto = {k: stack.enter_context(nc.semaphore(f"auto_{k}"))
                     for k in self.AUTOKEYS}
        self.cnt = {k: 0 for k in self.AUTOKEYS}
        self._nsem = 0
        self._stack = stack

    def sem(self, name=None):
        self._nsem += 1
        return self._stack.enter_context(
            self.nc.semaphore(name or f"s{self._nsem}"))

    def v(self, fn):
        self.block.vector(lambda e: fn(_Serial(e, self.auto["v"], self, "v")))

    def s(self, fn):
        self.block.scalar(lambda e: fn(_Serial(e, self.auto["s"], self, "s")))

    def g(self, fn):
        self.block.gpsimd(lambda e: fn(_Serial(e, self.auto["g"], self, "g")))

    def t(self, fn):
        self.block.tensor(fn)

    def y(self, fn):
        self.block.sync(fn)

    def mark(self, key):
        if key == "g":
            return (self.cnt["g"], self.cnt["gd"])
        return self.cnt[key]

    def wait_mark(self, eng, key, mark):
        if key == "g":
            a, b = mark if isinstance(mark, tuple) else (mark, 0)
            if a > 0:
                eng.wait_ge(self.auto["g"], a)
            if b > 0:
                eng.wait_ge(self.auto["gd"], b)
        elif mark > 0:
            eng.wait_ge(self.auto[key], mark)


CONSTS_A = [LN_EPS, -0.5, -1.0, 1.0]


def _build_A():
    nc = bass.Bass("TRN2", target_bir_lowering=False)
    xr = nc.dram_tensor("xr", [TPC, D], F32, kind="ExternalInput").ap()
    wx = nc.dram_tensor("wx", [D, 96], F32, kind="ExternalInput").ap()
    wdt = nc.dram_tensor("wdt", [DTR, D], F32, kind="ExternalInput").ap()
    bdtc = nc.dram_tensor("bdtc", [128, NDC], F32, kind="ExternalInput").ap()
    ident = nc.dram_tensor("ident", [128, 128], F32, kind="ExternalInput").ap()
    constv = nc.dram_tensor("constv", [128, len(CONSTS_A)], F32,
                            kind="ExternalInput").ap()
    o_dt = nc.dram_tensor("o_dt", [D, TPC], F32, kind="ExternalOutput").ap()
    o_dtu = nc.dram_tensor("o_dtu", [D, TPC], F32, kind="ExternalOutput").ap()
    o_bc = nc.dram_tensor("o_bc", [32, TPC], F32, kind="ExternalOutput").ap()

    st = ExitStack()
    E = st.enter_context
    xtm = E(nc.sbuf_tensor("xtm", [128, 2 * D], F32))
    sqt = E(nc.sbuf_tensor("sqt", [128, 2 * D], F32))
    x0t = E(nc.sbuf_tensor("x0t", [128, 2 * D], F32))
    x0n = E(nc.sbuf_tensor("x0n", [128, NDC * TPC], F32))
    x0nb = E(nc.sbuf_tensor("x0nb", [128, NDC * TPC], BF16))
    statc = E(nc.sbuf_tensor("statc", [128, NTT * 8], F32))
    wx_sb = E(nc.sbuf_tensor("wx_sb", [128, NDC * 96], BF16))
    wdt_sb = E(nc.sbuf_tensor("wdt_sb", [DTR, D], BF16))
    bdt_sb = E(nc.sbuf_tensor("bdt_sb", [128, NDC], F32))
    id_sb = E(nc.sbuf_tensor("id_sb", [128, 128], F32))
    cst_sb = E(nc.sbuf_tensor("cst_sb", [128, len(CONSTS_A)], F32))
    delta_bf = E(nc.sbuf_tensor("delta_bf", [DTR, TPC], BF16))
    bc_sb = E(nc.sbuf_tensor("bc_sb", [32, TPC], F32))
    absz = E(nc.sbuf_tensor("absz", [128, TPC], F32))
    l1p = E(nc.sbuf_tensor("l1p", [128, TPC], F32))
    relu = E(nc.sbuf_tensor("relu", [128, TPC], F32))
    dt_sb = E(nc.sbuf_tensor("dt_sb", [128, 2 * TPC], F32))
    dtu_sb = E(nc.sbuf_tensor("dtu_sb", [128, 2 * TPC], F32))

    P = [E(nc.psum_tensor(f"P{i}", [128, 512], F32)) for i in range(8)]

    for i, v in enumerate(CONSTS_A):
        nc.const_aps.register(F32, v, cst_sb[:, i:i + 1])

    p = Prog(nc, st)
    sLc = p.sem("sLc")
    sLx = [p.sem("sLx0"), p.sem("sLx1")]
    sLw = p.sem("sLw")
    sTr = p.sem("sTr")
    sXz = p.sem("sXz")
    sDtM = p.sem("sDtM")
    sOutD = [p.sem("sOutD0"), p.sem("sOutD1")]
    sOutB = p.sem("sOutB")

    def consts(y):
        y.dma_start(out=bdt_sb[:, :], in_=bdtc).then_inc(sLc, 16)
        y.dma_start(out=id_sb[:, :], in_=ident).then_inc(sLc, 16)
        y.dma_start(out=cst_sb[:, :], in_=constv).then_inc(sLc, 16)
    p.y(consts)

    def wloads(g):
        g.dma_start(out=wx_sb[:, :], in_=wx.rearrange("(c p) n -> p c n", p=128))
        g.dma_start(out=wdt_sb[:, :], in_=wdt)
    p.g(wloads)
    m_wload = p.mark("g")

    aff_mark = [0] * NTT
    red_mark = [0] * NTT
    evac_mark = [0] * NTT

    for tt in range(NTT):
        sl = slice((tt % 2) * D, (tt % 2) * D + D)

        def load(y, tt=tt, sl=sl):
            if tt >= 2:
                y.wait_ge(p.auto["v"], aff_mark[tt - 2])
            y.dma_start(out=xtm[:, sl],
                        in_=xr[tt * 128:(tt + 1) * 128, :]).then_inc(sLx[tt % 2], 16)
        p.y(load)

        def square(s, tt=tt, sl=sl):
            s.wait_ge(sLc, 48)
            s.wait_ge(sLx[tt % 2], 16 * (tt // 2 + 1))
            if tt >= 2:
                s.wait_mark("v", red_mark[tt - 2])
            s.activation(out=sqt[:, sl], in_=xtm[:, sl], func=AF.Square)
        p.s(square)
        m_sq = p.mark("s")

        def stats(v, tt=tt, sl=sl):
            c = lambda k: statc[:, tt * 8 + k: tt * 8 + k + 1]
            v.wait_mark("s", m_sq)
            v.reduce_sum(c(0), xtm[:, sl], axis=AX.X)
            v.reduce_sum(c(1), sqt[:, sl], axis=AX.X)
            red_mark[tt] = p.cnt["v"]
            v.tensor_scalar_mul(c(2), c(0), 1.0 / D)
            v.tensor_scalar_mul(c(3), c(1), 1.0 / D)
            v.tensor_mul(c(7), c(2), c(2))
            v.tensor_sub(c(4), c(3), c(7))
        p.v(stats)
        red_mark[tt] = red_mark[tt]
        m_var = p.mark("v")

        def rstd(s, tt=tt):
            c = lambda k: statc[:, tt * 8 + k: tt * 8 + k + 1]
            s.wait_mark("v", m_var)
            s.activation(out=c(5), in_=c(4), func=AF.Ln, bias=LN_EPS)
            s.activation(out=c(6), in_=c(5), func=AF.Exp, scale=-0.5)
        p.s(rstd)
        m_rstd = p.mark("s")

        def affine(v, tt=tt, sl=sl):
            c = lambda k: statc[:, tt * 8 + k: tt * 8 + k + 1]
            v.wait_mark("s", m_rstd)
            if tt >= 2:
                v.wait_ge(sTr, tt - 1)
            v.tensor_scalar(out=x0t[:, sl], in0=xtm[:, sl], scalar1=c(2),
                            scalar2=c(6), op0=ALU.subtract, op1=ALU.mult)
            aff_mark[tt] = p.cnt["v"]
        p.v(affine)

        def transposes(t, tt=tt):
            t.wait_ge(p.auto["v"], aff_mark[tt])
            t.wait_ge(sLc, 48)
            if tt >= 2:
                t.wait_ge(p.auto["s"], evac_mark[tt - 2])
            last = None
            for k in range(NDC):
                bank = 4 * (tt % 2) + k // 4
                last = t.transpose(
                    P[bank][:, (k % 4) * 128:(k % 4) * 128 + 128],
                    x0t[:, (tt % 2) * D + k * 128:(tt % 2) * D + (k + 1) * 128],
                    id_sb[:, :])
            last.then_inc(sTr, 1)
        p.t(transposes)

        def evacs(s, tt=tt):
            s.wait_ge(sTr, tt + 1)
            for k in range(NDC):
                bank = 4 * (tt % 2) + k // 4
                s.activation(
                    out=x0n[:, k * TPC + tt * 128: k * TPC + (tt + 1) * 128],
                    in_=P[bank][:, (k % 4) * 128:(k % 4) * 128 + 128],
                    func=AF.Copy)
            evac_mark[tt] = p.cnt["s"]
        p.s(evacs)

    def casts(s):
        for dc in range(NDC):
            s.activation(out=x0nb[:, dc * TPC:(dc + 1) * TPC],
                         in_=x0n[:, dc * TPC:(dc + 1) * TPC], func=AF.Copy)
    p.s(casts)
    m_cast = p.mark("s")

    def xz_mm(t):
        t.wait_ge(p.auto["s"], m_cast)
        p.wait_mark(t, "g", m_wload)
        for dc in range(NDC):
            for h in range(2):
                mm = t.matmul(
                    P[h][0:96, :], wx_sb[:, dc * 96:(dc + 1) * 96],
                    x0nb[:, dc * TPC + h * 512: dc * TPC + (h + 1) * 512],
                    start=(dc == 0), stop=(dc == NDC - 1))
        mm.then_inc(sXz, 1)
    p.t(xz_mm)

    def delta_evac(s):
        s.wait_ge(sXz, 1)
        for h in range(2):
            s.activation(out=delta_bf[:, h * 512:(h + 1) * 512],
                         in_=P[h][0:DTR, :], func=AF.Copy)
        s.activation(out=bc_sb[:, 0:512], in_=P[0][64:96, :], func=AF.Copy)
        s.activation(out=bc_sb[:, 512:1024], in_=P[1][64:96, :], func=AF.Copy)
    p.s(delta_evac)
    m_dbf = p.mark("s")

    abs_mark = [[0, 0] for _ in range(NDC)]
    relu_mark = [[0, 0] for _ in range(NDC)]
    dtl_mark = [[0, 0] for _ in range(NDC)]
    dtu_mark = [0] * NDC

    for dc in range(NDC):
        def dt_mm(t, dc=dc):
            if dc == 0:
                t.wait_ge(p.auto["s"], m_dbf)
            for h in range(2):
                if dc > 0:
                    t.wait_ge(p.auto["s"], abs_mark[dc - 1][h])
                    t.wait_ge(p.auto["v"], relu_mark[dc - 1][h])
                t.matmul(P[2 + h][:, :], wdt_sb[:, dc * 128:(dc + 1) * 128],
                         delta_bf[:, h * 512:(h + 1) * 512],
                         start=True, stop=True).then_inc(sDtM, 1)
        p.t(dt_mm)

        def softplus(s, dc=dc):
            for h in range(2):
                s.wait_ge(sDtM, 2 * dc + h + 1)
                if dc > 0:
                    s.wait_mark("v", dtl_mark[dc - 1][h])
                hs = slice(h * 512, (h + 1) * 512)
                s.activation(out=absz[:, hs], in_=P[2 + h][:, :], func=AF.Abs,
                             bias=bdt_sb[:, dc:dc + 1])
                abs_mark[dc][h] = p.cnt["s"]
                s.activation(out=absz[:, hs], in_=absz[:, hs], func=AF.Exp,
                             scale=-1.0)
                s.activation(out=l1p[:, hs], in_=absz[:, hs], func=AF.Ln,
                             bias=1.0)
        p.s(softplus)
        m_sp = [abs_mark[dc][0] + 2, abs_mark[dc][1] + 2]

        def dt_ep(v, dc=dc, m_sp=m_sp):
            dsl0 = (dc % 2) * TPC
            for h in range(2):
                hs = slice(h * 512, (h + 1) * 512)
                v.wait_ge(sDtM, 2 * dc + h + 1)
                v.tensor_scalar(out=relu[:, hs], in0=P[2 + h][:, :],
                                scalar1=bdt_sb[:, dc:dc + 1], scalar2=0.0,
                                op0=ALU.add, op1=ALU.max)
                relu_mark[dc][h] = p.cnt["v"]
                v.wait_mark("s", m_sp[h])
                if dc >= 2 and h == 0:
                    v.wait_ge(sOutD[dc % 2], 32 * (dc // 2))
                v.tensor_add(dt_sb[:, dsl0 + h * 512: dsl0 + (h + 1) * 512],
                             relu[:, hs], l1p[:, hs])
                dtl_mark[dc][h] = p.cnt["v"]
            v.tensor_mul(dtu_sb[:, dsl0:dsl0 + TPC], dt_sb[:, dsl0:dsl0 + TPC],
                         x0n[:, dc * TPC:(dc + 1) * TPC])
            dtu_mark[dc] = p.cnt["v"]
        p.v(dt_ep)

        def dt_out(y, dc=dc):
            y.wait_ge(p.auto["v"], dtu_mark[dc])
            dsl = slice((dc % 2) * TPC, (dc % 2) * TPC + TPC)
            y.dma_start(out=o_dt[dc * 128:(dc + 1) * 128, :],
                        in_=dt_sb[:, dsl]).then_inc(sOutD[dc % 2], 16)
            y.dma_start(out=o_dtu[dc * 128:(dc + 1) * 128, :],
                        in_=dtu_sb[:, dsl]).then_inc(sOutD[dc % 2], 16)
        p.y(dt_out)

    def finish(y):
        y.wait_ge(p.auto["s"], m_dbf)
        y.dma_start(out=o_bc, in_=bc_sb[:, :]).then_inc(sOutB, 16)
        y.wait_ge(sOutD[0], 32 * (NDC // 2))
        y.wait_ge(sOutD[1], 32 * (NDC // 2))
        y.wait_ge(sOutB, 16)
    p.y(finish)

    st.close()
    return nc


CONSTS_W = [1.0]


def _build_W():
    nc = bass.Bass("TRN2", target_bir_lowering=False)
    wr4 = nc.dram_tensor("wr4", [1024, D], F32, kind="ExternalInput").ap()
    wt4 = nc.dram_tensor("wt4", [D, 1024], F32, kind="ExternalInput").ap()
    w1s = nc.dram_tensor("w1s", [D, 1024], F32, kind="ExternalInput").ap()
    w2s = nc.dram_tensor("w2s", [D, 1024], F32, kind="ExternalInput").ap()
    w3s = nc.dram_tensor("w3s", [1024, D], F32, kind="ExternalInput").ap()
    constv = nc.dram_tensor("constv", [128, 1], F32, kind="ExternalInput").ap()
    o_q4 = nc.dram_tensor("o_q4", [1024, D], BF16, kind="ExternalOutput").ap()
    o_s4 = nc.dram_tensor("o_s4", [128, 8], F32, kind="ExternalOutput").ap()
    o_wbf = nc.dram_tensor("o_wbf", [D, 1024], BF16, kind="ExternalOutput").ap()
    o_msum = nc.dram_tensor("o_msum", [1, 4], F32, kind="ExternalOutput").ap()

    st = ExitStack()
    E = st.enter_context
    wrow = E(nc.sbuf_tensor("wrow", [128, 2 * D], F32))
    wqr = E(nc.sbuf_tensor("wqr", [128, D], F32))
    qir = E(nc.sbuf_tensor("qir", [128, 2 * D], BF16))
    wscl = E(nc.sbuf_tensor("wscl", [128, 8], F32))
    rbuf = E(nc.sbuf_tensor("rbuf", [128, 1], F32))
    cbuf = E(nc.sbuf_tensor("cbuf", [128, 2 * 1024], BF16))
    mld = E(nc.sbuf_tensor("mld", [128, 2 * 2048], F32))
    mab = E(nc.sbuf_tensor("mab", [128, 2048], F32))
    macc = E(nc.sbuf_tensor("macc", [128, 4], F32))
    mrow = E(nc.sbuf_tensor("mrow", [1, 4], F32))
    cst_sb = E(nc.sbuf_tensor("cst_sb", [128, 1], F32))
    P0 = E(nc.psum_tensor("P0", [128, 512], F32))

    for i, v in enumerate(CONSTS_W):
        nc.const_aps.register(F32, v, cst_sb[:, i:i + 1])

    p = Prog(nc, st)
    sLc = p.sem("sLc")
    sWr = [p.sem("sWr0"), p.sem("sWr1")]
    sQo = [p.sem("sQo0"), p.sem("sQo1")]
    sCo = [p.sem("sCo0"), p.sem("sCo1")]
    sMl = [p.sem("sMl0"), p.sem("sMl1")]
    sMM = p.sem("sMM")
    sOutS = p.sem("sOutS")
    sOutM = p.sem("sOutM")

    def consts(y):
        y.dma_start(out=cst_sb[:, :], in_=constv).then_inc(sLc, 16)
    p.y(consts)

    def init(v):
        v.wait_ge(sLc, 16)
        v.memset(macc[:, :], 0.0)
    p.v(init)

    q_mark = {}
    for rb in range(8):
        slot = rb % 2

        def qload(y, rb=rb, slot=slot):
            if rb >= 2:
                y.wait_ge(sQo[slot], 16 * (rb // 2))
            y.dma_start(out=wrow[:, slot * D:(slot + 1) * D],
                        in_=wr4[rb * 128:(rb + 1) * 128, :]
                        ).then_inc(sWr[slot], 16)
        p.y(qload)

        def quant(v, rb=rb, slot=slot):
            ss = slice(slot * D, (slot + 1) * D)
            v.wait_ge(sWr[slot], 16 * (rb // 2 + 1))
            if rb >= 2:
                v.wait_ge(sQo[slot], 16 * (rb // 2))
            c = wscl[:, rb:rb + 1]
            v.reduce_max(c, wrow[:, ss], axis=AX.X, apply_absolute_value=True)
            v.tensor_scalar_mul(c, c, 1.0 / 7.0)
            v.reciprocal(rbuf[:, 0:1], c)
            v.tensor_scalar_mul(wqr[:, :], wrow[:, ss], rbuf[:, 0:1])
            v.tensor_scalar_add(wrow[:, ss], wqr[:, :], LAD4_C)
            h0 = slot * D
            _emit_ladder(v, [(wrow[:, h0:h0 + 1024], qir[:, h0:h0 + 1024]),
                             (wrow[:, h0 + 1024:h0 + 2048],
                              qir[:, h0 + 1024:h0 + 2048])], LAD4)
            v.scalar_tensor_tensor(out=wqr[:, :], in0=wrow[:, ss], scalar=-1.0,
                                   in1=wqr[:, :], op0=ALU.mult, op1=ALU.add)
            v.tensor_scalar_add(qir[:, ss], wqr[:, :], 0.5)
            q_mark[rb] = p.cnt["v"]
        p.v(quant)

        def qout(y, rb=rb, slot=slot):
            y.wait_ge(p.auto["v"], q_mark[rb])
            y.dma_start(out=o_q4[rb * 128:(rb + 1) * 128, :],
                        in_=qir[:, slot * D:(slot + 1) * D]
                        ).then_inc(sQo[slot], 16)
        p.y(qout)

    def sclout(y):
        y.wait_ge(p.auto["v"], q_mark[7])
        y.dma_start(out=o_s4, in_=wscl[:, :]).then_inc(sOutS, 16)
    p.y(sclout)

    c_mark = {}
    for ct in range(16):
        slot = ct % 2

        def cload(g, ct=ct, slot=slot):
            if ct >= 2:
                g.wait_ge(sCo[slot], 16 * (ct // 2))
            g.dma_start(out=cbuf[:, slot * 1024:(slot + 1) * 1024],
                        in_=wt4[ct * 128:(ct + 1) * 128, :])
            c_mark[ct] = p.cnt["gd"]
        p.g(cload)

        def cout(y, ct=ct, slot=slot):
            y.wait_ge(p.auto["gd"], c_mark[ct])
            y.dma_start(out=o_wbf[ct * 128:(ct + 1) * 128, :],
                        in_=cbuf[:, slot * 1024:(slot + 1) * 1024]
                        ).then_inc(sCo[slot], 16)
        p.y(cout)

    mtiles = []
    for w, (src, R, C) in enumerate(((w1s, D, 1024), (w2s, D, 1024),
                                     (w3s, 1024, D))):
        for a in range(R // 128):
            mtiles.append((w, src, a, C))
    mk = {}
    ml_cnt = [0, 0]
    for mi, (w, src, a, C) in enumerate(mtiles):
        slot = mi % 2

        def mload(y, src=src, a=a, C=C, mi=mi, slot=slot):
            if mi >= 2:
                y.wait_ge(p.auto["s"], mk[("ab", mi - 2)])
            y.dma_start(out=mld[:, slot * 2048:slot * 2048 + C],
                        in_=src[a * 128:(a + 1) * 128, :]
                        ).then_inc(sMl[slot], 16)
            ml_cnt[slot] += 16
        p.y(mload)
        need = ml_cnt[slot]

        def mabs(s, mi=mi, slot=slot, C=C, need=need):
            s.wait_ge(sMl[slot], need)
            if mi >= 1:
                s.wait_mark("v", mk.get(("acc", mi - 1), 0))
            s.activation(out=mab[:, 0:C],
                         in_=mld[:, slot * 2048:slot * 2048 + C], func=AF.Abs)
            mk[("ab", mi)] = p.cnt["s"]
        p.s(mabs)

        def macc_(v, w=w, mi=mi, C=C):
            v.wait_mark("s", mk[("ab", mi)])
            v.reduce_sum(rbuf[:, 0:1], mab[:, 0:C], axis=AX.X)
            v.tensor_add(macc[:, w:w + 1], macc[:, w:w + 1], rbuf[:, 0:1])
            mk[("acc", mi)] = p.cnt["v"]
        p.v(macc_)
    m_macc = p.mark("v")

    def msum_mm(t):
        t.wait_ge(p.auto["v"], m_macc)
        t.wait_ge(sLc, 16)
        t.matmul(P0[0:1, 0:4], cst_sb[:, 0:1], macc[:, 0:4], start=True,
                 stop=True).then_inc(sMM, 1)
    p.t(msum_mm)

    def mevac(v):
        v.wait_ge(sMM, 1)
        v.tensor_copy(out=mrow[0:1, 0:4], in_=P0[0:1, 0:4])
    p.v(mevac)
    m_mrow = p.mark("v")

    def mout(y):
        y.wait_ge(p.auto["v"], m_mrow)
        y.dma_start(out=o_msum, in_=mrow[0:1, :]).then_inc(sOutM, 16)
    p.y(mout)

    def finish(y):
        y.wait_ge(sQo[0], 16 * 4)
        y.wait_ge(sQo[1], 16 * 4)
        y.wait_ge(sCo[0], 16 * 8)
        y.wait_ge(sCo[1], 16 * 8)
        y.wait_ge(sOutS, 16)
        y.wait_ge(sOutM, 16)
    p.y(finish)

    st.close()
    return nc


def _mk_masks():
    m = np.zeros((128, 16, 128), np.float32)
    for lb in range(16):
        for j in range(8):
            for n in range(16):
                m[j * 16 + n, lb, 8 * lb + j] = 1.0
    return m.reshape(128, 16 * 128).astype(BF)


def _alog_lanes(alog_slice):
    a = alog_slice.reshape(4, 16, 8, 16)
    return np.ascontiguousarray(a.transpose(2, 3, 0, 1).reshape(128, 64))


def _build_B():
    TC = 1024
    NTC = T // TC
    nc = bass.Bass("TRN2", target_bir_lowering=False)
    dt_in = nc.dram_tensor("dt_in", [512, T], F32, kind="ExternalInput").ap()
    dtu_in = nc.dram_tensor("dtu_in", [512, T], F32, kind="ExternalInput").ap()
    b_in = nc.dram_tensor("b_in", [NST, T], F32, kind="ExternalInput").ap()
    c_in = nc.dram_tensor("c_in", [NST, T], F32, kind="ExternalInput").ap()
    alogl = nc.dram_tensor("alogl", [128, 64], F32, kind="ExternalInput").ap()
    masks = nc.dram_tensor("masks", [128, 16 * 128], BF16,
                           kind="ExternalInput").ap()
    o_y = nc.dram_tensor("o_y", [512, T], F32, kind="ExternalOutput").ap()

    st = ExitStack()
    E = st.enter_context
    dtr = E(nc.sbuf_tensor("dtr", [128, 2 * 4 * TC], F32))
    dur = E(nc.sbuf_tensor("dur", [128, 2 * 4 * TC], F32))
    a_sb = E(nc.sbuf_tensor("a_sb", [128, 4 * TC], F32))
    b_sb = E(nc.sbuf_tensor("b_sb", [128, 4 * TC], F32))
    h_sb = E(nc.sbuf_tensor("h_sb", [128, 2 * TC], F32))
    tmp_sb = E(nc.sbuf_tensor("tmp_sb", [128, 4 * TC], BF16))
    b16 = E(nc.sbuf_tensor("b16", [128, TC], F32))
    c16 = E(nc.sbuf_tensor("c16", [128, TC], F32))
    nal = E(nc.sbuf_tensor("nal", [128, 64], F32))
    msk = E(nc.sbuf_tensor("msk", [128, 16 * 128], BF16))
    stt = E(nc.sbuf_tensor("stt", [128, 64], F32))
    y_sb = E(nc.sbuf_tensor("y_sb", [128, 2 * TC], F32))

    Py = [E(nc.psum_tensor(f"Py{i}", [128, 512], F32)) for i in range(4)]

    p = Prog(nc, st)
    sLc = p.sem("sLc")
    sRepD = [p.sem("sRepD0"), p.sem("sRepD1")]
    sRepU = [p.sem("sRepU0"), p.sem("sRepU1")]
    sB16 = p.sem("sB16")
    sYmm = p.sem("sYmm")
    sOutY = [p.sem("sOutY0"), p.sem("sOutY1")]

    def consts(y):
        y.dma_start(out=nal[:, :], in_=alogl).then_inc(sLc, 16)
        y.dma_start(out=msk[:, :], in_=masks).then_inc(sLc, 16)
    p.y(consts)

    def negate(s):
        s.wait_ge(sLc, 32)
        s.activation(out=nal[:, :], in_=nal[:, :], func=AF.Exp)
        s.mul(nal[:, :], nal[:, :], -1.0)
    p.s(negate)
    m_nal = p.mark("s")

    a_mark = {}
    b_mark = {}
    tmp_mark = {}
    scan_mark = {}
    mmcnt_after = {}
    ev_mark = {}
    mm_cnt = 0
    b16_last_v = 0
    b16_last_g = 0

    for tc in range(NTC):
        tsl = slice(tc * TC, (tc + 1) * TC)

        def b16_load(y, tc=tc, tsl=tsl):
            if tc > 0:
                y.wait_ge(p.auto["g"], b16_last_g)
                y.wait_ge(p.auto["v"], b16_last_v)
            y.dma_start(out=b16[:, :],
                        in_=b_in[None, :, tsl].broadcast_to([8, NST, TC])
                        ).then_inc(sB16, 16)
            y.dma_start(out=c16[:, :],
                        in_=c_in[None, :, tsl].broadcast_to([8, NST, TC])
                        ).then_inc(sB16, 16)
        p.y(b16_load)

        for cc in range(4):
            for gr in range(4):
                gidx = (tc * 4 + cc) * 4 + gr
                slot = gidx % 2

                def rep_load(y, tc=tc, cc=cc, gr=gr, gidx=gidx, slot=slot):
                    if gidx >= 2:
                        pg = gidx - 2
                        y.wait_ge(p.auto["s"], a_mark[(pg, 3)])
                        y.wait_ge(p.auto["g"], b_mark[(pg, 3)])
                    r0 = cc * 128 + gr * 32
                    for src, dst, sem in ((dt_in, dtr, sRepD[slot]),
                                          (dtu_in, dur, sRepU[slot])):
                        for k in range(4):
                            v = src[r0 + 8 * k:r0 + 8 * k + 8,
                                    tc * TC:(tc + 1) * TC]
                            v = v[:, None, :].broadcast_to([8, NST, TC])
                            y.dma_start(
                                out=dst[:, (slot * 4 + k) * TC:
                                        (slot * 4 + k + 1) * TC],
                                in_=v).then_inc(sem, 16)
                p.y(rep_load)

                def a_ops(s, tc=tc, cc=cc, gr=gr, gidx=gidx, slot=slot):
                    s.wait_mark("s", m_nal)
                    s.wait_ge(sRepD[slot], 64 * (gidx // 2 + 1))
                    for k in range(4):
                        li = gidx * 4 + k
                        if li >= 4:
                            s.wait_mark("v", scan_mark.get(li - 4, 0))
                        col = cc * 16 + gr * 4 + k
                        s.activation(
                            out=a_sb[:, (li % 4) * TC:(li % 4 + 1) * TC],
                            in_=dtr[:, (slot * 4 + k) * TC:(slot * 4 + k + 1) * TC],
                            func=AF.Exp, scale=nal[:, col:col + 1])
                        a_mark[(gidx, k)] = p.cnt["s"]
                p.s(a_ops)

                def b_ops(g, tc=tc, gidx=gidx, slot=slot):
                    g.wait_ge(sRepU[slot], 64 * (gidx // 2 + 1))
                    g.wait_ge(sB16, 32 * (tc + 1))
                    for k in range(4):
                        li = gidx * 4 + k
                        if li >= 4:
                            g.wait_mark("v", scan_mark.get(li - 4, 0))
                        g.tensor_tensor(
                            out=b_sb[:, (li % 4) * TC:(li % 4 + 1) * TC],
                            in0=dur[:, (slot * 4 + k) * TC:(slot * 4 + k + 1) * TC],
                            in1=b16[:, :], op=ALU.mult)
                        b_mark[(gidx, k)] = p.cnt["g"]
                p.g(b_ops)

                def v_ops(v, tc=tc, cc=cc, gr=gr, gidx=gidx):
                    nonlocal b16_last_v
                    for k in range(4):
                        li = gidx * 4 + k
                        col = cc * 16 + gr * 4 + k
                        v.wait_mark("s", a_mark[(gidx, k)])
                        v.wait_mark("g", b_mark[(gidx, k)])
                        asl = a_sb[:, (li % 4) * TC:(li % 4 + 1) * TC]
                        bsl = b_sb[:, (li % 4) * TC:(li % 4 + 1) * TC]
                        hsl = h_sb[:, (li % 2) * TC:(li % 2 + 1) * TC]
                        init = 0.0 if tc == 0 else stt[:, col:col + 1]
                        v.tensor_tensor_scan(out=hsl, data0=asl, data1=bsl,
                                             initial=init, op0=ALU.mult,
                                             op1=ALU.add)
                        scan_mark[li] = p.cnt["v"]
                        v.tensor_copy(out=stt[:, col:col + 1],
                                      in_=hsl[:, TC - 1:TC])
                        if li >= 4:
                            v.wait_ge(sYmm, mmcnt_after.get(li - 4, 0))
                        v.tensor_tensor(
                            out=tmp_sb[:, (li % 4) * TC:(li % 4 + 1) * TC],
                            in0=hsl, in1=c16[:, :], op=ALU.mult)
                        tmp_mark[li] = p.cnt["v"]
                        b16_last_v = p.cnt["v"]
                p.v(v_ops)

                def t_ops(t, tc=tc, cc=cc, gr=gr, gidx=gidx):
                    nonlocal mm_cnt
                    pb = 2 * (cc % 2)
                    for k in range(4):
                        li = gidx * 4 + k
                        lb = gr * 4 + k
                        t.wait_ge(p.auto["v"], tmp_mark[li])
                        if lb == 0:
                            t.wait_ge(p.auto["s"], ev_mark.get((tc - 1, cc), 0)
                                      if tc > 0 else 0)
                        for h in range(2):
                            t.matmul(
                                Py[pb + h][:, :],
                                msk[:, lb * 128:(lb + 1) * 128],
                                tmp_sb[:, (li % 4) * TC + h * 512:
                                       (li % 4) * TC + (h + 1) * 512],
                                start=(lb == 0), stop=(lb == 15)
                            ).then_inc(sYmm, 1)
                            mm_cnt += 1
                        mmcnt_after[li] = mm_cnt
                p.t(t_ops)

            def y_evac(s, tc=tc, cc=cc):
                pb = 2 * (cc % 2)
                s.wait_ge(sYmm, mm_cnt)
                s.wait_ge(sOutY[cc % 2], 16 * (tc * 2 + cc // 2))
                for h in range(2):
                    s.activation(out=y_sb[:, (cc % 2) * TC + h * 512:
                                          (cc % 2) * TC + (h + 1) * 512],
                                 in_=Py[pb + h][:, :], func=AF.Copy)
                ev_mark[(tc, cc)] = p.cnt["s"]
            p.s(y_evac)

            def y_out(y, tc=tc, cc=cc):
                y.wait_ge(p.auto["s"], ev_mark[(tc, cc)])
                y.dma_start(
                    out=o_y[cc * 128:(cc + 1) * 128, tc * TC:(tc + 1) * TC],
                    in_=y_sb[:, (cc % 2) * TC:(cc % 2 + 1) * TC]
                ).then_inc(sOutY[cc % 2], 16)
            p.y(y_out)

        def b16_done(g, tc=tc):
            nonlocal b16_last_g
            b16_last_g = p.cnt["g"]
        b16_done(None)

    def finish(y):
        y.wait_ge(sOutY[0], 16 * 8)
        y.wait_ge(sOutY[1], 16 * 8)
    p.y(finish)

    st.close()
    return nc




def _ln_fm(p, P, MM, mm_cnt, sMM, cst_sb, or_sb, rows, lnscr, src_sb, dst_sb,
           row0, extra_wait_mark=None, first_sq_waits=()):
    m_src = p.mark("v")
    R = lambda k: rows[0:1, (row0 + k) * TPC:(row0 + k) * TPC + TPC]

    def s1mm(t):
        t.wait_ge(p.auto["v"], m_src)
        if extra_wait_mark is not None:
            t.wait_ge(p.auto["s"], extra_wait_mark)
        for dc in range(NDC):
            for h in range(2):
                MM(t, P[h][0:1, :], cst_sb[:, 3:4],
                   src_sb[:, dc * TPC + h * 512:dc * TPC + (h + 1) * 512],
                   start=(dc == 0), stop=(dc == NDC - 1))
    p.t(s1mm)

    sqm = {}
    mm_after = {}
    for dc in range(NDC):
        def sq(s, dc=dc):
            if dc == 0:
                for sem_, val_ in first_sq_waits:
                    s.wait_ge(sem_, val_)
                s.wait_ge(p.auto["v"], m_src)
            if dc >= 1:
                s.wait_ge(sMM, mm_after[dc - 1])
            s.activation(out=lnscr[:, 0:TPC],
                         in_=src_sb[:, dc * TPC:(dc + 1) * TPC], func=AF.Square)
            sqm[dc] = p.cnt["s"]
        p.s(sq)

        def s2mm(t, dc=dc):
            t.wait_ge(p.auto["s"], sqm[dc])
            for h in range(2):
                MM(t, P[2 + h][0:1, :], cst_sb[:, 3:4],
                   lnscr[:, h * 512:(h + 1) * 512],
                   start=(dc == 0), stop=(dc == NDC - 1))
            mm_after[dc] = mm_cnt[0]
        p.t(s2mm)
    mm_s2 = mm_cnt[0]

    def rowstats(v):
        v.wait_ge(sMM, mm_s2)
        for h in range(2):
            hs = slice(h * 512, (h + 1) * 512)
            v.tensor_scalar_mul(R(0)[:, hs], P[h][0:1, :], 1.0 / D)
            v.tensor_scalar_mul(R(1)[:, hs], P[2 + h][0:1, :], 1.0 / D)
        v.tensor_mul(R(2), R(0), R(0))
        v.tensor_sub(R(1), R(1), R(2))
    p.v(rowstats)
    m_vr = p.mark("v")

    def rowrstd(s):
        s.wait_mark("v", m_vr)
        s.activation(out=R(1), in_=R(1), func=AF.Ln, bias=LN_EPS)
        s.activation(out=R(1), in_=R(1), func=AF.Exp, scale=-0.5)
    p.s(rowrstd)
    m_rr = p.mark("s")

    def rowmr(v):
        v.wait_mark("s", m_rr)
        v.tensor_mul(R(2), R(0), R(1))
    p.v(rowmr)
    m_mr = p.mark("v")

    def bcast(t):
        t.wait_ge(p.auto["v"], m_mr)
        for h in range(2):
            hs = slice(h * 512, (h + 1) * 512)
            MM(t, P[4 + h][:, :], or_sb[:, :], R(1)[:, hs])
            MM(t, P[6 + h][:, :], or_sb[:, :], R(2)[:, hs])
    p.t(bcast)
    mm_bc = mm_cnt[0]

    def norm(v):
        v.wait_ge(sMM, mm_bc)
        for dc in range(NDC):
            for h in range(2):
                cs = slice(dc * TPC + h * 512, dc * TPC + (h + 1) * 512)
                scr = lnscr[:, h * 512:(h + 1) * 512]
                v.tensor_mul(scr, src_sb[:, cs], P[4 + h][:, :])
                v.tensor_sub(dst_sb[:, cs], scr, P[6 + h][:, :])
    p.v(norm)
    return p.mark("v")


LN2C = 0.6931471805599453
CONSTS_C = [LN_EPS, -0.5, LN2C, 1.0]


def _build_C1():
    nc = bass.Bass("TRN2", target_bir_lowering=False)
    inp = lambda n, sh, dt=F32: nc.dram_tensor(n, sh, dt, kind="ExternalInput").ap()
    xr = inp("xr", [TPC, D])
    yin = inp("yin", [D, TPC])
    dpc = inp("dpc", [128, NDC])
    p1t = inp("p1t", [D, 4 * D])
    b1c = inp("b1c", [128, 64])
    p2t = inp("p2t", [4 * D, D])
    b2c = inp("b2c", [128, NDC])
    ident = inp("ident", [128, 128])
    onesrow = inp("onesrow", [1, 128])
    constv = inp("constv", [128, len(CONSTS_C)])
    msums = inp("msums", [8, 4])
    w1ts = inp("w1ts", [D, TPC])
    w2ts = inp("w2ts", [D, TPC])
    w3ts = inp("w3ts", [TPC, D])
    o_x2 = nc.dram_tensor("o_x2", [D, TPC], F32, kind="ExternalOutput").ap()
    o_w1q = nc.dram_tensor("o_w1q", [D, TPC], BF16, kind="ExternalOutput").ap()
    o_w2q = nc.dram_tensor("o_w2q", [D, TPC], BF16, kind="ExternalOutput").ap()
    o_w3q = nc.dram_tensor("o_w3q", [TPC, D], BF16, kind="ExternalOutput").ap()
    o_wm = nc.dram_tensor("o_wm", [1, 4], F32, kind="ExternalOutput").ap()

    st = ExitStack()
    E = st.enter_context
    x1 = E(nc.sbuf_tensor("x1", [128, NDC * TPC], F32))
    x1n = E(nc.sbuf_tensor("x1n", [128, NDC * TPC], BF16))
    lnscr = E(nc.sbuf_tensor("lnscr", [128, TPC], F32))
    rows = E(nc.sbuf_tensor("rows", [1, 4 * TPC], F32))
    statc = E(nc.sbuf_tensor("statc", [128, NTT * 8], F32))
    dp_sb = E(nc.sbuf_tensor("dp_sb", [128, NDC], F32))
    b1_sb = E(nc.sbuf_tensor("b1_sb", [128, 64], F32))
    b2_sb = E(nc.sbuf_tensor("b2_sb", [128, NDC], F32))
    id_sb = E(nc.sbuf_tensor("id_sb", [128, 128], F32))
    or_sb = E(nc.sbuf_tensor("or_sb", [1, 128], F32))
    cst_sb = E(nc.sbuf_tensor("cst_sb", [128, len(CONSTS_C)], F32))
    msb = E(nc.sbuf_tensor("msb", [8, 4], F32))
    wsm = E(nc.sbuf_tensor("wsm", [128, 4], F32))
    P = [E(nc.psum_tensor(f"P{i}", [128, 512], F32)) for i in range(8)]

    for i, v in enumerate(CONSTS_C):
        nc.const_aps.register(F32, v, cst_sb[:, i:i + 1])

    p = Prog(nc, st)
    sLc = p.sem("sLc")
    sLx = [p.sem("sLx0"), p.sem("sLx1")]
    sLy = [p.sem("sLy0"), p.sem("sLy1")]
    sTr = p.sem("sTr")
    sMM = p.sem("sMM")
    sOut = [p.sem("sOut0"), p.sem("sOut1")]
    sTl = [p.sem("sTl0"), p.sem("sTl1")]
    sTq = [p.sem("sTq0"), p.sem("sTq1")]
    sWm = p.sem("sWm")
    mm_cnt = [0]

    def MM(t, out_ap, lhs, rhs, start=True, stop=True):
        t.matmul(out_ap, lhs, rhs, start=start, stop=stop).then_inc(sMM, 1)
        mm_cnt[0] += 1

    def consts(y):
        for dst, srcd in ((dp_sb, dpc), (b1_sb, b1c), (b2_sb, b2c),
                          (id_sb, ident), (or_sb, onesrow), (cst_sb, constv),
                          (msb, msums)):
            y.dma_start(out=dst[:, :], in_=srcd).then_inc(sLc, 16)
    p.y(consts)
    NLC = 16 * 7

    ph1 = ExitStack()
    E1 = ph1.enter_context
    xtm = E1(nc.sbuf_tensor("xtm", [128, 2 * D], F32))
    sqt = E1(nc.sbuf_tensor("sqt", [128, 2 * D], F32))
    x0t = E1(nc.sbuf_tensor("x0t", [128, 2 * D], F32))
    y_t = E1(nc.sbuf_tensor("y_t", [128, 2 * D], F32))

    aff_mark = [0] * NTT
    red_mark = [0] * NTT
    ev1_mark = [0] * NTT

    for tt in range(NTT):
        sl = slice((tt % 2) * D, (tt % 2) * D + D)

        def load(y, tt=tt, sl=sl):
            if tt >= 2:
                y.wait_ge(p.auto["v"], aff_mark[tt - 2])
                y.wait_ge(sTr, 2 * (tt - 1))
            y.dma_start(out=xtm[:, sl],
                        in_=xr[tt * 128:(tt + 1) * 128, :]).then_inc(sLx[tt % 2], 16)
            if tt >= 2:
                y.wait_ge(p.auto["v"], ev1_mark[tt - 2])
            y.dma_start(
                out=y_t[:, sl],
                in_=yin.rearrange("(c p) t -> p c t", p=128)
                [:, :, tt * 128:(tt + 1) * 128]).then_inc(sLy[tt % 2], 16)
        p.y(load)

        def square(s, tt=tt, sl=sl):
            s.wait_ge(sLc, NLC)
            s.wait_ge(sLx[tt % 2], 16 * (tt // 2 + 1))
            if tt >= 2:
                s.wait_mark("v", red_mark[tt - 2])
            s.activation(out=sqt[:, sl], in_=xtm[:, sl], func=AF.Square)
        p.s(square)
        m_sq = p.mark("s")

        def stats(v, tt=tt, sl=sl):
            c = lambda k: statc[:, tt * 8 + k: tt * 8 + k + 1]
            v.wait_mark("s", m_sq)
            v.reduce_sum(c(0), xtm[:, sl], axis=AX.X)
            v.reduce_sum(c(1), sqt[:, sl], axis=AX.X)
            red_mark[tt] = p.cnt["v"]
            v.tensor_scalar_mul(c(2), c(0), 1.0 / D)
            v.tensor_scalar_mul(c(3), c(1), 1.0 / D)
            v.tensor_mul(c(7), c(2), c(2))
            v.tensor_sub(c(4), c(3), c(7))
        p.v(stats)
        m_var = p.mark("v")

        def rstd(s, tt=tt):
            c = lambda k: statc[:, tt * 8 + k: tt * 8 + k + 1]
            s.wait_mark("v", m_var)
            s.activation(out=c(5), in_=c(4), func=AF.Ln, bias=LN_EPS)
            s.activation(out=c(6), in_=c(5), func=AF.Exp, scale=-0.5)
        p.s(rstd)
        m_rstd = p.mark("s")

        def affine(v, tt=tt, sl=sl):
            c = lambda k: statc[:, tt * 8 + k: tt * 8 + k + 1]
            v.wait_mark("s", m_rstd)
            if tt >= 2:
                v.wait_ge(sTr, 2 * (tt - 1))
            v.tensor_scalar(out=x0t[:, sl], in0=xtm[:, sl], scalar1=c(2),
                            scalar2=c(6), op0=ALU.subtract, op1=ALU.mult)
            aff_mark[tt] = p.cnt["v"]
        p.v(affine)

        def transposes(t, tt=tt):
            t.wait_ge(p.auto["v"], aff_mark[tt])
            t.wait_ge(sLc, NLC)
            if tt >= 1:
                t.wait_ge(p.auto["v"], ev1_mark[tt - 1])
            last = None
            for k in range(NDC):
                last = t.transpose(
                    P[k // 4][:, (k % 4) * 128:(k % 4) * 128 + 128],
                    xtm[:, (tt % 2) * D + k * 128:(tt % 2) * D + (k + 1) * 128],
                    id_sb[:, :])
            for k in range(NDC):
                last = t.transpose(
                    P[4 + k // 4][:, (k % 4) * 128:(k % 4) * 128 + 128],
                    x0t[:, (tt % 2) * D + k * 128:(tt % 2) * D + (k + 1) * 128],
                    id_sb[:, :])
            last.then_inc(sTr, 2)
        p.t(transposes)

        def ev1(v, tt=tt):
            v.wait_ge(sTr, 2 * (tt + 1))
            v.wait_ge(sLy[tt % 2], 16 * (tt // 2 + 1))
            for k in range(NDC):
                xb = P[k // 4][:, (k % 4) * 128:(k % 4) * 128 + 128]
                nb = P[4 + k // 4][:, (k % 4) * 128:(k % 4) * 128 + 128]
                cs = slice(k * TPC + tt * 128, k * TPC + (tt + 1) * 128)
                v.scalar_tensor_tensor(
                    out=x1[:, cs], in0=nb, scalar=dp_sb[:, k:k + 1],
                    in1=y_t[:, (tt % 2) * D + k * 128:(tt % 2) * D + (k + 1) * 128],
                    op0=ALU.mult, op1=ALU.add)
                v.tensor_add(x1[:, cs], x1[:, cs], xb)
            ev1_mark[tt] = p.cnt["v"]
        p.v(ev1)

    ph1.close()

    ln1_mark = _ln_fm(p, P, MM, mm_cnt, sMM, cst_sb, or_sb, rows, lnscr,
                      x1, x1n, 0)

    ph2 = ExitStack()
    E2 = ph2.enter_context
    hh = E2(nc.sbuf_tensor("hh", [128, 64 * 512], BF16))
    w1 = E2(nc.sbuf_tensor("w1", [128, 2 * 2048], BF16))
    w2 = E2(nc.sbuf_tensor("w2", [128, 2 * 4096], BF16))

    HOC = 64
    wl_mark = {}
    mmw_mark = {}
    ge_mark = {}
    w2l_mark = {}
    mm2_mark = {}
    ev2_mark = {}

    for h in range(2):
        for oc in range(HOC):
            widx = h * HOC + oc
            ws_ = widx % 2

            def wload(g, oc=oc, widx=widx, ws_=ws_):
                if widx == 0:
                    g.wait_ge(p.auto["v"], ln1_mark)
                if widx >= 2:
                    g.wait_ge(sMM, mmw_mark.get(widx - 2, 0))
                g.dma_start(
                    out=w1[:, ws_ * 2048:ws_ * 2048 + 2048],
                    in_=p1t.rearrange("(c p) o -> p c o", p=128)
                    [:, :, oc * 128:(oc + 1) * 128])
                wl_mark[widx] = p.cnt["gd"]
            p.g(wload)

            def p1mm(t, h=h, oc=oc, widx=widx, ws_=ws_):
                t.wait_ge(p.auto["gd"], wl_mark[widx])
                t.wait_ge(p.auto["v"], ln1_mark)
                if widx >= 2:
                    t.wait_ge(p.auto["s"], ge_mark.get(widx - 2, 0))
                for dc in range(NDC):
                    MM(t, P[widx % 2][:, :],
                       w1[:, ws_ * 2048 + dc * 128: ws_ * 2048 + (dc + 1) * 128],
                       x1n[:, dc * TPC + h * 512: dc * TPC + (h + 1) * 512],
                       start=(dc == 0), stop=(dc == NDC - 1))
                mmw_mark[widx] = mm_cnt[0]
            p.t(p1mm)

            def gelu_ev(s, oc=oc, widx=widx):
                s.wait_ge(sMM, mmw_mark[widx])
                s.activation(out=hh[:, oc * 512:(oc + 1) * 512],
                             in_=P[widx % 2][:, :], func=AF.Gelu,
                             bias=b1_sb[:, oc:oc + 1])
                ge_mark[widx] = p.cnt["s"]
            p.s(gelu_ev)

        for ec in range(NDC):
            eidx = h * NDC + ec
            for kh in range(2):
                ki = 2 * eidx + kh

                def w2load(g, ec=ec, kh=kh, ki=ki):
                    if ki >= 2:
                        g.wait_ge(sMM, mm2_mark.get(ki - 2, 0))
                    g.dma_start(
                        out=w2[:, (ki % 2) * 4096:(ki % 2) * 4096 + 4096],
                        in_=p2t.rearrange("(c p) o -> p c o", p=128)
                        [:, kh * 32:(kh + 1) * 32, ec * 128:(ec + 1) * 128])
                    w2l_mark[ki] = p.cnt["gd"]
                p.g(w2load)

                def p2mm(t, h=h, ec=ec, eidx=eidx, kh=kh, ki=ki):
                    t.wait_ge(p.auto["gd"], w2l_mark[ki])
                    t.wait_ge(p.auto["s"], ge_mark[h * HOC + HOC - 1])
                    if kh == 0 and eidx >= 2:
                        t.wait_ge(p.auto["v"], ev2_mark.get(eidx - 2, 0))
                    for j in range(32):
                        hc = kh * 32 + j
                        MM(t, P[2 + eidx % 2][:, :],
                           w2[:, (ki % 2) * 4096 + j * 128:(ki % 2) * 4096 + (j + 1) * 128],
                           hh[:, hc * 512:(hc + 1) * 512],
                           start=(hc == 0), stop=(hc == 63))
                    mm2_mark[ki] = mm_cnt[0]
                p.t(p2mm)

            def x2ev(v, h=h, ec=ec, eidx=eidx):
                v.wait_ge(sMM, mm2_mark[2 * eidx + 1])
                cs = slice(ec * TPC + h * 512, ec * TPC + (h + 1) * 512)
                v.scalar_tensor_tensor(out=x1[:, cs], in0=P[2 + eidx % 2][:, :],
                                       scalar=b2_sb[:, ec:ec + 1], in1=x1[:, cs],
                                       op0=ALU.add, op1=ALU.add)
                ev2_mark[eidx] = p.cnt["v"]
            p.v(x2ev)

            def x2out(y, h=h, ec=ec, eidx=eidx):
                y.wait_ge(p.auto["v"], ev2_mark[eidx])
                y.dma_start(
                    out=o_x2[ec * 128:(ec + 1) * 128, h * 512:(h + 1) * 512],
                    in_=x1[:, ec * TPC + h * 512:ec * TPC + (h + 1) * 512]
                ).then_inc(sOut[eidx % 2], 16)
            p.y(x2out)

    ph2.close()
    mm_ffn = mm_cnt[0]

    def wmsum(t):
        t.wait_ge(sLc, NLC)
        MM(t, P[4][0:1, 0:4], cst_sb[0:8, 3:4], msb[0:8, 0:4])
    p.t(wmsum)
    mm_ws = mm_cnt[0]

    def wmev(v):
        v.wait_ge(sMM, mm_ws)
        v.tensor_scalar_mul(rows[0:1, 0:4], P[4][0:1, 0:4], 1.0 / (D * 4 * D))
        v.tensor_scalar_max(rows[0:1, 0:4], rows[0:1, 0:4], 1e-5)
        v.reciprocal(rows[0:1, 4:8], rows[0:1, 0:4])
    p.v(wmev)
    m_wmr = p.mark("v")

    def wmout(y):
        y.wait_ge(p.auto["v"], m_wmr)
        y.dma_start(out=o_wm, in_=rows[0:1, 0:4]).then_inc(sWm, 16)
    p.y(wmout)

    def wmbc(t):
        t.wait_ge(p.auto["v"], m_wmr)
        MM(t, P[5][:, 0:4], or_sb[:, :], rows[0:1, 4:8])
    p.t(wmbc)
    mm_wb = mm_cnt[0]

    def wmcols(v):
        v.wait_ge(sMM, mm_wb)
        v.tensor_copy(out=wsm[:, :], in_=P[5][:, 0:4])
    p.v(wmcols)
    m_wsm = p.mark("v")

    pht = ExitStack()
    Et = pht.enter_context
    twld = Et(nc.sbuf_tensor("twld", [128, 2 * 2048], F32))
    tc1 = Et(nc.sbuf_tensor("tc1", [128, 2048], F32))
    tc2 = Et(nc.sbuf_tensor("tc2", [128, 2048], F32))
    tqt = Et(nc.sbuf_tensor("tqt", [128, 2 * 2048], BF16))

    ttiles = []
    for w, (src, dst, R, C) in enumerate(((w1ts, o_w1q, D, TPC),
                                          (w2ts, o_w2q, D, TPC),
                                          (w3ts, o_w3q, TPC, D))):
        for a in range(R // 128):
            ttiles.append((w, src, dst, a, C))
    t_mark = {}
    for ti, (w, src, dst, a, C) in enumerate(ttiles):
        slot = ti % 2

        def tl(y, src=src, a=a, C=C, ti=ti, slot=slot):
            if ti < 2:
                y.wait_ge(sMM, mm_ffn)
            if ti >= 2:
                y.wait_ge(sTq[slot], 16 * (ti // 2))
            y.dma_start(out=twld[:, slot * 2048:slot * 2048 + C],
                        in_=src[a * 128:(a + 1) * 128, :]
                        ).then_inc(sTl[slot], 16)
        p.y(tl)

        def tq(v, w=w, C=C, ti=ti, slot=slot):
            v.wait_ge(sTl[slot], 16 * (ti // 2 + 1))
            if ti == 0:
                v.wait_mark("v", m_wsm)
            if ti >= 2:
                v.wait_ge(sTq[slot], 16 * (ti // 2))
            s0 = slot * 2048
            H = C // 2
            sc = wsm[:, w:w + 1]
            for h0 in (0, H):
                v.tensor_scalar(out=tc1[:, h0:h0 + H],
                                in0=twld[:, s0 + h0:s0 + h0 + H],
                                scalar1=sc, scalar2=0.5,
                                op0=ALU.mult, op1=ALU.is_gt)
            for h0 in (0, H):
                v.tensor_scalar(out=tc2[:, h0:h0 + H],
                                in0=twld[:, s0 + h0:s0 + h0 + H],
                                scalar1=sc, scalar2=-0.5,
                                op0=ALU.mult, op1=ALU.is_lt)
            for h0 in (0, H):
                v.tensor_sub(tqt[:, s0 + h0:s0 + h0 + H],
                             tc1[:, h0:h0 + H], tc2[:, h0:h0 + H])
            t_mark[ti] = p.cnt["v"]
        p.v(tq)

        def tout(y, dst=dst, a=a, C=C, ti=ti, slot=slot):
            y.wait_ge(p.auto["v"], t_mark[ti])
            y.dma_start(out=dst[a * 128:(a + 1) * 128, :],
                        in_=tqt[:, slot * 2048:slot * 2048 + C]
                        ).then_inc(sTq[slot], 16)
        p.y(tout)
    NTI = len(ttiles)
    pht.close()

    def finish(y):
        y.wait_ge(sOut[0], 16 * NDC)
        y.wait_ge(sOut[1], 16 * NDC)
        y.wait_ge(sTq[0], 16 * ((NTI + 1) // 2))
        y.wait_ge(sTq[1], 16 * (NTI // 2))
        y.wait_ge(sWm, 16)
    p.y(finish)

    st.close()
    return nc




def _tender_quant(p, P, MM, mm_cnt, sMM, id_sb, or_sb, lnscr, bufs,
                  src_sb, dst_bf, start_mm_wait, mbase):
    (chb, chb_bf, cmx, statq, tmx, t4, trow, tb, tbj, idxq, scq, rscq) = bufs
    CH = 4
    for ch in range(CH):
        for dc in range(NDC):
            def stat1(v, ch=ch, dc=dc):
                if ch == 0 and dc == 0:
                    v.wait_ge(sMM, start_mm_wait)
                cs = slice(dc * TPC + ch * 256, dc * TPC + (ch + 1) * 256)
                c0 = statq[:, 0:1]; c1 = statq[:, 1:2]; c2 = statq[:, 2:3]
                cb = chb[:, dc * 4 + ch:dc * 4 + ch + 1]
                v.reduce_max(c0, src_sb[:, cs], axis=AX.X)
                v.tensor_reduce(c1, src_sb[:, cs], axis=AX.X, op=ALU.min)
                v.tensor_add(c2, c0, c1)
                v.tensor_scalar_mul(cb, c2, 0.5)
                v.tensor_scalar(out=lnscr[:, 0:256], in0=src_sb[:, cs],
                                scalar1=cb, scalar2=None, op0=ALU.subtract)
                v.reduce_max(cmx[:, ch * NDC + dc:ch * NDC + dc + 1],
                             lnscr[:, 0:256], axis=AX.X,
                             apply_absolute_value=True)
            p.v(stat1)

        def tmax1(v, ch=ch):
            v.reduce_max(tmx[:, ch:ch + 1],
                         cmx[:, ch * NDC:(ch + 1) * NDC], axis=AX.X)
        p.v(tmax1)
    m_tmx = p.mark("v")

    def tmaxtr(t):
        t.wait_ge(p.auto["v"], m_tmx)
        t.transpose(P[0][0:4, 0:128], tmx[:, 0:4], id_sb[:, :]).then_inc(sMM, 1)
        mm_cnt[0] += 1
    p.t(tmaxtr)
    mm_t1 = mm_cnt[0]

    def tmaxred(v):
        v.wait_ge(sMM, mm_t1)
        v.reduce_max(t4[0:4, 0:1], P[0][0:4, 0:128], axis=AX.X)
    p.v(tmaxred)
    m_t4 = p.mark("v")

    def tmaxtr2(t):
        t.wait_ge(p.auto["v"], m_t4)
        t.transpose(P[1][0:1, 0:4], t4[0:4, 0:1],
                    id_sb[0:4, 0:4]).then_inc(sMM, 1)
        mm_cnt[0] += 1
    p.t(tmaxtr2)
    mm_t2 = mm_cnt[0]

    def trowev(v):
        v.wait_ge(sMM, mm_t2)
        v.tensor_copy(out=trow[0:1, :], in_=P[1][0:1, 0:4])
    p.v(trowev)
    m_trow = p.mark("v")

    def tbcast(t):
        t.wait_ge(p.auto["v"], m_trow)
        MM(t, P[1][:, 0:4], or_sb[:, :], trow[0:1, :])
    p.t(tbcast)
    mm_t3 = mm_cnt[0]

    def thr(v):
        v.wait_ge(sMM, mm_t3)
        v.tensor_copy(out=tb[:, :], in_=P[1][:, 0:4])
        for ch in range(CH):
            for j in range(7):
                v.tensor_scalar_mul(tbj[:, ch * 7 + j:ch * 7 + j + 1],
                                    tb[:, ch:ch + 1], 2.0 ** (j - 7))
        v.memset(idxq[:, :], 0.0)
        for ch in range(CH):
            for j in range(7):
                v.scalar_tensor_tensor(
                    out=idxq[:, ch * NDC:(ch + 1) * NDC],
                    in0=cmx[:, ch * NDC:(ch + 1) * NDC],
                    scalar=tbj[:, ch * 7 + j:ch * 7 + j + 1],
                    in1=idxq[:, ch * NDC:(ch + 1) * NDC],
                    op0=ALU.is_gt, op1=ALU.add)
    p.v(thr)
    m_idx = p.mark("v")

    def p2i(s):
        s.wait_mark("v", m_idx)
        s.activation(out=idxq[:, :], in_=idxq[:, :], func=AF.Exp, scale=LN2C)
    p.s(p2i)
    m_p2i = p.mark("s")

    def mkscale(v):
        v.wait_mark("s", m_p2i)
        for ch in range(CH):
            v.tensor_scalar(out=scq[:, ch * NDC:(ch + 1) * NDC],
                            in0=idxq[:, ch * NDC:(ch + 1) * NDC],
                            scalar1=tb[:, ch:ch + 1], scalar2=1.0 / 896.0,
                            op0=ALU.mult, op1=ALU.mult)
        v.tensor_scalar_max(scq[:, :], scq[:, :], 1e-9)
        v.reciprocal(rscq[:, :], scq[:, :])
        v.tensor_copy(out=chb_bf[:, :], in_=chb[:, :])
    p.v(mkscale)

    for dc in range(NDC):
        def rq(v, dc=dc):
            base = dc * TPC
            mbuf, mo = mbase(dc)
            for ch in range(CH):
                v.tensor_scalar(out=lnscr[:, ch * 256:(ch + 1) * 256],
                                in0=src_sb[:, base + ch * 256:
                                           base + (ch + 1) * 256],
                                scalar1=chb[:, dc * 4 + ch:dc * 4 + ch + 1],
                                scalar2=rscq[:, ch * NDC + dc:ch * NDC + dc + 1],
                                op0=ALU.subtract, op1=ALU.mult)
            v.tensor_scalar_add(mbuf[:, mo:mo + TPC], lnscr[:, 0:TPC], LAD4_C)
            _emit_ladder(v, [(mbuf[:, mo:mo + 512],
                              dst_bf[:, base:base + 512]),
                             (mbuf[:, mo + 512:mo + TPC],
                              dst_bf[:, base + 512:base + TPC])], LAD4)
            v.scalar_tensor_tensor(out=lnscr[:, 0:TPC],
                                   in0=mbuf[:, mo:mo + TPC], scalar=-1.0,
                                   in1=lnscr[:, 0:TPC],
                                   op0=ALU.mult, op1=ALU.add)
            for ch in range(CH):
                v.tensor_scalar(out=dst_bf[:, base + ch * 256:
                                           base + (ch + 1) * 256],
                                in0=lnscr[:, ch * 256:(ch + 1) * 256],
                                scalar1=0.5,
                                scalar2=scq[:, ch * NDC + dc:ch * NDC + dc + 1],
                                op0=ALU.add, op1=ALU.mult)
        p.v(rq)
    return p.mark("v")


def _build_C2():
    nc = bass.Bass("TRN2", target_bir_lowering=False)
    inp = lambda n, sh, dt=F32: nc.dram_tensor(n, sh, dt, kind="ExternalInput").ap()
    x2in = inp("x2in", [D, TPC])
    qit_d = [inp("qit0", [D, D], BF16), inp("qit1", [D, D], BF16),
             inp("qit2", [D, D], BF16)]
    wtr_d = [inp("wtbf0", [D, D], BF16), inp("wtbf1", [D, D], BF16),
             inp("wtbf2", [D, D], BF16)]
    winsc = inp("winsc", [128, 4 * NDC])
    bcols = [inp("qbc", [128, NDC]), inp("kbc", [128, NDC]),
             inp("vbc", [128, NDC])]
    ident = inp("ident", [128, 128])
    onesrow = inp("onesrow", [1, 128])
    constv = inp("constv", [128, len(CONSTS_C)])
    o_q = nc.dram_tensor("o_q", [D, TPC], BF16, kind="ExternalOutput").ap()
    o_k = nc.dram_tensor("o_k", [D, TPC], BF16, kind="ExternalOutput").ap()
    o_v = nc.dram_tensor("o_v", [TPC, D], BF16, kind="ExternalOutput").ap()

    SC = 0.125
    st = ExitStack()
    E = st.enter_context
    x2 = E(nc.sbuf_tensor("x2", [128, NDC * TPC], F32))
    resbf = E(nc.sbuf_tensor("resbf", [128, NDC * TPC], BF16))
    lnscr = E(nc.sbuf_tensor("lnscr", [128, TPC], F32))
    rows = E(nc.sbuf_tensor("rows", [1, 4 * TPC], F32))
    id_sb = E(nc.sbuf_tensor("id_sb", [128, 128], F32))
    or_sb = E(nc.sbuf_tensor("or_sb", [1, 128], F32))
    cst_sb = E(nc.sbuf_tensor("cst_sb", [128, len(CONSTS_C)], F32))
    bq_sb = E(nc.sbuf_tensor("bq_sb", [128, 3 * NDC], F32))
    chb = E(nc.sbuf_tensor("chb", [128, NDC * 4], F32))
    chb_bf = E(nc.sbuf_tensor("chb_bf", [128, NDC * 4], BF16))
    cmx = E(nc.sbuf_tensor("cmx", [128, 4 * NDC], F32))
    statq = E(nc.sbuf_tensor("statq", [128, 16], F32))
    tmx = E(nc.sbuf_tensor("tmx", [128, 4], F32))
    t4 = E(nc.sbuf_tensor("t4", [4, 1], F32))
    trow = E(nc.sbuf_tensor("trow", [1, 4], F32))
    tb = E(nc.sbuf_tensor("tb", [128, 4], F32))
    tbj = E(nc.sbuf_tensor("tbj", [128, 4 * 7], F32))
    idxq = E(nc.sbuf_tensor("idxq", [128, 4 * NDC], F32))
    scq = E(nc.sbuf_tensor("scq", [128, 4 * NDC], F32))
    rscq = E(nc.sbuf_tensor("rscq", [128, 4 * NDC], F32))
    ws = E(nc.sbuf_tensor("ws", [128, 4 * NDC], F32))
    wsx = E(nc.sbuf_tensor("wsx", [128, 3 * NDC], F32))
    cbw = E(nc.sbuf_tensor("cbw", [128, 3 * NDC * 4], F32))
    corr_s = E(nc.sbuf_tensor("corr_s", [4, 2048], F32))
    qv_sb = E(nc.sbuf_tensor("qv_sb", [128, 2 * TPC], BF16))
    vtm = E(nc.sbuf_tensor("vtm", [128, 2 * TPC], BF16))
    id_bf = E(nc.sbuf_tensor("id_bf", [128, 128], BF16))
    P = [E(nc.psum_tensor(f"P{i}", [128, 512], F32)) for i in range(8)]

    for i, v in enumerate(CONSTS_C):
        nc.const_aps.register(F32, v, cst_sb[:, i:i + 1])

    p = Prog(nc, st)
    sLc = p.sem("sLc")
    sLx2 = p.sem("sLx2")
    sMM = p.sem("sMM")
    sTi = p.sem("sTi")
    sOutQ = [p.sem("sOutQ0"), p.sem("sOutQ1")]
    sOutV = [p.sem("sOutV0"), p.sem("sOutV1")]
    mm_cnt = [0]

    def MM(t, out_ap, lhs, rhs, start=True, stop=True):
        t.matmul(out_ap, lhs, rhs, start=start, stop=stop).then_inc(sMM, 1)
        mm_cnt[0] += 1

    def consts(y):
        for dst, srcd in ((id_sb, ident), (or_sb, onesrow), (cst_sb, constv),
                          (ws, winsc)):
            y.dma_start(out=dst[:, :], in_=srcd).then_inc(sLc, 16)
        for i in range(3):
            y.dma_start(out=bq_sb[:, i * NDC:(i + 1) * NDC],
                        in_=bcols[i]).then_inc(sLc, 16)
        y.dma_start(out=x2[:, :],
                    in_=x2in.rearrange("(c p) t -> p c t", p=128)
                    ).then_inc(sLx2, 16)
    p.y(consts)
    NLC = 16 * 7

    def x2wait(v):
        v.wait_ge(sLx2, 16)
        v.wait_ge(sLc, NLC)
        v.tensor_copy(out=id_bf[:, :], in_=id_sb[:, :])
    p.v(x2wait)

    _ln_fm(p, P, MM, mm_cnt, sMM, cst_sb, or_sb, rows, lnscr, x2, x2, 0,
           first_sq_waits=((sLx2, 16), (sLc, NLC)))
    mm_ln = mm_cnt[0]

    tq_bufs = (chb, chb_bf, cmx, statq, tmx, t4, trow, tb, tbj, idxq, scq,
               rscq)
    _tender_quant(p, P, MM, mm_cnt, sMM, id_sb, or_sb, lnscr, tq_bufs,
                  x2, resbf, mm_ln, mbase=lambda dc: (x2, dc * TPC))
    m_res = p.mark("v")

    m_wq = p.mark("v")

    def wsx_prep(v):
        v.wait_ge(sLc, NLC)
        v.tensor_scalar_mul(wsx[:, 0:NDC], ws[:, 0:NDC], SC)
        v.tensor_copy(out=wsx[:, NDC:3 * NDC], in_=ws[:, NDC:3 * NDC])
    p.v(wsx_prep)

    ph_c = ExitStack()
    wtb = ph_c.enter_context(nc.sbuf_tensor("wtb", [128, 2 * 2048], BF16))
    wt_mark = {}
    cbw_done = {}
    for w in range(3):
        for dc in range(NDC):
            ci = w * NDC + dc
            slot = ci % 2

            def wtload(g, w=w, dc=dc, ci=ci, slot=slot):
                if ci < 2:
                    g.wait_ge(p.auto["v"], m_wq)
                if ci >= 2:
                    g.wait_ge(sMM, wt_mark.get(ci - 2, 0))
                g.dma_start(out=wtb[:, slot * 2048:(slot + 1) * 2048],
                            in_=wtr_d[w][dc * 128:(dc + 1) * 128, :])
            p.g(wtload)
            gm = p.mark("g")

            def cmm(t, w=w, dc=dc, ci=ci, slot=slot, gm=gm):
                p.wait_mark(t, "g", gm)
                t.wait_ge(p.auto["v"], m_res)
                if dc == 0:
                    t.wait_ge(p.auto["v"], cbw_done.get(w - 1, 0))
                for og in range(4):
                    MM(t, P[2 + og][0:4, :], chb_bf[:, dc * 4:(dc + 1) * 4],
                       wtb[:, slot * 2048 + og * 512:slot * 2048 + (og + 1) * 512],
                       start=(dc == 0), stop=(dc == NDC - 1))
                wt_mark[ci] = mm_cnt[0]
            p.t(cmm)

        def cevac(s, w=w):
            s.wait_ge(sMM, wt_mark[w * NDC + NDC - 1])
            for og in range(4):
                s.activation(out=corr_s[:, og * 512:(og + 1) * 512],
                             in_=P[2 + og][0:4, :], func=AF.Copy)
        p.s(cevac)
        m_ce = p.mark("s")

        def ctr(t, w=w, m_ce=m_ce):
            t.wait_ge(p.auto["s"], m_ce)
            if w > 0:
                t.wait_ge(p.auto["v"], cbw_done[w - 1])
            last = None
            for oc in range(NDC):
                last = t.transpose(P[6][:, oc * 4:(oc + 1) * 4],
                                   corr_s[0:4, oc * 128:(oc + 1) * 128],
                                   id_sb[0:4, 0:4])
            last.then_inc(sMM, 1)
            mm_cnt[0] += 1
        p.t(ctr)
        mm_ct = mm_cnt[0]

        def cbias(v, w=w, mm_ct=mm_ct):
            v.wait_ge(sMM, mm_ct)
            for oc in range(NDC):
                v.tensor_scalar(
                    out=cbw[:, (w * NDC + oc) * 4:(w * NDC + oc) * 4 + 4],
                    in0=P[6][:, oc * 4:(oc + 1) * 4],
                    scalar1=bq_sb[:, w * NDC + oc:w * NDC + oc + 1],
                    scalar2=SC if w == 0 else 1.0,
                    op0=ALU.add, op1=ALU.mult)
            cbw_done[w] = p.cnt["v"]
        p.v(cbias)
    ph_c.close()
    g_corr_end = p.mark("g")

    ph_q = ExitStack()
    qiT = ph_q.enter_context(nc.sbuf_tensor("qiT", [128, NDC * 2048], BF16))
    ep_mark = {}
    vtr_mark = {}
    for w in range(3):
        def tload(y, w=w):
            if w == 0:
                p.wait_mark(y, "g", g_corr_end)
                y.wait_ge(p.auto["v"], cbw_done[2])
            if w > 0:
                for oc_ in (NDC - 2, NDC - 1):
                    y.wait_ge(sMM, ep_mark.get((w - 1, oc_, "mm"), 0))
            for dc in range(NDC):
                y.dma_start(
                    out=qiT[:, dc * 2048:(dc + 1) * 2048],
                    in_=qit_d[w][dc * 128:(dc + 1) * 128, :]
                ).then_inc(sTi, 16)
        p.y(tload)
        m_ti = 16 * NDC * (w + 1)

        for oc in range(NDC):
            bk = [4 + 2 * (oc % 2), 5 + 2 * (oc % 2)]

            def qmm(t, w=w, oc=oc, bk=bk):
                t.wait_ge(sTi, m_ti)
                t.wait_ge(p.auto["v"], m_res)
                if oc >= 2:
                    t.wait_ge(p.auto["v"], ep_mark.get((w, oc - 2, "v"), 0))
                elif oc < 2 and w > 0:
                    t.wait_ge(p.auto["v"],
                              ep_mark.get((w - 1, NDC - 2 + oc, "v"), 0))
                for dc in range(NDC):
                    for h in range(2):
                        MM(t, P[bk[h]][:, :],
                           qiT[:, dc * 2048 + oc * 128:dc * 2048 + (oc + 1) * 128],
                           resbf[:, dc * TPC + h * 512:dc * TPC + (h + 1) * 512],
                           start=(dc == 0), stop=(dc == NDC - 1))
                ep_mark[(w, oc, "mm")] = mm_cnt[0]
            p.t(qmm)

            if w < 2:
                def epi(v, w=w, oc=oc, bk=bk):
                    v.wait_ge(sMM, ep_mark[(w, oc, "mm")])
                    if (w * NDC + oc) >= 2:
                        pw, poc = divmod(w * NDC + oc - 2, NDC)
                        v.wait_ge(sOutQ[oc % 2], 16 * ((w * NDC + oc) // 2))
                    for h in range(2):
                        for c2 in range(2):
                            ch = 2 * h + c2
                            v.tensor_scalar(
                                out=qv_sb[:, (oc % 2) * TPC + ch * 256:
                                          (oc % 2) * TPC + (ch + 1) * 256],
                                in0=P[bk[h]][:, c2 * 256:(c2 + 1) * 256],
                                scalar1=wsx[:, w * NDC + oc:w * NDC + oc + 1],
                                scalar2=cbw[:, (w * NDC + oc) * 4 + ch:
                                            (w * NDC + oc) * 4 + ch + 1],
                                op0=ALU.mult, op1=ALU.add)
                    ep_mark[(w, oc, "v")] = p.cnt["v"]
                p.v(epi)

                def qkout(y, w=w, oc=oc):
                    y.wait_ge(p.auto["v"], ep_mark[(w, oc, "v")])
                    dst = o_q if w == 0 else o_k
                    y.dma_start(out=dst[oc * 128:(oc + 1) * 128, :],
                                in_=qv_sb[:, (oc % 2) * TPC:(oc % 2 + 1) * TPC]
                                ).then_inc(sOutQ[oc % 2], 16)
                p.y(qkout)
            else:
                def epiv(v, w=w, oc=oc, bk=bk):
                    v.wait_ge(sMM, ep_mark[(w, oc, "mm")])
                    if oc >= 2:
                        v.wait_ge(sMM, ep_mark[(2, oc - 2, "vtr")])
                    else:
                        v.wait_ge(sOutQ[0], 16 * 16)
                        v.wait_ge(sOutQ[1], 16 * 16)
                    for h in range(2):
                        for c2 in range(2):
                            ch = 2 * h + c2
                            v.tensor_scalar(
                                out=qv_sb[:, (oc % 2) * TPC + ch * 256:
                                          (oc % 2) * TPC + (ch + 1) * 256],
                                in0=P[bk[h]][:, c2 * 256:(c2 + 1) * 256],
                                scalar1=wsx[:, w * NDC + oc:w * NDC + oc + 1],
                                scalar2=cbw[:, (w * NDC + oc) * 4 + ch:
                                            (w * NDC + oc) * 4 + ch + 1],
                                op0=ALU.mult, op1=ALU.add)
                    ep_mark[(w, oc, "v")] = p.cnt["v"]
                p.v(epiv)

                def vtr(t, oc=oc):
                    t.wait_ge(p.auto["v"], ep_mark[(2, oc, "v")])
                    if oc > 0:
                        t.wait_ge(p.auto["s"], vtr_mark[oc - 1])
                    pbf = P[0][:, :].bitcast(BF16)
                    last = None
                    for tt in range(8):
                        last = t.transpose(
                            pbf[:, tt * 128:(tt + 1) * 128],
                            qv_sb[:, (oc % 2) * TPC + tt * 128:
                                  (oc % 2) * TPC + (tt + 1) * 128],
                            id_bf[:, :])
                    last.then_inc(sMM, 1)
                    mm_cnt[0] += 1
                    ep_mark[(2, oc, "vtr")] = mm_cnt[0]
                p.t(vtr)
                mm_vt = mm_cnt[0]

                def vev(s, oc=oc, mm_vt=mm_vt):
                    s.wait_ge(sMM, mm_vt)
                    if oc >= 2:
                        s.wait_ge(sOutV[oc % 2], 16 * (oc // 2))
                    pbf = P[0][:, :].bitcast(BF16)
                    s.activation(out=vtm[:, (oc % 2) * TPC:(oc % 2 + 1) * TPC],
                                 in_=pbf[:, :], func=AF.Copy)
                    vtr_mark[oc] = p.cnt["s"]
                p.s(vev)

                def vout(y, oc=oc):
                    y.wait_ge(p.auto["s"], vtr_mark[oc])
                    y.dma_start(
                        out=o_v[:, oc * 128:(oc + 1) * 128].rearrange(
                            "(tt p) o -> p tt o", p=128),
                        in_=vtm[:, (oc % 2) * TPC:(oc % 2 + 1) * TPC].rearrange(
                            "p (tt o) -> p tt o", o=128)
                    ).then_inc(sOutV[oc % 2], 16)
                p.y(vout)
    ph_q.close()

    def finish(y):
        y.wait_ge(sOutQ[0], 16 * 16)
        y.wait_ge(sOutQ[1], 16 * 16)
        y.wait_ge(sOutV[0], 16 * 8)
        y.wait_ge(sOutV[1], 16 * 8)
    p.y(finish)

    st.close()
    return nc


CONSTS_D = [LN_EPS, -0.5, LN2C, 1.0, 1e-6]


def _build_D():
    nc = bass.Bass("TRN2", target_bir_lowering=False)
    inp = lambda n, sh, dt=F32: nc.dram_tensor(n, sh, dt, kind="ExternalInput").ap()
    x2f = inp("x2f", [D, TPC])
    qf_d = inp("qf", [D, TPC], BF16)
    kf_d = inp("kf", [D, T], BF16)
    vt_d = inp("vt", [T, D], BF16)
    oqit = inp("oqit", [D, D], BF16)
    osc_d = inp("osc", [128, NDC])
    owt_d = inp("owbf", [D, D], BF16)
    obc_d = inp("obc", [128, NDC])
    wq_scr = [inp("w1qt", [D, 4 * D], BF16), inp("w2qt", [D, 4 * D], BF16),
              inp("w3qt", [4 * D, D], BF16)]
    wmb_d = inp("wmb", [128, 4])
    ident = inp("ident", [128, 128])
    onesrow = inp("onesrow", [1, 128])
    oab_d = inp("oab", [128, 65])
    or64_d = inp("or64", [65, 128])
    constv = inp("constv", [128, len(CONSTS_D)])
    o_x4 = nc.dram_tensor("o_x4", [D, TPC], F32, kind="ExternalOutput").ap()

    st = ExitStack()
    E = st.enter_context
    x2sb = E(nc.sbuf_tensor("x2sb", [128, NDC * TPC], F32))
    lnscr = E(nc.sbuf_tensor("lnscr", [128, TPC], F32))
    rows = E(nc.sbuf_tensor("rows", [1, 3 * TPC], F32))
    id_sb = E(nc.sbuf_tensor("id_sb", [128, 128], F32))
    id_bf = E(nc.sbuf_tensor("id_bf", [128, 128], BF16))
    or_sb = E(nc.sbuf_tensor("or_sb", [1, 128], F32))
    oab_sb = E(nc.sbuf_tensor("oab_sb", [128, 65], F32))
    or64_sb = E(nc.sbuf_tensor("or64_sb", [65, 128], F32))
    cst_sb = E(nc.sbuf_tensor("cst_sb", [128, len(CONSTS_D)], F32))
    osc_sb = E(nc.sbuf_tensor("osc_sb", [128, NDC], F32))
    obc_sb = E(nc.sbuf_tensor("obc_sb", [128, NDC], F32))
    chb = E(nc.sbuf_tensor("chb", [128, NDC * 4], F32))
    chb_bf = E(nc.sbuf_tensor("chb_bf", [128, NDC * 4], BF16))
    cmx = E(nc.sbuf_tensor("cmx", [128, 4 * NDC], F32))
    statq = E(nc.sbuf_tensor("statq", [128, 16], F32))
    tmx = E(nc.sbuf_tensor("tmx", [128, 4], F32))
    t4 = E(nc.sbuf_tensor("t4", [4, 1], F32))
    trow = E(nc.sbuf_tensor("trow", [1, 4], F32))
    tb = E(nc.sbuf_tensor("tb", [128, 4], F32))
    tbj = E(nc.sbuf_tensor("tbj", [128, 4 * 7], F32))
    idxq = E(nc.sbuf_tensor("idxq", [128, 4 * NDC], F32))
    scq = E(nc.sbuf_tensor("scq", [128, 4 * NDC], F32))
    rscq = E(nc.sbuf_tensor("rscq", [128, 4 * NDC], F32))
    cbo = E(nc.sbuf_tensor("cbo", [128, NDC * 4], F32))
    wsc = E(nc.sbuf_tensor("wsc", [128, 8], F32))
    P = [E(nc.psum_tensor(f"P{i}", [128, 512], F32)) for i in range(4)]
    SA2 = E(nc.psum_tensor("SA2", [128, 1024], F32))
    SB2 = E(nc.psum_tensor("SB2", [128, 1024], F32))

    phCtx = ExitStack()
    ctxf = phCtx.enter_context(nc.sbuf_tensor("ctxf", [128, NDC * TPC], BF16))
    PL = P + [SA2[:, 0:512], SA2[:, 512:1024], SB2[:, 0:512], SB2[:, 512:1024]]
    for i, v in enumerate(CONSTS_D):
        nc.const_aps.register(F32, v, cst_sb[:, i:i + 1])

    p = Prog(nc, st)
    sLc = p.sem("sLc")
    sLx2 = p.sem("sLx2")
    sMM = p.sem("sMM")
    sKL = [p.sem("sKL0"), p.sem("sKL1")]
    sVA = [p.sem("sVA0"), p.sem("sVA1")]
    sQL = p.sem("sQL")
    sOQ = [p.sem("sOQ0"), p.sem("sOQ1")]
    sOW = [p.sem("sOW0"), p.sem("sOW1")]
    sOut = [p.sem("sOutD0"), p.sem("sOutD1")]
    sCtxB = p.sem("sCtxB")
    mm_cnt = [0]

    def MM(t, out_ap, lhs, rhs, start=True, stop=True):
        t.matmul(out_ap, lhs, rhs, start=start, stop=stop).then_inc(sMM, 1)
        mm_cnt[0] += 1

    def consts(y):
        for dst, srcd in ((id_sb, ident), (or_sb, onesrow), (cst_sb, constv),
                          (oab_sb, oab_d), (osc_sb, osc_d), (obc_sb, obc_d),
                          (or64_sb, or64_d)):
            y.dma_start(out=dst[:, :], in_=srcd).then_inc(sLc, 16)
        y.dma_start(out=wsc[:, 4:8], in_=wmb_d).then_inc(sLc, 16)
        y.dma_start(out=x2sb[:, :],
                    in_=x2f.rearrange("(c p) t -> p c t", p=128)
                    ).then_inc(sLx2, 16)
    p.y(consts)
    NLC = 16 * 8

    phA = ExitStack()
    EA = phA.enter_context
    qsb = EA(nc.sbuf_tensor("qsb", [128, NDC * TPC], BF16))
    khp = EA(nc.sbuf_tensor("khp", [128, T], BF16))
    vaA = EA(nc.sbuf_tensor("vaA", [128, 32 * 65], BF16))
    vaB = EA(nc.sbuf_tensor("vaB", [128, 32 * 65], BF16))
    ptA = EA(nc.sbuf_tensor("ptA", [128, 2 * 1024], BF16))
    ptB = EA(nc.sbuf_tensor("ptB", [128, 2 * 1024], BF16))
    sqs = EA(nc.sbuf_tensor("sqs", [128, T], F32))
    qsq = EA(nc.sbuf_tensor("qsq", [128, TPC], F32))
    nrm = EA(nc.sbuf_tensor("nrm", [128, 8], F32))
    mcols = EA(nc.sbuf_tensor("mcols", [128, 2], F32))
    rl_sb = EA(nc.sbuf_tensor("rl_sb", [1, 2 * 512], F32))
    rlb = EA(nc.sbuf_tensor("rlb", [128, 2 * 512], F32))
    ctmp = EA(nc.sbuf_tensor("ctmp", [64, 512], BF16))

    def qload(y):
        y.dma_start(out=qsb[:, :],
                    in_=qf_d.rearrange("(c p) t -> p c t", p=128)
                    ).then_inc(sQL, 16)
    p.y(qload)

    def idcast(v):
        v.wait_ge(sLc, NLC)
        v.tensor_copy(out=id_bf[:, :], in_=id_sb[:, :])
        for ti in range(32):
            v.memset(vaA[:, ti * 65 + 64:ti * 65 + 65], 1.0)
            v.memset(vaB[:, ti * 65 + 64:ti * 65 + 65], 1.0)
    p.v(idcast)
    m_idc = p.mark("v")

    ev_ctx = {}
    exp_mark = {}
    sv_done = {}

    for hp in range(16):
        slot = hp % 2

        def kvload(y, hp=hp, slot=slot):
            if hp >= 1:
                y.wait_ge(sMM, sv_done[(hp - 1, "smm")])
            else:
                y.wait_ge(p.auto["v"], m_idc)
            y.dma_start(out=khp[:, :],
                        in_=kf_d[hp * 128:(hp + 1) * 128, :]).then_inc(sKL[0], 16)
            if hp >= 1:
                y.wait_ge(sMM, sv_done[(hp - 1, "pvmm")])
            for hh, va in ((0, vaA), (1, vaB)):
                y.dma_start(
                    out=va.ap().rearrange("p (ti e) -> p ti e", e=65)[:, :, 0:64],
                    in_=vt_d[:, (2 * hp + hh) * 64:(2 * hp + hh + 1) * 64]
                    .rearrange("(ti p) o -> p ti o", p=128)
                ).then_inc(sVA[0], 16)
        p.y(kvload)

        def msq(s, hp=hp, slot=slot):
            s.wait_ge(sQL, 16)
            s.wait_ge(sLc, NLC)
            s.activation(out=qsq[:, :], in_=qsb[:, hp * TPC:(hp + 1) * TPC],
                         func=AF.Square)
            s.wait_ge(sKL[0], 16 * (hp + 1))
            s.activation(out=sqs[:, :], in_=khp[:, :], func=AF.Square)
        p.s(msq)
        m_msq = p.mark("s")

        def mnorm_mm(t, hp=hp, slot=slot):
            t.wait_ge(p.auto["s"], m_msq)
            if hp >= 1:
                t.wait_ge(p.auto["s"], exp_mark.get((hp - 1, "last"), 0))
            for h in range(2):
                MM(t, SA2[0:65, h * 512:(h + 1) * 512], oab_sb[:, :],
                   qsq[:, h * 512:(h + 1) * 512])
        p.t(mnorm_mm)
        mm_q = mm_cnt[0]

        def mq_red(v, hp=hp):
            v.wait_ge(sMM, mm_q)
            v.reduce_max(nrm[0:1, 0:1], SA2[0:1, 0:1024], axis=AX.X)
            v.reduce_max(nrm[64:65, 0:1], SA2[64:65, 0:1024], axis=AX.X)
        p.v(mq_red)
        m_nq = p.mark("v")

        knred_mark = {}
        for j in range(8):
            def knorm_mm(t, hp=hp, j=j):
                if j < 2:
                    t.wait_ge(p.auto["v"], m_nq)
                else:
                    t.wait_ge(p.auto["v"], knred_mark[j - 2])
                MM(t, SA2[0:65, (j % 2) * 512:(j % 2) * 512 + 512], oab_sb[:, :],
                   sqs[:, j * 512:(j + 1) * 512])
            p.t(knorm_mm)
            mm_kj = mm_cnt[0]

            def kn_red(v, hp=hp, j=j, mm_kj=mm_kj):
                v.wait_ge(sMM, mm_kj)
                for r in (slice(0, 1), slice(64, 65)):
                    v.reduce_max(nrm[r, 2 + (j % 2):3 + (j % 2)],
                                 SA2[r, (j % 2) * 512:(j % 2) * 512 + 512],
                                 axis=AX.X)
                    if j == 0:
                        v.tensor_copy(out=nrm[r, 4:5], in_=nrm[r, 2:3])
                    else:
                        v.tensor_max(nrm[r, 4:5], nrm[r, 4:5],
                                     nrm[r, 2 + (j % 2):3 + (j % 2)])
                knred_mark[j] = p.cnt["v"]
            p.v(kn_red)

        def kn_fin(v, hp=hp):
            for r in (slice(0, 1), slice(64, 65)):
                v.tensor_mul(nrm[r, 5:6], nrm[r, 0:1], nrm[r, 4:5])
        p.v(kn_fin)
        m_kn = p.mark("v")

        def msqrt(s, hp=hp):
            s.wait_mark("v", m_kn)
            s.activation(out=nrm[0:1, 6:7], in_=nrm[0:1, 5:6], func=AF.Sqrt)
            s.activation(out=nrm[64:65, 6:7], in_=nrm[64:65, 5:6], func=AF.Sqrt)
        p.s(msqrt)
        m_ms = p.mark("s")

        def mneg(v, hp=hp):
            v.wait_mark("s", m_ms)
            v.tensor_scalar_mul(nrm[0:1, 7:8], nrm[0:1, 6:7], -1.0)
            v.tensor_scalar_mul(nrm[64:65, 7:8], nrm[64:65, 6:7], -1.0)
        p.v(mneg)
        m_mn = p.mark("v")

        def mbcast(t, hp=hp):
            t.wait_ge(p.auto["v"], m_mn)
            MM(t, SA2[:, 0:1], or_sb[:, :], nrm[0:1, 7:8])
            MM(t, SA2[:, 512:513], or64_sb[64:65, :], nrm[64:65, 7:8])
        p.t(mbcast)
        mm_mb = mm_cnt[0]

        def mevac(v, hp=hp):
            v.wait_ge(sMM, mm_mb)
            v.tensor_copy(out=mcols[:, 0:1], in_=SA2[:, 0:1])
            v.tensor_copy(out=mcols[:, 1:2], in_=SA2[:, 512:513])
        p.v(mevac)
        m_mc = p.mark("v")

        for tq in range(2):
            for tkp in range(16):
                li = (hp * 2 + tq) * 16 + tkp

                def smm(t, hp=hp, slot=slot, tq=tq, tkp=tkp, li=li):
                    t.wait_ge(p.auto["v"], m_mc)
                    if tkp >= 1:
                        t.wait_ge(p.auto["s"], exp_mark[(li - 1, "e")])
                    elif (hp, tq) != (0, 0):
                        t.wait_ge(p.auto["s"], exp_mark[(li - 1, "e")])
                    for u in range(2):
                        tkt = 2 * tkp + u
                        for hh, bank in ((0, SA2), (1, SB2)):
                            MM(t, bank[:, u * 512:(u + 1) * 512],
                               khp[hh * 64:(hh + 1) * 64,
                                   tkt * 128:(tkt + 1) * 128],
                               qsb[hh * 64:(hh + 1) * 64,
                                   hp * TPC + tq * 512: hp * TPC + (tq + 1) * 512]
                               if False else
                               qsb[:, hp * TPC + tq * 512:
                                   hp * TPC + (tq + 1) * 512][hh * 64:(hh + 1) * 64, :],
                               start=True, stop=True)
                    sv_done[(hp, "smm")] = mm_cnt[0]
                p.t(smm)
                mm_s = mm_cnt[0]

                def expo(s, hp=hp, tq=tq, tkp=tkp, li=li, mm_s=mm_s):
                    s.wait_ge(sMM, mm_s)
                    if li >= 2:
                        s.wait_ge(sMM, sv_done.get((li - 2, "pv"), 0))
                    s.activation(out=ptA[:, (tkp % 2) * 1024:(tkp % 2 + 1) * 1024],
                                 in_=SA2[:, :], func=AF.Exp,
                                 bias=mcols[:, 0:1])
                    s.activation(out=ptB[:, (tkp % 2) * 1024:(tkp % 2 + 1) * 1024],
                                 in_=SB2[:, :], func=AF.Exp,
                                 bias=mcols[:, 1:2])
                    exp_mark[(li, "e")] = p.cnt["s"]
                    exp_mark[(hp, "last")] = p.cnt["s"]
                p.s(expo)

                def pvmm(t, hp=hp, slot=slot, tq=tq, tkp=tkp, li=li):
                    t.wait_ge(p.auto["s"], exp_mark[(li, "e")])
                    t.wait_ge(sVA[0], 32 * (hp + 1))
                    if tkp == 0:
                        prev = (hp, 0) if tq == 1 else (hp - 1, 1)
                        t.wait_ge(p.auto["v"], ev_ctx.get(prev, 0))
                    for u in range(2):
                        tkt = 2 * tkp + u
                        for hh, va, pt in ((0, vaA, ptA), (1, vaB, ptB)):
                            MM(t, P[2 + hh][0:65, :],
                               va[:, tkt * 65:(tkt + 1) * 65],
                               pt[:, (tkp % 2) * 1024 + u * 512:
                                  (tkp % 2) * 1024 + (u + 1) * 512],
                               start=(tkt == 0), stop=(tkt == 31))
                    sv_done[(li, "pv")] = mm_cnt[0]
                    sv_done[(hp, "pvmm")] = mm_cnt[0]
                p.t(pvmm)

            mm_pv = mm_cnt[0]

            def lrec(v, hp=hp, tq=tq, mm_pv=mm_pv):
                v.wait_ge(sMM, mm_pv)
                for hh in range(2):
                    v.tensor_copy(out=rl_sb[0:1, hh * 512:(hh + 1) * 512],
                                  in_=P[2 + hh][64:65, :])
                v.reciprocal(rl_sb[0:1, :], rl_sb[0:1, :])
            p.v(lrec)
            m_lr = p.mark("v")

            def lbcast(t, hp=hp, tq=tq, m_lr=m_lr):
                t.wait_ge(p.auto["v"], m_lr)
                prev = (hp, 0) if tq == 1 else (hp - 1, 1)
                t.wait_ge(p.auto["v"], ev_ctx.get(prev, 0))
                for hh in range(2):
                    MM(t, P[hh][:, :], or_sb[:, :],
                       rl_sb[0:1, hh * 512:(hh + 1) * 512])
            p.t(lbcast)
            mm_lb = mm_cnt[0]

            def ctxev(v, hp=hp, tq=tq, mm_lb=mm_lb):
                idx = hp * 2 + tq
                v.wait_ge(sMM, mm_lb)
                for hh in range(2):
                    v.tensor_copy(out=rlb[:, hh * 512:(hh + 1) * 512],
                                  in_=P[hh][:, :])
                cs = slice(hp * TPC + tq * 512, hp * TPC + (tq + 1) * 512)
                v.tensor_mul(ctxf[0:64, cs], P[2][0:64, :], rlb[0:64, 0:512])
                if idx >= 1:
                    v.wait_ge(sCtxB, 16 * idx)
                v.tensor_mul(ctmp[:, :], P[3][0:64, :], rlb[0:64, 512:1024])
                ev_ctx[(hp, tq)] = p.cnt["v"]
                ev_ctx[hp] = p.cnt["v"]
            p.v(ctxev)

            def ctxb(y, hp=hp, tq=tq):
                y.wait_ge(p.auto["v"], ev_ctx[(hp, tq)])
                cs = slice(hp * TPC + tq * 512, hp * TPC + (tq + 1) * 512)
                y.dma_start(out=ctxf[64:128, cs],
                            in_=ctmp[:, :]).then_inc(sCtxB, 16)
            p.y(ctxb)

    phA.close()

    phB = ExitStack()
    r2bf = phB.enter_context(nc.sbuf_tensor("r2bf", [128, NDC * TPC], BF16))
    tqm = phB.enter_context(nc.sbuf_tensor("tqm", [128, TPC], F32))

    def bwait(v):
        v.wait_ge(sLx2, 16)
        v.wait_ge(sCtxB, 16 * 32)
        v.tensor_copy(out=lnscr[:, 0:1], in_=ctxf[:, 0:1])
    p.v(bwait)
    tq_bufs = (chb, chb_bf, cmx, statq, tmx, t4, trow, tb, tbj, idxq, scq,
               rscq)
    m_r2 = _tender_quant(p, PL, MM, mm_cnt, sMM, id_sb, or_sb, lnscr, tq_bufs,
                         ctxf, r2bf, 0, mbase=lambda dc: (tqm, 0))

    phC = ExitStack()
    wtb = phC.enter_context(nc.sbuf_tensor("wtb", [128, 2 * 2048], BF16))
    corr_s = phC.enter_context(nc.sbuf_tensor("corr_s", [4, 2048], F32))
    wt_mark = {}
    for dc in range(NDC):
        slot = dc % 2

        def owload(g, dc=dc, slot=slot):
            if dc < 2:
                g.wait_ge(p.auto["v"], m_r2)
            if dc >= 2:
                g.wait_ge(sMM, wt_mark.get(dc - 2, 0))
            g.dma_start(out=wtb[:, slot * 2048:(slot + 1) * 2048],
                        in_=owt_d[dc * 128:(dc + 1) * 128, :])
        p.g(owload)
        gm = p.mark("g")

        def cmm(t, dc=dc, slot=slot, gm=gm):
            p.wait_mark(t, "g", gm)
            t.wait_ge(p.auto["v"], m_r2)
            for og in range(4):
                MM(t, P[og][0:4, :], chb_bf[:, dc * 4:(dc + 1) * 4],
                   wtb[:, slot * 2048 + og * 512:slot * 2048 + (og + 1) * 512],
                   start=(dc == 0), stop=(dc == NDC - 1))
            wt_mark[dc] = mm_cnt[0]
        p.t(cmm)

    def cevac(s):
        s.wait_ge(sMM, wt_mark[NDC - 1])
        for og in range(4):
            s.activation(out=corr_s[:, og * 512:(og + 1) * 512],
                         in_=P[og][0:4, :], func=AF.Copy)
    p.s(cevac)
    m_ce = p.mark("s")

    def ctr(t, m_ce=m_ce):
        t.wait_ge(p.auto["s"], m_ce)
        last = None
        for oc in range(NDC):
            last = t.transpose(SA2[:, oc * 4:(oc + 1) * 4],
                               corr_s[0:4, oc * 128:(oc + 1) * 128],
                               id_sb[0:4, 0:4])
        last.then_inc(sMM, 1)
        mm_cnt[0] += 1
    p.t(ctr)
    mm_ct = mm_cnt[0]

    def cbias(v, mm_ct=mm_ct):
        v.wait_ge(sMM, mm_ct)
        for oc in range(NDC):
            v.tensor_scalar(out=cbo[:, oc * 4:(oc + 1) * 4],
                            in0=SA2[:, oc * 4:(oc + 1) * 4],
                            scalar1=obc_sb[:, oc:oc + 1], scalar2=None,
                            op0=ALU.add)
    p.v(cbias)
    m_cbo = p.mark("v")

    oq_tile = phC.enter_context(nc.sbuf_tensor("oq_tile", [128, 2 * 2048], BF16))
    op_mark = {}
    for oc in range(NDC):
        slot = oc % 2

        def oqload(y, oc=oc, slot=slot):
            if oc < 2:
                y.wait_ge(p.auto["v"], m_r2)
            if oc >= 2:
                y.wait_ge(sMM, op_mark.get((oc - 2, "mm"), 0))
            y.dma_start(
                out=oq_tile[:, slot * 2048:(slot + 1) * 2048],
                in_=oqit.rearrange("(c p) o -> p c o", p=128)
                [:, :, oc * 128:(oc + 1) * 128]).then_inc(sOQ[slot], 16)
        p.y(oqload)

        def omm(t, oc=oc, slot=slot):
            t.wait_ge(sOQ[slot], 16 * (oc // 2 + 1))
            t.wait_ge(p.auto["v"], m_r2)
            if oc >= 2:
                t.wait_ge(p.auto["v"], op_mark.get((oc - 2, "ev"), 0))
            for h in range(2):
                for dc in range(NDC):
                    MM(t, P[2 * (oc % 2) + h][:, :],
                       oq_tile[:, slot * 2048 + dc * 128:slot * 2048 + (dc + 1) * 128],
                       r2bf[:, dc * TPC + h * 512:dc * TPC + (h + 1) * 512],
                       start=(dc == 0), stop=(dc == NDC - 1))
            op_mark[(oc, "mm")] = mm_cnt[0]
        p.t(omm)

        def oev(v, oc=oc):
            v.wait_ge(sMM, op_mark[(oc, "mm")])
            v.wait_mark("v", m_cbo)
            for h in range(2):
                for c2 in range(2):
                    ch = 2 * h + c2
                    cs = slice(oc * TPC + ch * 256, oc * TPC + (ch + 1) * 256)
                    v.tensor_scalar(out=lnscr[:, 0:256],
                                    in0=P[2 * (oc % 2) + h][:, c2 * 256:(c2 + 1) * 256],
                                    scalar1=osc_sb[:, oc:oc + 1],
                                    scalar2=cbo[:, oc * 4 + ch:oc * 4 + ch + 1],
                                    op0=ALU.mult, op1=ALU.add)
                    v.tensor_add(x2sb[:, cs], x2sb[:, cs], lnscr[:, 0:256])
            op_mark[(oc, "ev")] = p.cnt["v"]
        p.v(oev)
    phC.close()
    phB.close()
    phCtx.close()

    xqf = E(nc.sbuf_tensor("xqf", [128, NDC * TPC], BF16))
    phD = ExitStack()
    ED = phD.enter_context
    x3n = ED(nc.sbuf_tensor("x3n", [128, NDC * TPC], F32))
    xtt = ED(nc.sbuf_tensor("xtt", [128, 2048], F32))
    sqq = ED(nc.sbuf_tensor("sqq", [128, 2048], F32))
    xqt = ED(nc.sbuf_tensor("xqt", [128, 2048], BF16))

    def dwait(v):
        v.wait_ge(p.auto["v"], op_mark[(NDC - 1, "ev")])
        v.tensor_copy(out=lnscr[:, 0:1], in_=x2sb[:, 0:1])
    p.v(dwait)
    m_ln3v = _ln_fm(p, PL, MM, mm_cnt, sMM, cst_sb, or_sb, rows, lnscr,
                    x2sb, x3n, 0)

    RMS_EPS = 1e-6
    AQ_EPS = 1e-5
    xq_mark = {}
    for tt in range(NTT):
        def etr(t, tt=tt):
            if tt == 0:
                t.wait_ge(p.auto["v"], m_ln3v)
            if tt >= 1:
                t.wait_ge(p.auto["s"], xq_mark[(tt - 1, "bev")])
            last = None
            for k in range(NDC):
                last = t.transpose(
                    P[k // 4][:, (k % 4) * 128:(k % 4) * 128 + 128],
                    x3n[:, k * TPC + tt * 128:k * TPC + (tt + 1) * 128],
                    id_sb[:, :])
            last.then_inc(sMM, 1)
            mm_cnt[0] += 1
        p.t(etr)
        mm_et = mm_cnt[0]

        def eev(s, tt=tt, mm_et=mm_et):
            s.wait_ge(sMM, mm_et)
            for b in range(4):
                s.activation(out=xtt[:, b * 512:(b + 1) * 512],
                             in_=P[b][:, :], func=AF.Copy)
            s.activation(out=sqq[:, :], in_=xtt[:, :], func=AF.Square)
        p.s(eev)
        m_eev = p.mark("s")

        def equant(v, tt=tt, m_eev=m_eev):
            c = lambda k: statq[:, k:k + 1]
            v.wait_mark("s", m_eev)
            v.reduce_sum(c(0), sqq[:, :], axis=AX.X)
            v.tensor_scalar_mul(c(0), c(0), 1.0 / D)
            v.reduce_max(c(1), xtt[:, :], axis=AX.X, apply_absolute_value=True)
            xq_mark[(tt, "st")] = p.cnt["v"]
        p.v(equant)
        m_est = p.mark("v")

        def erstd(s, tt=tt, m_est=m_est):
            c = lambda k: statq[:, k:k + 1]
            s.wait_mark("v", m_est)
            s.activation(out=c(2), in_=c(0), func=AF.Ln, bias=RMS_EPS)
            s.activation(out=c(2), in_=c(2), func=AF.Exp, scale=-0.5)
        p.s(erstd)
        m_ers = p.mark("s")

        def equant2(v, tt=tt, m_ers=m_ers):
            c = lambda k: statq[:, k:k + 1]
            v.wait_mark("s", m_ers)
            v.tensor_mul(c(3), c(1), c(2))
            v.tensor_scalar_max(c(3), c(3), AQ_EPS)
            v.reciprocal(c(4), c(3))
            v.tensor_scalar_mul(c(4), c(4), 127.0)
            v.tensor_scalar_mul(c(5), c(3), 1.0 / 127.0)
            v.tensor_mul(c(6), c(2), c(4))
            v.tensor_scalar_mul(sqq[:, :], xtt[:, :], c(6))
            v.tensor_scalar_add(xtt[:, :], sqq[:, :], LAD8_C)
            _emit_ladder(v, [(xtt[:, 0:1024], xqt[:, 0:1024]),
                             (xtt[:, 1024:2048], xqt[:, 1024:2048])], LAD8)
            v.scalar_tensor_tensor(out=sqq[:, :], in0=xtt[:, :], scalar=-1.0,
                                   in1=sqq[:, :], op0=ALU.mult, op1=ALU.add)
            v.tensor_scalar(out=xqt[:, :], in0=sqq[:, :], scalar1=0.5,
                            scalar2=c(5), op0=ALU.add, op1=ALU.mult)
            xq_mark[(tt, "q")] = p.cnt["v"]
        p.v(equant2)

        def ebtr(t, tt=tt):
            t.wait_ge(p.auto["v"], xq_mark[(tt, "q")])
            pbf0 = P[0][:, :].bitcast(BF16)
            pbf1 = P[1][:, :].bitcast(BF16)
            last = None
            for k in range(NDC):
                dstp = pbf0 if k < 8 else pbf1
                last = t.transpose(dstp[:, (k % 8) * 128:(k % 8) * 128 + 128],
                                   xqt[:, k * 128:(k + 1) * 128], id_bf[:, :])
            last.then_inc(sMM, 1)
            mm_cnt[0] += 1
        p.t(ebtr)
        mm_eb = mm_cnt[0]

        def ebev(s, tt=tt, mm_eb=mm_eb):
            s.wait_ge(sMM, mm_eb)
            pbf0 = P[0][:, :].bitcast(BF16)
            pbf1 = P[1][:, :].bitcast(BF16)
            for k in range(NDC):
                dstp = pbf0 if k < 8 else pbf1
                s.activation(out=xqf[:, k * TPC + tt * 128:k * TPC + (tt + 1) * 128],
                             in_=dstp[:, (k % 8) * 128:(k % 8) * 128 + 128],
                             func=AF.Copy)
            xq_mark[(tt, "bev")] = p.cnt["s"]
        p.s(ebev)
    phD.close()

    HOC = 64
    QH = 4
    TQH = 256
    hbf = E(nc.sbuf_tensor("hbf", [128, HOC * TQH], BF16))
    sHW = [p.sem("sHW0"), p.sem("sHW1")]
    sW3 = [p.sem("sW30"), p.sem("sW31")]
    h_mark = {}
    m_xq_done = xq_mark[(NTT - 1, "bev")]

    for qh in range(QH):
        phG = ExitStack()
        wqt = phG.enter_context(nc.sbuf_tensor(f"wqt{qh}", [128, 2 * 4096], BF16))
        t1s = phG.enter_context(nc.sbuf_tensor(f"t1s{qh}", [128, TQH], F32))

        for oc in range(HOC):
            oi = qh * HOC + oc
            slot = oi % 2

            def hwload(y, qh=qh, oc=oc, oi=oi, slot=slot):
                if oi == 0 and qh == 0:
                    y.wait_ge(p.auto["s"], m_xq_done)
                if oi >= 2:
                    y.wait_ge(p.auto["v"], h_mark[("hev", oi - 2)])
                for wi in range(2):
                    flat = wq_scr[wi].rearrange("(c p) o -> p c o", p=128)
                    y.dma_start(
                        out=wqt[:, slot * 4096 + wi * 2048:
                                slot * 4096 + (wi + 1) * 2048],
                        in_=flat[:, :, oc * 128:(oc + 1) * 128]
                    ).then_inc(sHW[slot], 16)
            p.y(hwload)

            def hmm(t, qh=qh, oc=oc, oi=oi, slot=slot):
                t.wait_ge(sHW[slot], 32 * (oi // 2 + 1))
                if oi == 0:
                    t.wait_ge(p.auto["s"], m_xq_done)
                if oi >= 2:
                    t.wait_ge(p.auto["v"], h_mark[("hev", oi - 2)])
                if oc < 2 and qh > 0:
                    t.wait_ge(sMM, h_mark[("btrlast", qh - 1)])
                for wi in range(2):
                    for ic in range(NDC):
                        MM(t, P[2 * (oc % 2) + wi][:, 0:TQH],
                           wqt[:, slot * 4096 + wi * 2048 + ic * 128:
                               slot * 4096 + wi * 2048 + (ic + 1) * 128],
                           xqf[:, ic * TPC + qh * TQH:ic * TPC + (qh + 1) * TQH],
                           start=(ic == 0), stop=(ic == NDC - 1))
                h_mark[("mm", oi)] = mm_cnt[0]
            p.t(hmm)

            def hsilu(s, oc=oc, oi=oi):
                s.wait_ge(sMM, h_mark[("mm", oi)])
                if oi > 0:
                    s.wait_mark("v", h_mark[("hev", oi - 1)])
                s.activation(out=t1s[:, :], in_=P[2 * (oc % 2)][:, 0:TQH],
                             func=AF.Silu, scale=wsc[:, 4:5])
            p.s(hsilu)
            m_hs = p.mark("s")

            def hev(v, oc=oc, oi=oi, m_hs=m_hs):
                v.wait_mark("s", m_hs)
                v.scalar_tensor_tensor(
                    out=hbf[:, oc * TQH:(oc + 1) * TQH],
                    in0=P[2 * (oc % 2) + 1][:, 0:TQH], scalar=wsc[:, 5:6],
                    in1=t1s[:, :], op0=ALU.mult, op1=ALU.mult)
                h_mark[("hev", oi)] = p.cnt["v"]
            p.v(hev)
        phG.close()
        mG_v = p.mark("v")

        phQ = ExitStack()
        htt = phQ.enter_context(nc.sbuf_tensor(f"htt{qh}", [128, 8192], BF16))
        hq2 = phQ.enter_context(nc.sbuf_tensor(f"hq2{qh}", [128, 4096], F32))
        msc = phQ.enter_context(nc.sbuf_tensor(f"msc{qh}", [128, 4096], F32))

        for tl in range(2):
            for rnd in range(2):
                def hftr(t, qh=qh, tl=tl, rnd=rnd):
                    if (tl, rnd) == (0, 0):
                        t.wait_ge(p.auto["v"], mG_v)
                    else:
                        t.wait_ge(p.auto["s"], h_mark[("fev", qh, tl, rnd - 1)
                                                      if rnd == 1 else
                                                      ("bev", qh, tl - 1, 3)])
                    last = None
                    for k in range(32):
                        hc = rnd * 32 + k
                        pb = P[k // 8][:, :].bitcast(BF16)
                        last = t.transpose(
                            pb[:, (k % 8) * 128:(k % 8) * 128 + 128],
                            hbf[:, hc * TQH + tl * 128:hc * TQH + tl * 128 + 128],
                            id_bf[:, :])
                    last.then_inc(sMM, 1)
                    mm_cnt[0] += 1
                    h_mark[("ftr", qh, tl, rnd)] = mm_cnt[0]
                p.t(hftr)

                def hfev(s, qh=qh, tl=tl, rnd=rnd):
                    s.wait_ge(sMM, h_mark[("ftr", qh, tl, rnd)])
                    for b in range(4):
                        pb = P[b][:, :].bitcast(BF16)
                        s.activation(out=htt[:, rnd * 4096 + b * 1024:
                                             rnd * 4096 + (b + 1) * 1024],
                                     in_=pb[:, :], func=AF.Copy)
                    h_mark[("fev", qh, tl, rnd)] = p.cnt["s"]
                p.s(hfev)

            def hsq1(s, qh=qh, tl=tl):
                s.wait_ge(p.auto["s"], 0)
                s.activation(out=hq2[:, :], in_=htt[:, 0:4096], func=AF.Square)
                s.activation(out=msc[:, :], in_=htt[:, 4096:8192], func=AF.Square)
            p.s(hsq1)
            m_hq1 = p.mark("s")

            def hqstat(v, qh=qh, tl=tl, m_hq1=m_hq1):
                c = lambda k: statq[:, k:k + 1]
                v.wait_mark("s", m_hq1)
                v.reduce_sum(c(0), hq2[:, :], axis=AX.X)
                v.reduce_sum(c(1), msc[:, :], axis=AX.X)
                v.tensor_add(c(0), c(0), c(1))
                v.tensor_scalar_mul(c(0), c(0), 1.0 / (4 * D))
                v.reduce_max(c(1), htt[:, 0:4096], axis=AX.X,
                             apply_absolute_value=True)
                v.reduce_max(c(2), htt[:, 4096:8192], axis=AX.X,
                             apply_absolute_value=True)
                v.tensor_max(c(1), c(1), c(2))
            p.v(hqstat)
            m_hst = p.mark("v")

            def hrstd(s, qh=qh, tl=tl, m_hst=m_hst):
                c = lambda k: statq[:, k:k + 1]
                s.wait_mark("v", m_hst)
                s.activation(out=c(2), in_=c(0), func=AF.Ln, bias=RMS_EPS)
                s.activation(out=c(2), in_=c(2), func=AF.Exp, scale=-0.5)
            p.s(hrstd)
            m_hrs = p.mark("s")

            def hqops(v, qh=qh, tl=tl, m_hrs=m_hrs):
                c = lambda k: statq[:, k:k + 1]
                v.wait_mark("s", m_hrs)
                v.tensor_mul(c(3), c(1), c(2))
                v.tensor_scalar_max(c(3), c(3), AQ_EPS)
                v.reciprocal(c(4), c(3))
                v.tensor_scalar_mul(c(4), c(4), 127.0)
                v.tensor_scalar_mul(c(5), c(3), 1.0 / 127.0)
                v.tensor_mul(c(6), c(2), c(4))
                for half in range(2):
                    h0 = half * 4096
                    hs_ = slice(h0, h0 + 4096)
                    v.tensor_scalar_mul(hq2[:, :], htt[:, hs_], c(6))
                    v.tensor_scalar_add(msc[:, :], hq2[:, :], LAD8_C)
                    _emit_ladder(v, [(msc[:, 0:2048], htt[:, h0:h0 + 2048]),
                                     (msc[:, 2048:4096],
                                      htt[:, h0 + 2048:h0 + 4096])], LAD8)
                    v.scalar_tensor_tensor(out=hq2[:, :], in0=msc[:, :],
                                           scalar=-1.0, in1=hq2[:, :],
                                           op0=ALU.mult, op1=ALU.add)
                    v.tensor_scalar(out=htt[:, hs_], in0=hq2[:, :], scalar1=0.5,
                                    scalar2=c(5), op0=ALU.add, op1=ALU.mult)
                h_mark[("q", qh, tl)] = p.cnt["v"]
            p.v(hqops)

            for rnd in range(4):
                def hbtr(t, qh=qh, tl=tl, rnd=rnd):
                    if rnd == 0:
                        t.wait_ge(p.auto["v"], h_mark[("q", qh, tl)])
                    else:
                        t.wait_ge(p.auto["s"], h_mark[("bev", qh, tl, rnd - 1)])
                    last = None
                    for k in range(16):
                        hc = rnd * 16 + k
                        pb = P[k // 8][:, :].bitcast(BF16)
                        last = t.transpose(
                            pb[:, (k % 8) * 128:(k % 8) * 128 + 128],
                            htt[:, hc * 128:(hc + 1) * 128], id_bf[:, :])
                    last.then_inc(sMM, 1)
                    mm_cnt[0] += 1
                    h_mark[("btr", qh, tl, rnd)] = mm_cnt[0]
                p.t(hbtr)

                def hbev(s, qh=qh, tl=tl, rnd=rnd):
                    s.wait_ge(sMM, h_mark[("btr", qh, tl, rnd)])
                    for k in range(16):
                        hc = rnd * 16 + k
                        pb = P[k // 8][:, :].bitcast(BF16)
                        s.activation(
                            out=hbf[:, hc * TQH + tl * 128:hc * TQH + tl * 128 + 128],
                            in_=pb[:, (k % 8) * 128:(k % 8) * 128 + 128],
                            func=AF.Copy)
                    h_mark[("bev", qh, tl, rnd)] = p.cnt["s"]
                p.s(hbev)
        h_mark[("btrlast", qh)] = h_mark[("btr", qh, 1, 3)]
        h_mark[("bevlast", qh)] = h_mark[("bev", qh, 1, 3)]
        phQ.close()

        phI = ExitStack()
        w3tl = phI.enter_context(nc.sbuf_tensor(f"w3tl{qh}", [128, 2 * 8192], BF16))
        for ec in range(NDC):
            ei = qh * NDC + ec
            slot = ei % 2

            def w3load(y, qh=qh, ec=ec, ei=ei, slot=slot):
                if ec < 2:
                    y.wait_ge(sMM, h_mark[("mm", qh * HOC + HOC - 1)])
                    y.wait_ge(sMM, h_mark[("btrlast", qh)])
                    y.wait_ge(p.auto["s"], h_mark[("bevlast", qh)])
                if ei >= 2:
                    y.wait_ge(sMM, h_mark.get(("w3mm", ei - 2), 0))
                flat = wq_scr[2].rearrange("(c p) o -> p c o", p=128)
                y.dma_start(out=w3tl[:, slot * 8192:(slot + 1) * 8192],
                            in_=flat[:, :, ec * 128:(ec + 1) * 128]
                            ).then_inc(sW3[slot], 16)
            p.y(w3load)

            def w3mm(t, qh=qh, ec=ec, ei=ei, slot=slot):
                t.wait_ge(sW3[slot], 16 * (ei // 2 + 1))
                t.wait_ge(p.auto["s"], h_mark[("bevlast", qh)])
                if ei >= 2:
                    t.wait_ge(p.auto["v"], h_mark.get(("x4", ei - 2), 0))
                bank = SA2 if ec % 2 == 0 else SB2
                for hc in range(64):
                    MM(t, bank[:, 0:TQH],
                       w3tl[:, slot * 8192 + hc * 128:slot * 8192 + (hc + 1) * 128],
                       hbf[:, hc * TQH:(hc + 1) * TQH],
                       start=(hc == 0), stop=(hc == 63))
                h_mark[("w3mm", ei)] = mm_cnt[0]
            p.t(w3mm)

            def x4ev(v, qh=qh, ec=ec, ei=ei):
                v.wait_ge(sMM, h_mark[("w3mm", ei)])
                if qh > 0:
                    v.wait_ge(sOut[0], 128 * qh)
                    v.wait_ge(sOut[1], 128 * qh)
                bank = SA2 if ec % 2 == 0 else SB2
                cs = slice(ec * TPC + qh * TQH, ec * TPC + (qh + 1) * TQH)
                v.scalar_tensor_tensor(out=x2sb[:, cs], in0=bank[:, 0:TQH],
                                       scalar=wsc[:, 6:7], in1=x2sb[:, cs],
                                       op0=ALU.mult, op1=ALU.add)
                h_mark[("x4", ei)] = p.cnt["v"]
            p.v(x4ev)

            def x4out(y, qh=qh, ec=ec, ei=ei):
                y.wait_ge(p.auto["v"], h_mark[("x4", ei)])
                y.dma_start(out=o_x4[ec * 128:(ec + 1) * 128,
                                     qh * TQH:(qh + 1) * TQH],
                            in_=x2sb[:, ec * TPC + qh * TQH:
                                     ec * TPC + (qh + 1) * TQH]
                            ).then_inc(sOut[ec % 2], 16)
            p.y(x4out)
        phI.close()

    def finish(y):
        y.wait_ge(sOut[0], 16 * 32)
        y.wait_ge(sOut[1], 16 * 32)
    p.y(finish)

    st.close()
    return nc


_NC_CACHE = {}


def _get_nc(name):
    if name not in _NC_CACHE:
        _NC_CACHE[name] = globals()["_build_" + name]()
    return _NC_CACHE[name]


def _run(name, in_maps, cores):
    import time
    t0 = time.time()
    r = run_bass_kernel_spmd(_get_nc(name), in_maps, core_ids=cores).results
    if os.environ.get("KTIME"):
        print(f"[ktime] launch {name}: {time.time() - t0:.1f}s",
              file=sys.stderr, flush=True)
    return r


def _cols(v):
    v = np.asarray(v)
    return np.ascontiguousarray(v.reshape(-1, 128).T)


def _t(a):
    return np.ascontiguousarray(np.asarray(a).T)


def kernel(**inputs):
    inputs.get("L0_ffn_gate_w")
    x = np.asarray(inputs["L0_attn_x"])
    xf = np.ascontiguousarray(x.reshape(8192, D))
    ident = np.eye(128, dtype=np.float32)
    onesrow = np.ones((1, 128), np.float32)
    oab = np.zeros((128, 65), np.float32)
    oab[0:64, 0] = 1.0
    oab[64:128, 64] = 1.0
    or64 = np.ones((65, 128), np.float32)
    cA = _make_consts(CONSTS_A)
    cC = _make_consts(CONSTS_C)
    cD = _make_consts(CONSTS_D)
    cores = list(range(NCORES))

    qw = np.asarray(inputs["L1_attn_q_weight"])
    kw = np.asarray(inputs["L1_attn_k_weight"])
    vw = np.asarray(inputs["L1_attn_v_weight"])
    ow = np.asarray(inputs["L1_attn_o_weight"])
    w1t = _t(inputs["L1_ffn_w1"])
    w2t = _t(inputs["L1_ffn_w2"])
    w3t = _t(inputs["L1_ffn_w3"])
    qwt, kwt, vwt, owt = _t(qw), _t(kw), _t(vw), _t(ow)
    cW = _make_consts(CONSTS_W)
    inW = []
    for i in cores:
        sl = slice(256 * i, 256 * (i + 1))
        s2 = slice(TPC * i, TPC * (i + 1))
        inW.append(dict(
            wr4=np.ascontiguousarray(
                np.concatenate([qw[sl], kw[sl], vw[sl], ow[sl]], axis=0)),
            wt4=np.ascontiguousarray(
                np.concatenate([qwt[:, sl], kwt[:, sl], vwt[:, sl],
                                owt[:, sl]], axis=1)),
            w1s=np.ascontiguousarray(w1t[:, s2]),
            w2s=np.ascontiguousarray(w2t[:, s2]),
            w3s=np.ascontiguousarray(w3t[s2, :]),
            constv=cW))
    rW = _run("W", inW, cores)

    qint = [np.concatenate([rW[i]["o_q4"][256 * w:256 * (w + 1)]
                            for i in cores], axis=0) for w in range(4)]
    qitT = [_t(qint[w]) for w in range(4)]
    ws64 = np.zeros((128, 64), np.float32)
    for w in range(4):
        for i in cores:
            ws64[:, w * 16 + 2 * i:w * 16 + 2 * i + 2] = \
                np.asarray(rW[i]["o_s4"])[:, 2 * w:2 * w + 2]
    wtbf = [np.concatenate([np.asarray(rW[i]["o_wbf"])[:, 256 * w:256 * (w + 1)]
                            for i in cores], axis=1) for w in range(4)]
    msums = np.concatenate([np.asarray(rW[i]["o_msum"]) for i in cores],
                           axis=0)

    bdtc = _cols(inputs["L0_attn_bdt"])
    inA = [dict(xr=np.ascontiguousarray(xf[i * TPC:(i + 1) * TPC]),
                wx=np.asarray(inputs["L0_attn_Wx"]),
                wdt=np.asarray(inputs["L0_attn_Wdt"]),
                bdtc=bdtc, ident=ident, constv=cA) for i in cores]
    rA = _run("A", inA, cores)

    dt_g = [np.concatenate([rA[4 * b + j]["o_dt"] for j in range(4)], axis=1)
            for b in range(2)]
    dtu_g = [np.concatenate([rA[4 * b + j]["o_dtu"] for j in range(4)], axis=1)
             for b in range(2)]
    b_nt = [np.concatenate([rA[4 * b + j]["o_bc"][0:16] for j in range(4)],
                           axis=1) for b in range(2)]
    c_nt = [np.concatenate([rA[4 * b + j]["o_bc"][16:32] for j in range(4)],
                           axis=1) for b in range(2)]

    alog = np.asarray(inputs["L0_attn_A_log"])
    masks = _mk_masks()
    inB = []
    for i in cores:
        b, q = divmod(i, 4)
        rs = slice(512 * q, 512 * (q + 1))
        inB.append(dict(
            dt_in=np.ascontiguousarray(dt_g[b][rs]),
            dtu_in=np.ascontiguousarray(dtu_g[b][rs]),
            b_in=np.ascontiguousarray(b_nt[b]),
            c_in=np.ascontiguousarray(c_nt[b]),
            alogl=_alog_lanes(alog[rs]), masks=masks))
    rB = _run("B", inB, cores)
    y_g = [np.concatenate([rB[4 * b + q]["o_y"] for q in range(4)], axis=0)
           for b in range(2)]

    p1t = _t(inputs["L0_ffn_proj1"])
    p2t = _t(inputs["L0_ffn_proj2"])
    inC1 = []
    for i in cores:
        b, q = divmod(i, 4)
        inC1.append(dict(
            xr=np.ascontiguousarray(xf[i * TPC:(i + 1) * TPC]),
            yin=np.ascontiguousarray(y_g[b][:, TPC * q:TPC * (q + 1)]),
            dpc=_cols(inputs["L0_attn_Dp"]),
            p1t=p1t, b1c=_cols(inputs["L0_ffn_proj1_bias"]),
            p2t=p2t, b2c=_cols(inputs["L0_ffn_proj2_bias"]),
            msums=msums,
            w1ts=np.ascontiguousarray(w1t[:, TPC * i:TPC * (i + 1)]),
            w2ts=np.ascontiguousarray(w2t[:, TPC * i:TPC * (i + 1)]),
            w3ts=np.ascontiguousarray(w3t[TPC * i:TPC * (i + 1), :]),
            ident=ident, onesrow=onesrow, constv=cC))
    rC1 = _run("C1", inC1, cores)

    w1qt = np.concatenate([np.asarray(rC1[i]["o_w1q"]) for i in cores], axis=1)
    w2qt = np.concatenate([np.asarray(rC1[i]["o_w2q"]) for i in cores], axis=1)
    w3qt = np.concatenate([np.asarray(rC1[i]["o_w3q"]) for i in cores], axis=0)
    wmb = np.ascontiguousarray(
        np.broadcast_to(np.asarray(rC1[0]["o_wm"]), (128, 4)))

    inC2 = []
    for i in cores:
        inC2.append(dict(
            x2in=np.ascontiguousarray(rC1[i]["o_x2"]),
            qit0=qitT[0], qit1=qitT[1], qit2=qitT[2],
            wtbf0=wtbf[0], wtbf1=wtbf[1], wtbf2=wtbf[2],
            winsc=ws64,
            qbc=_cols(inputs["L1_attn_q_bias"]),
            kbc=_cols(inputs["L1_attn_k_bias"]),
            vbc=_cols(inputs["L1_attn_v_bias"]),
            ident=ident, onesrow=onesrow, constv=cC))
    rC2 = _run("C2", inC2, cores)

    oqit = qitT[3]
    osc = np.ascontiguousarray(ws64[:, 48:64])
    inD = []
    for i in cores:
        b, q = divmod(i, 4)
        kf = np.concatenate([rC2[4 * b + j]["o_k"] for j in range(4)], axis=1)
        vt = np.concatenate([rC2[4 * b + j]["o_v"] for j in range(4)], axis=0)
        inD.append(dict(
            x2f=np.ascontiguousarray(rC1[i]["o_x2"]),
            qf=np.ascontiguousarray(rC2[i]["o_q"]),
            kf=np.ascontiguousarray(kf), vt=np.ascontiguousarray(vt),
            oqit=oqit, osc=osc, owbf=wtbf[3],
            obc=_cols(inputs["L1_attn_o_bias"]),
            w1qt=w1qt, w2qt=w2qt, w3qt=w3qt, wmb=wmb,
            ident=ident, onesrow=onesrow, oab=oab, or64=or64, constv=cD))
    rD = _run("D", inD, cores)

    out = np.concatenate([np.asarray(rD[i]["o_x4"]).T for i in cores], axis=0)
    return out.reshape(2, T, D).astype(np.float32)
